# version 22
# baseline (speedup 1.0000x reference)
import contextlib
import numpy as np
import ml_dtypes
import concourse.bass as bass
import concourse.mybir as mybir
from concourse.bass_utils import run_bass_kernel_spmd

F32 = mybir.dt.float32
BF16 = mybir.dt.bfloat16
I32 = mybir.dt.int32
AF = mybir.ActivationFunctionType
ALU = mybir.AluOpType
LN2 = float(np.log(2.0))
bf16 = ml_dtypes.bfloat16



def _threefry2x32(k0, k1, x0, x1):
    R0 = [13, 15, 26, 6]
    R1 = [17, 29, 16, 24]
    ks0 = np.uint32(k0)
    ks1 = np.uint32(k1)
    ks2 = np.uint32(ks0 ^ ks1 ^ np.uint32(0x1BD11BDA))
    ks = [ks0, ks1, ks2]
    x0 = (x0 + ks[0]).astype(np.uint32)
    x1 = (x1 + ks[1]).astype(np.uint32)
    for i, rots in enumerate([R0, R1, R0, R1, R0]):
        for r in rots:
            x0 = (x0 + x1).astype(np.uint32)
            x1 = ((x1 << np.uint32(r)) | (x1 >> np.uint32(32 - r))).astype(np.uint32)
            x1 = (x1 ^ x0).astype(np.uint32)
        x0 = (x0 + ks[(i + 1) % 3]).astype(np.uint32)
        x1 = (x1 + ks[(i + 2) % 3] + np.uint32(i + 1)).astype(np.uint32)
    return x0, x1


def dropout_keep_mask(shape, key=42, p_keep=0.8):
    n = int(np.prod(shape))
    y0, y1 = _threefry2x32(0, key, np.zeros(n, np.uint32),
                           np.arange(n, dtype=np.uint32))
    bits = (y0 ^ y1).astype(np.uint32)
    f = ((bits >> np.uint32(9)) | np.uint32(0x3F800000)).view(np.float32) \
        - np.float32(1.0)
    return (f < np.float32(p_keep)).reshape(shape)




class Cfg:
    def __init__(self, T=8192, OWN=2048, SW=512):
        self.T = T
        self.OWN = OWN
        self.SW = SW
        self.D = 1024
        self.H = 16
        self.HD = 64
        self.HF = 4096
        self.NSW = OWN // SW
        self.NKT = T // 128
        self.NTT = T // 128
        self.NCH = self.D // 128
        self.NHJ = self.HF // 128


CPOOL_VALS = [
    1e-38,
    1e-5,
    -0.5,
    -80.0,
    -LN2,
    LN2,
    9.0 * LN2,
    -9.0 * LN2,
    24.0 * LN2,
    -24.0 * LN2,
]


class P:

    def __init__(self, nc, cfg, t, eng_name, eng):
        self.nc = nc
        self.cfg = cfg
        self.t = t
        self.eng_name = eng_name
        self.eng = eng
        self.sems = t["sems"]
        self.cnt = {k: 0 for k in self.sems}
        self.tr_evac = [0, 0]
        self.tr_store = [0, 0]
        self.vt_store = [0, 0]
        self.xb_done = [0, 0]
        self.kb_done = [0, 0]
        self.kch_free = [0, 0]
        self.vsl_free = [0, 0]
        self.mega_freeA = [0, 0]
        self.mega_freeB = [0, 0]
        self.et_free = [0, 0]
        self.nd_free = 0
        self.bc_free = 0
        self.bc2_free = 0
        self.mmps_free = [0, 0]
        self.w1t_free = [0, 0]
        self.w2t_free = [0, 0]
        self.qf_free = 0
        self.wbuf_done = [0, 0]
        self.ych_store = 0
        self.xoch_done = 0
        self.mkch_done = 0
        self.xnc_free = [0, 0]
        self.drec_free = 0
        self.stats_free = 0
        self.xq_free = 0
        self.hq_free = 0
        self.ffn_free = [0, 0]
        self.evac_mark = 0
        self.nd_free_act = 0
        self.nd_free_dve = 0
        self.attnN_free = 0
        self.dall_free = 0
        self.xq_ready = 0
        self.hq_ready = 0
        self.prep_par = 0
        self.prep_spar = 0
        self.wbN_done = [0, 0]
        self.nat_mark = [0, 0]
        self.stg_store = [0, 0]
        self.wj_marks = {}
        self.w1T = [0, 0]
        self.w2T = [0, 0]

    def on(self, name):
        return self.eng_name == name

    def bump(self, sem, n):
        self.cnt[sem] += n
        return self.cnt[sem]

    def wait(self, engname, sem, val):
        if self.on(engname) and val > 0:
            self.eng.wait_ge(self.sems[sem], val)

    def dma(self, q, out, in_, sem="ld"):
        v = self.bump(sem, 16)
        if self.on(q):
            self.eng.dma_start(out=out, in_=in_).then_inc(self.sems[sem], 16)
        return v

    def op(self, engname, sem, fn):
        v = self.bump(sem, 1)
        if self.on(engname):
            fn(self.eng).then_inc(self.sems[sem], 1)
        return v

    def opq(self, engname, fn):
        if self.on(engname):
            fn(self.eng)

    def vop(self, fn):
        self.op("vector", "dve", fn)
        self.wait("vector", "dve", self.cnt["dve"])
        return self.cnt["dve"]

    def ladder(self, z_ap, b_ap, nlev):
        for k in reversed(range(nlev)):
            c = float(2.0 ** k)
            self.vop(lambda e, c=c: e.tensor_scalar(b_ap, z_ap, c, None, ALU.is_ge))
            self.vop(lambda e, c=c: e.scalar_tensor_tensor(
                z_ap, b_ap, -c, z_ap, ALU.mult, ALU.add))

    def q43_tile(self, v_ap, out_ap, W, elev, bufs=None):
        t = self.t
        if bufs is None:
            qa = t["qa"][:, 0:W]
            qb = t["qb"][:, 0:W]
            qc = t["qc"][:, 0:W]
            qd = t["qd"][:, 0:W]
        else:
            qa, qb, qc, qd = (b[:, 0:W] for b in bufs)
        cp = t["cpool"]
        self.wait("scalar", "dve", self.cnt["dve"])
        self.op("scalar", "act", lambda e: e.activation(
            qa, v_ap, AF.Square, bias=0.0, scale=1.0))
        self.wait("scalar", "act", self.cnt["act"])
        self.op("scalar", "act", lambda e: e.activation(
            qb, qa, AF.Ln, bias=cp[:, 0:1], scale=1.0))
        self.wait("scalar", "act", self.cnt["act"])
        av = self.cnt["act"]
        self.wait("vector", "act", av)
        self.vop(lambda e: e.tensor_scalar(qb, qb, 0.5 / LN2, 6.0, ALU.mult, ALU.add))
        self.vop(lambda e: e.tensor_scalar(
            qb, qb, 0.0, float(2.0 ** elev) - 1e-3, ALU.max, ALU.min))
        self.vop(lambda e: e.tensor_copy(qc, qb))
        self.ladder(qb, qd, elev)
        self.vop(lambda e: e.tensor_tensor(qc, qc, qb, ALU.subtract))
        dv = self.cnt["dve"]
        self.wait("scalar", "dve", dv)
        self.op("scalar", "act", lambda e: e.activation(
            qb, qc, AF.Exp, bias=cp[:, 6:7], scale=cp[:, 4:5]))
        self.op("scalar", "act", lambda e: e.activation(
            qd, qc, AF.Exp, bias=cp[:, 7:8], scale=cp[:, 5:6]))
        av = self.cnt["act"]
        self.wait("vector", "act", av)
        self.vop(lambda e: e.tensor_tensor(qa, v_ap, qb, ALU.mult))
        self.vop(lambda e: e.tensor_scalar(qa, qa, 16.5, None, ALU.add))
        self.vop(lambda e: e.tensor_copy(qc, qa))
        self.ladder(qa, qb, 6)
        self.vop(lambda e: e.tensor_tensor(qc, qc, qa, ALU.subtract))
        self.vop(lambda e: e.scalar_tensor_tensor(
            out_ap, qc, -16.0, qd, ALU.add, ALU.mult))
        return self.cnt["dve"]

    def qh_tile(self, psum_ap, b1col, out_ap, W):
        t = self.t
        qa = t["qa"][:, 0:W]
        qb = t["qb"][:, 0:W]
        qc = t["qc"][:, 0:W]
        qd = t["qd"][:, 0:W]
        qe = t["qe"][:, 0:W]
        cp = t["cpool"]
        self.vop(lambda e: e.tensor_scalar(
            qa, psum_ap, b1col, 1e-30, ALU.add, ALU.max))
        self.evac_mark = self.cnt["dve"]
        dv = self.cnt["dve"]
        self.wait("scalar", "dve", dv)
        self.op("scalar", "act", lambda e: e.activation(
            qb, qa, AF.Ln, bias=0.0, scale=1.0))
        self.wait("scalar", "act", self.cnt["act"])
        av = self.cnt["act"]
        self.wait("vector", "act", av)
        self.vop(lambda e: e.tensor_scalar(qb, qb, 1.0 / LN2, 6.0, ALU.mult, ALU.add))
        self.vop(lambda e: e.tensor_scalar(qb, qb, 0.0, 12.0 - 1e-3, ALU.max, ALU.min))
        self.vop(lambda e: e.tensor_copy(qc, qb))
        self.ladder(qb, qd, 4)
        self.vop(lambda e: e.tensor_tensor(qc, qc, qb, ALU.subtract))
        dv = self.cnt["dve"]
        self.wait("scalar", "dve", dv)
        self.op("scalar", "act", lambda e: e.activation(
            qb, qc, AF.Exp, bias=cp[:, 6:7], scale=cp[:, 4:5]))
        self.op("scalar", "act", lambda e: e.activation(
            qd, qc, AF.Exp, bias=cp[:, 7:8], scale=cp[:, 5:6]))
        av = self.cnt["act"]
        self.wait("vector", "act", av)
        self.vop(lambda e: e.tensor_tensor(qa, qa, qb, ALU.mult))
        self.vop(lambda e: e.tensor_copy(qc, qa))
        self.ladder(qa, qe, 4)
        self.vop(lambda e: e.tensor_tensor(qc, qc, qa, ALU.subtract))
        self.vop(lambda e: e.tensor_scalar(qa, qa, -0.5, None, ALU.add))
        self.vop(lambda e: e.scalar_tensor_tensor(
            qb, qa, 0.0, qc, ALU.is_ge, ALU.add))
        self.vop(lambda e: e.tensor_tensor(qc, qc, qe, ALU.add))
        self.vop(lambda e: e.tensor_tensor(qa, qa, qa, ALU.mult))
        self.vop(lambda e: e.tensor_scalar(
            qa, qa, float(2.0 ** -16 * 1.002), None, ALU.is_le))
        self.vop(lambda e: e.tensor_tensor(qc, qc, qb, ALU.subtract))
        self.vop(lambda e: e.tensor_tensor(qc, qa, qc, ALU.mult))
        self.vop(lambda e: e.tensor_tensor(qb, qb, qc, ALU.add))
        self.vop(lambda e: e.tensor_tensor(out_ap, qb, qd, ALU.mult))
        return self.cnt["dve"]

    def qy_tile(self, psum_ap, b2col, out_ap, W):
        t = self.t
        qa = t["qa"][:, 0:W]
        qb = t["qb"][:, 0:W]
        qc = t["qc"][:, 0:W]
        qd = t["qd"][:, 0:W]
        qi = t["qi"][:, 0:W]
        cp = t["cpool"]
        self.vop(lambda e: e.tensor_scalar(qa, psum_ap, b2col, None, ALU.add))
        self.evac_mark = self.cnt["dve"]
        self.wait("scalar", "dve", self.cnt["dve"])
        self.op("scalar", "act", lambda e: e.activation(
            qb, qa, AF.Square, bias=0.0, scale=1.0))
        self.wait("scalar", "act", self.cnt["act"])
        self.op("scalar", "act", lambda e: e.activation(
            qb, qb, AF.Ln, bias=cp[:, 0:1], scale=1.0))
        self.wait("scalar", "act", self.cnt["act"])
        av = self.cnt["act"]
        self.wait("vector", "act", av)
        self.vop(lambda e: e.tensor_scalar(qb, qb, 0.5 / LN2, 14.0, ALU.mult, ALU.add))
        self.vop(lambda e: e.tensor_scalar(qb, qb, 0.0, 20.0 - 1e-3, ALU.max, ALU.min))
        self.vop(lambda e: e.tensor_copy(qc, qb))
        self.ladder(qb, qd, 5)
        self.vop(lambda e: e.tensor_tensor(qc, qc, qb, ALU.subtract))
        dv = self.cnt["dve"]
        self.wait("scalar", "dve", dv)
        self.op("scalar", "act", lambda e: e.activation(
            qb, qc, AF.Exp, bias=cp[:, 8:9], scale=cp[:, 4:5]))
        self.op("scalar", "act", lambda e: e.activation(
            qd, qc, AF.Exp, bias=cp[:, 9:10], scale=cp[:, 5:6]))
        av = self.cnt["act"]
        self.wait("vector", "act", av)
        self.vop(lambda e: e.tensor_tensor(qa, qa, qb, ALU.mult))
        self.vop(lambda e: e.tensor_copy(qi, qa))
        self.vop(lambda e: e.tensor_copy(qa, qi))
        self.vop(lambda e: e.tensor_tensor(out_ap, qa, qd, ALU.mult))
        return self.cnt["dve"]

    def emit(self):
        cfg = self.cfg
        t = self.t
        cp = t["cpool"]
        SW = cfg.SW

        for j, val in enumerate(CPOOL_VALS):
            self.op("gpsimd", "gp", lambda e, j=j, val=val: e.memset(
                cp[:, j:j + 1], val))
        for par in range(2):
            self.op("gpsimd", "gp", lambda e, par=par: e.memset(
                t["vsl"][par][:, 64:65], 1.0))
            self.op("gpsimd", "gp", lambda e, par=par: e.memset(
                t["vsl"][par][:, 129:130], 1.0))
        self.op("gpsimd", "gp", lambda e: e.memset(t["onesf32"], 1.0))
        self.op("gpsimd", "gp", lambda e: e.memset(t["onesf32c"], 1.0))
        gp_init = self.cnt["gp"]
        for en in ("vector", "scalar", "tensor"):
            self.wait(en, "gp", gp_init)

        for tt in range(cfg.NTT):
            par = tt % 2
            xbuf = t["xbuf"][par]
            self.wait("sync", "dve", self.xb_done[par])
            ldv = self.dma("sync", xbuf, t["x_tok"][tt * 128:(tt + 1) * 128, :],
                           sem=f"l_xb{par}")
            self.wait("vector", f"l_xb{par}", ldv)
            self.vop(lambda e: e.tensor_tensor(t["sq1"], xbuf, xbuf, ALU.mult))
            self.vop(lambda e: e.reduce_sum(t["s1"], xbuf,
                                            axis=mybir.AxisListType.X))
            self.vop(lambda e: e.reduce_sum(t["s2"], t["sq1"],
                                            axis=mybir.AxisListType.X))
            self.vop(lambda e: e.tensor_scalar(
                t["s1"], t["s1"], 1.0 / 1024.0, None, ALU.mult))
            self.vop(lambda e: e.tensor_tensor(t["s3"], t["s1"], t["s1"], ALU.mult))
            self.vop(lambda e: e.scalar_tensor_tensor(
                t["s2"], t["s2"], 1.0 / 1024.0, t["s3"], ALU.mult, ALU.subtract))
            dvv = self.cnt["dve"]
            self.wait("scalar", "dve", dvv)
            self.op("scalar", "act", lambda e: e.activation(
                t["s2"], t["s2"], AF.Ln, bias=cp[:, 1:2], scale=1.0))
            self.wait("scalar", "act", self.cnt["act"])
            self.op("scalar", "act", lambda e: e.activation(
                t["s2"], t["s2"], AF.Exp, bias=0.0, scale=cp[:, 2:3]))
            self.wait("scalar", "act", self.cnt["act"])
            avv = self.cnt["act"]
            self.wait("vector", "act", avv)
            self.wait("vector", f"s_vt{par}", self.vt_store[par])
            self.vop(lambda e, par=par: e.tensor_scalar(
                t["xnc"][par], xbuf, t["s1"][:, 0:1], t["s2"][:, 0:1],
                ALU.subtract, ALU.mult))
            self.xb_done[par] = self.cnt["dve"]
            xnc_dve = self.cnt["dve"]
            self.wait("gpsimd", "dve", xnc_dve)
            self.vt_store[par] = self.dma(
                "gpsimd", t["Xn"][tt * 128:(tt + 1) * 128, :],
                t["xnc"][par], sem=f"s_vt{par}")
        p0_vt = [self.cnt["s_vt0"], self.cnt["s_vt1"]]
        for ch in range(cfg.NCH):
            par = ch % 2
            self.wait("sync", "s_vt0", p0_vt[0])
            self.wait("sync", "s_vt1", p0_vt[1])
            self.wait("sync", f"s_kc{par}", self.kb_done[par])
            if self.on("sync"):
                self.eng.dma_start_transpose(
                    t["kch"][par], t["Xn"][:, ch * 128:(ch + 1) * 128]
                ).then_inc(self.sems[f"l_kc{par}"], 16)
            ldv = self.bump(f"l_kc{par}", 16)
            self.wait("vector", f"l_kc{par}", ldv)
            self.vop(lambda e, par=par: e.reduce_sum(
                t["s1"], t["kch"][par], axis=mybir.AxisListType.X))
            self.vop(lambda e: e.tensor_scalar(
                t["s1"], t["s1"], 1.0 / float(cfg.T), None, ALU.mult))
            self.vop(lambda e, par=par: e.tensor_scalar(
                t["kch"][par], t["kch"][par], t["s1"][:, 0:1], None, ALU.subtract))
            kcd = self.cnt["dve"]
            self.wait("gpsimd", "dve", kcd)
            self.kb_done[par] = self.dma(
                "gpsimd", t["Kfm"][ch * 128:(ch + 1) * 128, :], t["kch"][par],
                sem=f"s_kc{par}")
        p1_kc = [self.cnt["s_kc0"], self.cnt["s_kc1"]]
        self.dma("sync", t["b1q"], t["b1f"], sem="l_b")
        ldb = self.dma("sync", t["b2q"], t["b2f"], sem="l_b")
        self.wait("vector", "l_b", ldb)
        self.wait("scalar", "l_b", ldb)
        self.q43_tile(t["b1q"], t["b1q"], cfg.NHJ, 3)
        self.q43_tile(t["b2q"], t["b2q"], cfg.NCH, 3)

        ntr1 = max(1, cfg.NHJ * 128 // 1024)
        nwtr1 = (cfg.D // 128) * ntr1
        nwtr2 = cfg.HF // 128
        nwtr = nwtr1 + nwtr2
        self.w1T = [16 * ((nwtr1 + 1) // 2), 16 * (nwtr1 // 2)]
        self.w2T = [16 * ((nwtr + 1) // 2), 16 * (nwtr // 2)]

        def emit_prep_unit(u):
            if u[0] == "wj":
                _, src, nat, row0, col0, ncol, last = u
                par = self.prep_par
                self.prep_par ^= 1
                wb = t["wbN"][par][:, 0:ncol]
                self.wait("sync", "dve", self.wbN_done[par])
                ldv = self.dma("sync", wb,
                               src[row0:row0 + 128, col0:col0 + ncol],
                               sem=f"l_wb{par}")
                self.wait("vector", f"l_wb{par}", ldv)
                self.wait("scalar", f"l_wb{par}", ldv)
                self.wait("vector", f"s_nat{par}", self.nat_mark[par])
                self.wait("vector", f"s_vt{par}", self.vt_store[par])
                for sl in range(ncol // SW):
                    self.q43_tile(wb[:, sl * SW:(sl + 1) * SW],
                                  t["xnc"][par][:, sl * SW:(sl + 1) * SW],
                                  SW, 3)
                self.wbN_done[par] = self.cnt["dve"]
                self.wait("gpsimd", "dve", self.wbN_done[par])
                self.nat_mark[par] = self.dma(
                    "gpsimd", nat[row0:row0 + 128, col0:col0 + ncol],
                    t["xnc"][par][:, 0:ncol], sem=f"s_nat{par}")
                self.wj_marks[last] = (self.cnt["s_nat0"], self.cnt["s_nat1"])
            else:
                _, nat, dst, ch, r0, nrow, dep = u
                spar = self.prep_spar
                self.prep_spar ^= 1
                g = self.wj_marks[dep]
                self.wait("sync", "s_nat0", g[0])
                self.wait("sync", "s_nat1", g[1])
                self.wait("sync", f"s_dst{spar}", self.stg_store[spar])
                if self.on("sync"):
                    self.eng.dma_start_transpose(
                        t["stg"][spar][:, 0:nrow],
                        nat[r0:r0 + nrow, ch * 128:(ch + 1) * 128]
                    ).then_inc(self.sems[f"l_tr{spar}"], 16)
                trl = self.bump(f"l_tr{spar}", 16)
                self.wait("gpsimd", f"l_tr{spar}", trl)
                self.stg_store[spar] = self.dma(
                    "gpsimd", dst[ch * 128:(ch + 1) * 128, r0:r0 + nrow],
                    t["stg"][spar][:, 0:nrow], sem=f"s_dst{spar}")

        prep_q = []
        wid = 0
        for qq in range(ntr1):
            for j in range(8 * qq, min(8 * qq + 8, cfg.NHJ)):
                prep_q.append(("wj", t["w1"], t["w1qn"], j * 128, 0, cfg.D,
                               wid))
                wid += 1
            for ch in range(cfg.D // 128):
                prep_q.append(("wtr", t["w1qn"], t["w1qT"], ch, qq * 1024,
                               min(1024, cfg.NHJ * 128), wid - 1))
        nq2 = cfg.HF // cfg.D
        for q in range(nq2):
            for j in range(cfg.NCH):
                prep_q.append(("wj", t["w2"], t["w2qn"], j * 128, q * cfg.D,
                               cfg.D, wid))
                wid += 1
            for ch in range(8 * q, 8 * q + 8):
                prep_q.append(("wtr", t["w2qn"], t["w2qT"], ch, 0, cfg.D,
                               wid - 1))
        self.prep_q = prep_q
        self.prep_i = 0

        def pump_prep(kmax):
            n = 0
            while self.prep_i < len(self.prep_q) and n < kmax:
                emit_prep_unit(self.prep_q[self.prep_i])
                self.prep_i += 1
                n += 1

        def emit_ffn_unit(fs, kind, idx):
            resid = t["resid"][fs % 2]
            if kind == "hj":
                hj = idx
                mpar = hj % 2
                self.wait("sync", "pe", self.w1t_free[mpar])
                self.wait("sync", "s_dst0", self.w1T[0])
                self.wait("sync", "s_dst1", self.w1T[1])
                for ch in range(cfg.NCH):
                    self.dma("sync", t["w1t"][mpar][:, ch * 128:(ch + 1) * 128],
                             t["w1qT"][ch * 128:(ch + 1) * 128,
                                       hj * 128:(hj + 1) * 128],
                             sem=f"l_w1{mpar}")
                w1_ld = self.cnt[f"l_w1{mpar}"]
                self.wait("tensor", f"l_w1{mpar}", w1_ld)
                self.wait("tensor", "dve", self.xq_ready)
                self.wait("tensor", "dve", self.ffn_free[mpar])
                if mpar == 0:
                    self.wait("tensor", "dve", self.bc_free)
                else:
                    self.wait("tensor", "dve", self.bc2_free)
                for ch in range(cfg.NCH - 1):
                    self.opq("tensor", lambda e, ch=ch, mpar=mpar: e.matmul(
                        t["ffnps"][mpar], t["w1t"][mpar][:, ch * 128:(ch + 1) * 128],
                        t["xq"][:, ch * SW:(ch + 1) * SW],
                        start=(ch == 0), stop=False))
                self.op("tensor", "pe", lambda e, mpar=mpar: e.matmul(
                    t["ffnps"][mpar],
                    t["w1t"][mpar][:, (cfg.NCH - 1) * 128:cfg.NCH * 128],
                    t["xq"][:, (cfg.NCH - 1) * SW:cfg.NCH * SW],
                    start=False, stop=True))
                mm1_pe = self.cnt["pe"]
                self.w1t_free[mpar] = mm1_pe
                self.xq_free = mm1_pe
                self.wait("vector", "pe", mm1_pe)
                self.qh_tile(t["ffnps"][mpar], t["b1q"][:, hj:hj + 1],
                             t["hq"][:, hj * SW:(hj + 1) * SW], SW)
                self.ffn_free[mpar] = self.evac_mark
                if hj == cfg.NHJ - 1:
                    self.hq_ready = self.cnt["dve"]
            else:
                dj = idx
                mpar = dj % 2
                self.wait("sync", "pe", self.w2t_free[mpar])
                self.wait("sync", "s_dst0", self.w2T[0])
                self.wait("sync", "s_dst1", self.w2T[1])
                for k in range(cfg.NHJ):
                    self.dma("sync", t["w2t"][mpar][:, k * 128:(k + 1) * 128],
                             t["w2qT"][k * 128:(k + 1) * 128,
                                       dj * 128:(dj + 1) * 128],
                             sem=f"l_w2{mpar}")
                w2_ld = self.cnt[f"l_w2{mpar}"]
                self.wait("tensor", f"l_w2{mpar}", w2_ld)
                self.wait("tensor", "dve", self.hq_ready)
                self.wait("tensor", "dve", self.ffn_free[mpar])
                if mpar == 0:
                    self.wait("tensor", "dve", self.bc_free)
                else:
                    self.wait("tensor", "dve", self.bc2_free)
                for k in range(cfg.NHJ - 1):
                    self.opq("tensor", lambda e, k=k, mpar=mpar: e.matmul(
                        t["ffnps"][mpar], t["w2t"][mpar][:, k * 128:(k + 1) * 128],
                        t["hq"][:, k * SW:(k + 1) * SW],
                        start=(k == 0), stop=False))
                self.op("tensor", "pe", lambda e, mpar=mpar: e.matmul(
                    t["ffnps"][mpar],
                    t["w2t"][mpar][:, (cfg.NHJ - 1) * 128:cfg.NHJ * 128],
                    t["hq"][:, (cfg.NHJ - 1) * SW:cfg.NHJ * SW],
                    start=False, stop=True))
                mm2f_pe = self.cnt["pe"]
                self.w2t_free[mpar] = mm2f_pe
                self.hq_free = mm2f_pe
                self.wait("vector", "pe", mm2f_pe)
                self.wait("vector", "s_out", self.ych_store)
                self.qy_tile(t["ffnps"][mpar], t["b2q"][:, dj:dj + 1], t["ych"], SW)
                self.ffn_free[mpar] = self.evac_mark
                self.wait("sync", "dve", self.mkch_done)
                mk_ld = self.dma("sync", t["mkch"],
                                 t["mask_own"][dj * 128:(dj + 1) * 128,
                                               fs * SW:(fs + 1) * SW], sem="l_mk")
                self.wait("vector", "l_mk", mk_ld)
                self.vop(lambda e: e.tensor_tensor(
                    t["ych"], t["ych"], t["mkch"], ALU.mult))
                self.mkch_done = self.cnt["dve"]
                self.vop(lambda e, dj=dj, resid=resid: e.tensor_tensor(
                    t["ych"], t["ych"], resid[:, dj * SW:(dj + 1) * SW],
                    ALU.add))
                ych_dve = self.cnt["dve"]
                self.wait("gpsimd", "dve", ych_dve)
                self.ych_store = self.dma(
                    "gpsimd", t["out_fm"][dj * 128:(dj + 1) * 128,
                                          fs * SW:(fs + 1) * SW],
                    t["ych"], sem="s_out")

        units = [("hj", j) for j in range(cfg.NHJ)] + \
                [("dj", j) for j in range(cfg.NCH)]
        per_slot = (len(units) + cfg.H // 2 - 1) // (cfg.H // 2)

        for s in range(cfg.NSW + 1):
            if s < cfg.NSW:
                resid = t["resid"][s % 2]
                self.wait("sync", "s_kc0", p1_kc[0])
                self.wait("sync", "s_kc1", p1_kc[1])
                self.wait("sync", "s_vt0", p0_vt[0])
                self.wait("sync", "s_vt1", p0_vt[1])
                self.wait("sync", "pec", self.qf_free)
                for ch in range(cfg.NCH):
                    self.dma("sync", t["qf"][:, ch * SW:(ch + 1) * SW],
                             t["Kfm"][ch * 128:(ch + 1) * 128,
                                      s * SW:(s + 1) * SW], sem="l_qf")
                qf_ld = self.cnt["l_qf"]

                for p in range(cfg.H // 2):
                    par = p % 2
                    self.wait("sync", "pec", self.kch_free[par])
                    ldk = self.dma("sync", t["kch"][par],
                                   t["Kfm"][p * 128:(p + 1) * 128, :],
                                   sem=f"l_kc{par}")
                    prev = None
                    for kt in range(cfg.NKT + 1):
                        if kt < cfg.NKT:
                            kpar = kt % 2
                            self.wait("sync", "pe", self.vsl_free[kpar])
                            self.dma("sync", t["vsl"][kpar][:, 0:64],
                                     t["Xn"][kt * 128:(kt + 1) * 128,
                                             (2 * p) * 64:(2 * p + 1) * 64],
                                     sem=f"l_vs{kpar}")
                            ldvs = self.dma("sync", t["vsl"][kpar][:, 65:129],
                                            t["Xn"][kt * 128:(kt + 1) * 128,
                                                    (2 * p + 1) * 64:(2 * p + 2) * 64],
                                            sem=f"l_vs{kpar}")
                            if kt == 0:
                                self.wait("tensor", f"l_kc{par}", ldk)
                                self.wait("tensor", "l_qf", qf_ld)
                            self.wait("tensor", "act", self.mega_freeA[kpar])
                            self.wait("tensor", "act", self.mega_freeB[kpar])
                            self.opq("tensor", lambda e, p=p, kt=kt, kpar=kpar, par=par:
                                     e.matmul(t["megaA"][kpar],
                                              t["kch"][par][0:64, kt * 128:(kt + 1) * 128],
                                              t["qf"][0:64, p * SW:(p + 1) * SW],
                                              start=True, stop=True, tile_position=(0, 0)))
                            self.op("tensor", "pec", lambda e, p=p, kt=kt, kpar=kpar, par=par:
                                    e.matmul(t["megaB"][kpar],
                                             t["kch"][par][64:128, kt * 128:(kt + 1) * 128],
                                             t["qf"][64:128, p * SW:(p + 1) * SW],
                                             start=True, stop=True, tile_position=(64, 0)))
                            corr_pe = self.cnt["pec"]
                            self.wait("scalar", "pec", corr_pe)
                            self.wait("scalar", "pe", self.et_free[kpar])
                            self.op("scalar", "act", lambda e, kpar=kpar: e.activation(
                                t["etA"][kpar], t["megaA"][kpar], AF.Exp,
                                bias=cp[:, 3:4], scale=1.0))
                            self.op("scalar", "act", lambda e, kpar=kpar: e.activation(
                                t["etB"][kpar], t["megaB"][kpar], AF.Exp,
                                bias=cp[:, 3:4], scale=1.0))
                            exp_act = self.cnt["act"]
                            self.mega_freeA[kpar] = exp_act
                            self.mega_freeB[kpar] = exp_act
                            cur = (kt, kpar, exp_act, ldvs)
                        else:
                            cur = None
                        if prev is not None:
                            pkt, pkpar, pexp, pldvs = prev
                            self.wait("tensor", "act", pexp)
                            self.wait("tensor", f"l_vs{pkpar}", pldvs)
                            if pkt == 0:
                                self.wait("tensor", "act", self.nd_free_act)
                                self.wait("tensor", "dve", self.nd_free_dve)
                            self.opq("tensor", lambda e, pkt=pkt, pkpar=pkpar: e.matmul(
                                t["ndA"], t["vsl"][pkpar][:, 0:65], t["etA"][pkpar],
                                start=(pkt == 0), stop=(pkt == cfg.NKT - 1)))
                            self.op("tensor", "pe", lambda e, pkt=pkt, pkpar=pkpar: e.matmul(
                                t["ndB"], t["vsl"][pkpar][:, 65:130], t["etB"][pkpar],
                                start=(pkt == 0), stop=(pkt == cfg.NKT - 1)))
                            mm2_pe = self.cnt["pe"]
                            self.et_free[pkpar] = mm2_pe
                            self.vsl_free[pkpar] = mm2_pe
                        prev = cur
                    self.kch_free[par] = self.cnt["pec"]
                    nd_pe = self.cnt["pe"]
                    h0, h1 = 2 * p, 2 * p + 1
                    self.wait("scalar", "pe", nd_pe)
                    self.wait("scalar", "dve", self.attnN_free)
                    self.wait("scalar", "dve", self.dall_free)
                    self.op("scalar", "act", lambda e, p=p: e.copy(
                        t["attnN"][0:64, p * SW:(p + 1) * SW], t["ndA"][0:64, :]))
                    self.op("scalar", "act", lambda e, p=p: e.copy(
                        t["attnN"][64:128, p * SW:(p + 1) * SW], t["ndB"][0:64, :]))
                    self.op("scalar", "act", lambda e, h0=h0: e.copy(
                        t["Dall"][h0], t["ndA"][64:65, :]))
                    self.op("scalar", "act", lambda e, h1=h1: e.copy(
                        t["Dall"][h1], t["ndB"][64:65, :]))
                    self.nd_free_act = self.cnt["act"]
                    self.wait("vector", "act", self.nd_free_act)
                    self.wait("vector", "pe", self.drec_free)
                    self.vop(lambda e, h0=h0: e.reciprocal(t["Drec"][h0], t["Dall"][h0]))
                    self.vop(lambda e, h1=h1: e.reciprocal(t["Drec"][h1], t["Dall"][h1]))
                    self.dall_free = self.cnt["dve"]
                    self.wait("tensor", "dve", self.dall_free)
                    self.wait("tensor", "dve", self.ffn_free[0])
                    self.wait("tensor", "dve", self.bc_free)
                    self.opq("tensor", lambda e, h0=h0: e.matmul(
                        t["bcps"][0:64, :], t["onesf32c"][0:1, 0:64], t["Drec"][h0],
                        start=True, stop=True, tile_position=(0, 0)))
                    self.op("tensor", "pe", lambda e, h1=h1: e.matmul(
                        t["bcps"][64:128, :], t["onesf32c"][0:1, 0:64], t["Drec"][h1],
                        start=True, stop=True, tile_position=(0, 64)))
                    bc_pe = self.cnt["pe"]
                    self.drec_free = bc_pe
                    self.wait("sync", "dve", self.xoch_done)
                    xo_ld = self.dma("sync", t["xoch"],
                                     t["x_own"][p * 128:(p + 1) * 128,
                                                s * SW:(s + 1) * SW], sem="l_xo")
                    self.wait("vector", "pe", bc_pe)
                    self.wait("vector", "l_xo", xo_ld)
                    self.vop(lambda e, p=p, resid=resid: e.tensor_tensor(
                        resid[:, p * SW:(p + 1) * SW],
                        t["attnN"][:, p * SW:(p + 1) * SW], t["bcps"], ALU.mult))
                    self.bc_free = self.cnt["dve"]
                    self.vop(lambda e, p=p, resid=resid: e.tensor_tensor(
                        resid[:, p * SW:(p + 1) * SW],
                        resid[:, p * SW:(p + 1) * SW], t["xoch"], ALU.add))
                    self.xoch_done = self.cnt["dve"]
                    if s > 0:
                        for u in units[p * per_slot:(p + 1) * per_slot]:
                            emit_ffn_unit(s - 1, u[0], u[1])
                    pump_prep(10 if s == 0 else 8)
            else:
                pump_prep(len(self.prep_q))
                for u in units:
                    emit_ffn_unit(s - 1, u[0], u[1])

            if s < cfg.NSW:
                resid = t["resid"][s % 2]
                for ch in range(cfg.NCH):
                    self.vop(lambda e, ch=ch, resid=resid: e.tensor_tensor(
                        t["attnN"][:, ch * SW:(ch + 1) * SW],
                        resid[:, ch * SW:(ch + 1) * SW],
                        resid[:, ch * SW:(ch + 1) * SW], ALU.mult))
                sq_dve = self.cnt["dve"]
                self.wait("tensor", "dve", sq_dve)
                self.wait("tensor", "act", self.nd_free_act)
                self.wait("tensor", "dve", self.nd_free_dve)
                for ch in range(cfg.NCH):
                    if ch < cfg.NCH - 1:
                        self.opq("tensor", lambda e, ch=ch, resid=resid: e.matmul(
                            t["ndA"][0:1, :], t["onesf32"][:, 0:1],
                            resid[:, ch * SW:(ch + 1) * SW],
                            start=(ch == 0), stop=False))
                    else:
                        self.op("tensor", "pe", lambda e, ch=ch, resid=resid: e.matmul(
                            t["ndA"][0:1, :], t["onesf32"][:, 0:1],
                            resid[:, ch * SW:(ch + 1) * SW],
                            start=False, stop=True))
                for ch in range(cfg.NCH):
                    if ch < cfg.NCH - 1:
                        self.opq("tensor", lambda e, ch=ch: e.matmul(
                            t["ndB"][0:1, :], t["onesf32"][:, 0:1],
                            t["attnN"][:, ch * SW:(ch + 1) * SW],
                            start=(ch == 0), stop=False))
                    else:
                        self.op("tensor", "pe", lambda e, ch=ch: e.matmul(
                            t["ndB"][0:1, :], t["onesf32"][:, 0:1],
                            t["attnN"][:, ch * SW:(ch + 1) * SW],
                            start=False, stop=True))
                sums_pe = self.cnt["pe"]
                self.wait("vector", "pe", sums_pe)
                self.wait("vector", "pe", self.stats_free)
                self.vop(lambda e: e.tensor_scalar(
                    t["mrow"], t["ndA"][0:1, :], 1.0 / 1024.0, None, ALU.mult))
                self.vop(lambda e: e.tensor_tensor(
                    t["vrow"], t["mrow"], t["mrow"], ALU.mult))
                self.vop(lambda e: e.scalar_tensor_tensor(
                    t["vrow"], t["ndB"][0:1, :], 1.0 / 1024.0, t["vrow"],
                    ALU.mult, ALU.subtract))
                self.nd_free_dve = self.cnt["dve"]
                st_dve = self.cnt["dve"]
                self.wait("scalar", "dve", st_dve)
                self.op("scalar", "act", lambda e: e.activation(
                    t["vrow"], t["vrow"], AF.Ln, bias=cp[0:1, 1:2], scale=1.0))
                self.wait("scalar", "act", self.cnt["act"])
                self.op("scalar", "act", lambda e: e.activation(
                    t["vrow"], t["vrow"], AF.Exp, bias=0.0, scale=cp[0:1, 2:3]))
                self.wait("scalar", "act", self.cnt["act"])
                stat_act = self.cnt["act"]
                self.wait("tensor", "act", stat_act)
                self.wait("tensor", "dve", self.bc_free)
                self.wait("tensor", "dve", self.ffn_free[0])
                self.op("tensor", "pe", lambda e: e.matmul(
                    t["bcps"], t["onesf32c"], t["mrow"], start=True, stop=True))
                m_pe = self.cnt["pe"]
                self.stats_free = m_pe
                self.wait("vector", "pe", m_pe)
                self.wait("vector", "pe", self.xq_free)
                for ch in range(cfg.NCH):
                    self.vop(lambda e, ch=ch, resid=resid: e.tensor_tensor(
                        t["attnN"][:, ch * SW:(ch + 1) * SW],
                        resid[:, ch * SW:(ch + 1) * SW], t["bcps"], ALU.subtract))
                self.bc_free = self.cnt["dve"]
                self.wait("tensor", "dve", self.bc2_free)
                self.wait("tensor", "dve", self.ffn_free[1])
                self.op("tensor", "pe", lambda e: e.matmul(
                    t["bcps2"], t["onesf32c"], t["vrow"], start=True, stop=True))
                bc2_pe = self.cnt["pe"]
                self.stats_free = bc2_pe
                self.wait("vector", "pe", bc2_pe)
                for ch in range(cfg.NCH):
                    self.vop(lambda e, ch=ch: e.tensor_tensor(
                        t["attnN"][:, ch * SW:(ch + 1) * SW],
                        t["attnN"][:, ch * SW:(ch + 1) * SW], t["bcps2"], ALU.mult))
                    self.bc2_free = self.cnt["dve"]
                    self.q43_tile(t["attnN"][:, ch * SW:(ch + 1) * SW],
                                  t["xq"][:, ch * SW:(ch + 1) * SW], SW, 4)
                self.xq_ready = self.cnt["dve"]
                self.attnN_free = self.cnt["dve"]
                self.qf_free = self.cnt["pec"]
        self.wait("gpsimd", "s_out", self.cnt["s_out"])


def build_program(cfg):
    nc = bass.Bass("TRN2", target_bir_lowering=False, debug=False)
    T, D, HF, SW = cfg.T, cfg.D, cfg.HF, cfg.SW
    t = {}
    t["x_tok"] = nc.dram_tensor("x_tok", [T, D], F32, kind="ExternalInput").ap()
    t["x_own"] = nc.dram_tensor("x_own", [D, cfg.OWN], F32, kind="ExternalInput").ap()
    t["mask_own"] = nc.dram_tensor("mask_own", [D, cfg.OWN], F32,
                                   kind="ExternalInput").ap()
    t["w1"] = nc.dram_tensor("w1", [HF, D], F32, kind="ExternalInput").ap()
    t["w2"] = nc.dram_tensor("w2", [D, HF], F32, kind="ExternalInput").ap()
    t["b1f"] = nc.dram_tensor("b1f", [128, HF // 128], F32, kind="ExternalInput").ap()
    t["b2f"] = nc.dram_tensor("b2f", [128, D // 128], F32, kind="ExternalInput").ap()
    t["identin"] = nc.dram_tensor("identin", [128, 128], BF16,
                                  kind="ExternalInput").ap()
    t["out_fm"] = nc.dram_tensor("out_fm", [D, cfg.OWN], F32,
                                 kind="ExternalOutput").ap()
    t["Xn"] = nc.dram_tensor("Xn", [T, D], BF16, kind="Internal").ap()
    t["Kfm"] = nc.dram_tensor("Kfm", [D, T], BF16, kind="Internal").ap()
    t["w1qn"] = nc.dram_tensor("w1qn", [HF, D], BF16, kind="Internal").ap()
    t["w2qn"] = nc.dram_tensor("w2qn", [D, HF], BF16, kind="Internal").ap()
    t["w1qT"] = nc.dram_tensor("w1qT", [D, HF], BF16, kind="Internal").ap()
    t["w2qT"] = nc.dram_tensor("w2qT", [HF, D], BF16, kind="Internal").ap()

    with contextlib.ExitStack() as es:
        def sb(name, shape, dtype):
            return es.enter_context(nc.sbuf_tensor(name, shape, dtype))[:]

        t["cpool"] = sb("cpool", [128, len(CPOOL_VALS)], F32)
        t["onesf32"] = sb("onesf32", [128, 1], F32)
        t["onesf32c"] = sb("onesf32c", [1, 128], F32)
        t["xbuf"] = [sb(f"xbuf{i}", [128, D], F32) for i in range(2)]
        t["s1"] = sb("s1", [128, 1], F32)
        t["s2"] = sb("s2", [128, 1], F32)
        t["s3"] = sb("s3", [128, 1], F32)
        xnch = [es.enter_context(nc.sbuf_tensor(f"xnc{i}", [128, D], BF16))
                for i in range(2)]
        t["xnc"] = [h[:] for h in xnch]
        t["xnc3"] = [h.ap().rearrange("p (h c) -> p h c", h=cfg.H, c=64)
                     for h in xnch]
        t["kch"] = [sb(f"kch{i}", [128, T], BF16) for i in range(2)]
        t["b1q"] = sb("b1q", [128, HF // 128], F32)
        t["b2q"] = sb("b2q", [128, D // 128], F32)
        for nm in ("qa", "qb", "qc", "qd", "qe"):
            t[nm] = sb(nm, [128, SW], F32)
        t["qi"] = sb("qi", [128, SW], I32)
        t["qf"] = sb("qfq", [128, (D // 128) * SW], BF16)
        t["vsl"] = [sb(f"vsl{i}", [128, 130], BF16) for i in range(2)]
        t["etA"] = [sb(f"etA{i}", [128, SW], BF16) for i in range(2)]
        t["etB"] = [sb(f"etB{i}", [128, SW], BF16) for i in range(2)]
        t["attnN"] = sb("attnN", [128, cfg.NCH * SW], F32)
        t["attnR"] = sb("attnR", [128, cfg.NCH * SW], F32)
        t["sqch"] = sb("sqch", [128, cfg.NCH * SW], F32)
        t["xq"] = sb("xq", [128, cfg.NCH * SW], BF16)
        t["hq"] = sb("hq", [128, max(cfg.NHJ * SW, HF)], BF16)
        t["wbN"] = [sb(f"wbN{i}", [128, D], F32) for i in range(2)]
        t["stg"] = [sb(f"stg{i}", [128, 1024], BF16) for i in range(2)]
        dall = sb("Dall", [1, 2 * SW], F32)
        drec = sb("Drec", [1, 2 * SW], F32)
        t["Dall"] = [dall[:, (h % 2) * SW:(h % 2 + 1) * SW] for h in range(cfg.H)]
        t["Drec"] = [drec[:, (h % 2) * SW:(h % 2 + 1) * SW] for h in range(cfg.H)]
        t["mrow"] = sb("mrow", [1, SW], F32)
        t["vrow"] = sb("vrow", [1, SW], F32)
        t["xoch"] = sb("xoch", [128, SW], F32)
        t["mkch"] = sb("mkch", [128, SW], F32)
        t["ych"] = sb("ych", [128, SW], F32)
        t["w1t"] = [sb(f"w1t{i}", [128, D], BF16) for i in range(2)]
        t["w2t"] = [sb(f"w2t{i}", [128, HF], BF16) for i in range(2)]
        t["sq1"] = t["attnN"][:, 0:D]
        t["resid"] = [t["attnR"], t["sqch"]]

        t["megaA"] = [es.enter_context(nc.psum_tensor(f"megaA{i}", [128, SW], F32))[:]
                      for i in range(2)]
        t["megaB"] = [es.enter_context(nc.psum_tensor(f"megaB{i}", [128, SW], F32))[:]
                      for i in range(2)]
        t["ndA"] = es.enter_context(nc.psum_tensor("ndA", [65, SW], F32))[:]
        t["ndB"] = es.enter_context(nc.psum_tensor("ndB", [65, SW], F32))[:]
        t["ffnps"] = [es.enter_context(nc.psum_tensor(f"ffnps{i}", [128, SW], F32))[:]
                      for i in range(2)]
        t["bcps"] = t["ffnps"][0]
        t["bcps2"] = t["ffnps"][1]

        sems = {}
        semnames = ["gp", "dve", "act", "pe", "pec", "l_id", "l_b", "l_qf", "l_xo",
                    "l_mk", "s_out"]
        for i in range(2):
            semnames += [f"l_xb{i}", f"l_kc{i}", f"l_wb{i}", f"l_vs{i}",
                         f"l_w1{i}", f"l_w2{i}", f"s_vt{i}", f"s_tr{i}",
                         f"s_kc{i}", f"l_tr{i}", f"s_dst{i}", f"s_nat{i}"]
        for nm in semnames:
            sems[nm] = es.enter_context(nc.semaphore(nm))
        t["sems"] = sems

        with nc.Block() as blk:
            @blk.sync
            def _(sync):
                P(nc, cfg, t, "sync", sync).emit()

            @blk.gpsimd
            def _(g):
                P(nc, cfg, t, "gpsimd", g).emit()

            @blk.vector
            def _(v):
                P(nc, cfg, t, "vector", v).emit()

            @blk.scalar
            def _(sc):
                P(nc, cfg, t, "scalar", sc).emit()

            @blk.tensor
            def _(te):
                P(nc, cfg, t, "tensor", te).emit()
    return nc




def make_in_maps(x, w1, b1, w2, b2, cfg, n_cores=8):
    B, T, D = x.shape
    keep = dropout_keep_mask((B, T, D))
    mask = np.where(keep, np.float32(1.25), np.float32(0.0))
    ident = np.eye(128).astype(bf16)
    b1f = np.ascontiguousarray(b1.reshape(cfg.NHJ, 128).T)
    b2f = np.ascontiguousarray(b2.reshape(cfg.NCH, 128).T)
    per_batch = n_cores // B
    in_maps = []
    for c in range(n_cores):
        b = c // per_batch
        qlo = (c % per_batch) * cfg.OWN
        in_maps.append({
            "x_tok": np.ascontiguousarray(np.roll(x[b], -qlo, axis=0)),
            "x_own": np.ascontiguousarray(x[b, qlo:qlo + cfg.OWN].T),
            "mask_own": np.ascontiguousarray(mask[b, qlo:qlo + cfg.OWN].T),
            "w1": w1, "w2": w2, "b1f": b1f, "b2f": b2f,
            "identin": ident,
        })
    return in_maps


def kernel(L0_attn_x, L0_ffn_w1, L0_ffn_b1, L0_ffn_w2, L0_ffn_b2):
    x = np.ascontiguousarray(np.asarray(L0_attn_x, np.float32))
    w1 = np.ascontiguousarray(np.asarray(L0_ffn_w1, np.float32))
    b1 = np.asarray(L0_ffn_b1, np.float32)
    w2 = np.ascontiguousarray(np.asarray(L0_ffn_w2, np.float32))
    b2 = np.asarray(L0_ffn_b2, np.float32)
    B, T, D = x.shape
    cfg = Cfg(T=T, OWN=T * B // 8, SW=512)
    nc = build_program(cfg)
    in_maps = make_in_maps(x, w1, b1, w2, b2, cfg)
    res = run_bass_kernel_spmd(nc, in_maps, core_ids=list(range(8)))
    out = np.empty((B, T, D), np.float32)
    per_batch = 8 // B
    for c in range(8):
        b = c // per_batch
        qlo = (c % per_batch) * cfg.OWN
        out[b, qlo:qlo + cfg.OWN] = np.asarray(res.results[c]["out_fm"],
                                               np.float32).T
    return out


# version 23
# speedup vs baseline: 1.0033x; 1.0033x over previous
import contextlib
import numpy as np
import ml_dtypes
import concourse.bass as bass
import concourse.mybir as mybir
from concourse.bass_utils import run_bass_kernel_spmd

F32 = mybir.dt.float32
BF16 = mybir.dt.bfloat16
I32 = mybir.dt.int32
AF = mybir.ActivationFunctionType
ALU = mybir.AluOpType
LN2 = float(np.log(2.0))
bf16 = ml_dtypes.bfloat16



def _threefry2x32(k0, k1, x0, x1):
    R0 = [13, 15, 26, 6]
    R1 = [17, 29, 16, 24]
    ks0 = np.uint32(k0)
    ks1 = np.uint32(k1)
    ks2 = np.uint32(ks0 ^ ks1 ^ np.uint32(0x1BD11BDA))
    ks = [ks0, ks1, ks2]
    x0 = (x0 + ks[0]).astype(np.uint32)
    x1 = (x1 + ks[1]).astype(np.uint32)
    for i, rots in enumerate([R0, R1, R0, R1, R0]):
        for r in rots:
            x0 = (x0 + x1).astype(np.uint32)
            x1 = ((x1 << np.uint32(r)) | (x1 >> np.uint32(32 - r))).astype(np.uint32)
            x1 = (x1 ^ x0).astype(np.uint32)
        x0 = (x0 + ks[(i + 1) % 3]).astype(np.uint32)
        x1 = (x1 + ks[(i + 2) % 3] + np.uint32(i + 1)).astype(np.uint32)
    return x0, x1


def dropout_keep_mask(shape, key=42, p_keep=0.8):
    n = int(np.prod(shape))
    y0, y1 = _threefry2x32(0, key, np.zeros(n, np.uint32),
                           np.arange(n, dtype=np.uint32))
    bits = (y0 ^ y1).astype(np.uint32)
    f = ((bits >> np.uint32(9)) | np.uint32(0x3F800000)).view(np.float32) \
        - np.float32(1.0)
    return (f < np.float32(p_keep)).reshape(shape)




class Cfg:
    def __init__(self, T=8192, OWN=2048, SW=512):
        self.T = T
        self.OWN = OWN
        self.SW = SW
        self.D = 1024
        self.H = 16
        self.HD = 64
        self.HF = 4096
        self.NSW = OWN // SW
        self.NKT = T // 128
        self.NTT = T // 128
        self.NCH = self.D // 128
        self.NHJ = self.HF // 128


CPOOL_VALS = [
    1e-38,
    1e-5,
    -0.5,
    -80.0,
    -LN2,
    LN2,
    9.0 * LN2,
    -9.0 * LN2,
    24.0 * LN2,
    -24.0 * LN2,
]


class P:

    def __init__(self, nc, cfg, t, eng_name, eng):
        self.nc = nc
        self.cfg = cfg
        self.t = t
        self.eng_name = eng_name
        self.eng = eng
        self.sems = t["sems"]
        self.cnt = {k: 0 for k in self.sems}
        self.tr_evac = [0, 0]
        self.tr_store = [0, 0]
        self.vt_store = [0, 0]
        self.xb_done = [0, 0]
        self.kb_done = [0, 0]
        self.kch_free = [0, 0]
        self.vsl_free = [0, 0]
        self.mega_freeA = [0, 0]
        self.mega_freeB = [0, 0]
        self.et_free = [0, 0]
        self.nd_free = 0
        self.bc_free = 0
        self.bc2_free = 0
        self.mmps_free = [0, 0]
        self.w1t_free = [0, 0]
        self.w2t_free = [0, 0]
        self.qf_free = 0
        self.wbuf_done = [0, 0]
        self.ych_store = 0
        self.xoch_done = 0
        self.mkch_done = 0
        self.xnc_free = [0, 0]
        self.drec_free = 0
        self.stats_free = 0
        self.xq_free = 0
        self.hq_free = 0
        self.ffn_free = [0, 0]
        self.evac_mark = 0
        self.nd_free_act = 0
        self.nd_free_dve = 0
        self.attnN_free = 0
        self.dall_free = 0
        self.xq_ready = 0
        self.hq_ready = 0

    def on(self, name):
        return self.eng_name == name

    def bump(self, sem, n):
        self.cnt[sem] += n
        return self.cnt[sem]

    def wait(self, engname, sem, val):
        if self.on(engname) and val > 0:
            self.eng.wait_ge(self.sems[sem], val)

    def dma(self, q, out, in_, sem="ld"):
        v = self.bump(sem, 16)
        if self.on(q):
            self.eng.dma_start(out=out, in_=in_).then_inc(self.sems[sem], 16)
        return v

    def op(self, engname, sem, fn):
        v = self.bump(sem, 1)
        if self.on(engname):
            fn(self.eng).then_inc(self.sems[sem], 1)
        return v

    def opq(self, engname, fn):
        if self.on(engname):
            fn(self.eng)

    def vop(self, fn):
        self.op("vector", "dve", fn)
        self.wait("vector", "dve", self.cnt["dve"])
        return self.cnt["dve"]

    def ladder(self, z_ap, b_ap, nlev):
        for k in reversed(range(nlev)):
            c = float(2.0 ** k)
            self.vop(lambda e, c=c: e.tensor_scalar(b_ap, z_ap, c, None, ALU.is_ge))
            self.vop(lambda e, c=c: e.scalar_tensor_tensor(
                z_ap, b_ap, -c, z_ap, ALU.mult, ALU.add))

    def q43_tile(self, v_ap, out_ap, W, elev, bufs=None):
        t = self.t
        if bufs is None:
            qa = t["qa"][:, 0:W]
            qb = t["qb"][:, 0:W]
            qc = t["qc"][:, 0:W]
            qd = t["qd"][:, 0:W]
        else:
            qa, qb, qc, qd = (b[:, 0:W] for b in bufs)
        cp = t["cpool"]
        self.wait("scalar", "dve", self.cnt["dve"])
        self.op("scalar", "act", lambda e: e.activation(
            qa, v_ap, AF.Square, bias=0.0, scale=1.0))
        self.wait("scalar", "act", self.cnt["act"])
        self.op("scalar", "act", lambda e: e.activation(
            qb, qa, AF.Ln, bias=cp[:, 0:1], scale=1.0))
        self.wait("scalar", "act", self.cnt["act"])
        av = self.cnt["act"]
        self.wait("vector", "act", av)
        self.vop(lambda e: e.tensor_scalar(qb, qb, 0.5 / LN2, 6.0, ALU.mult, ALU.add))
        self.vop(lambda e: e.tensor_scalar(
            qb, qb, 0.0, float(2.0 ** elev) - 1e-3, ALU.max, ALU.min))
        self.vop(lambda e: e.tensor_copy(qc, qb))
        self.ladder(qb, qd, elev)
        self.vop(lambda e: e.tensor_tensor(qc, qc, qb, ALU.subtract))
        dv = self.cnt["dve"]
        self.wait("scalar", "dve", dv)
        self.op("scalar", "act", lambda e: e.activation(
            qb, qc, AF.Exp, bias=cp[:, 6:7], scale=cp[:, 4:5]))
        self.op("scalar", "act", lambda e: e.activation(
            qd, qc, AF.Exp, bias=cp[:, 7:8], scale=cp[:, 5:6]))
        av = self.cnt["act"]
        self.wait("vector", "act", av)
        self.vop(lambda e: e.tensor_tensor(qa, v_ap, qb, ALU.mult))
        self.vop(lambda e: e.tensor_scalar(qa, qa, 16.5, None, ALU.add))
        self.vop(lambda e: e.tensor_copy(qc, qa))
        self.ladder(qa, qb, 6)
        self.vop(lambda e: e.tensor_tensor(qc, qc, qa, ALU.subtract))
        self.vop(lambda e: e.scalar_tensor_tensor(
            out_ap, qc, -16.0, qd, ALU.add, ALU.mult))
        return self.cnt["dve"]

    def qh_tile(self, psum_ap, b1col, out_ap, W):
        t = self.t
        qa = t["qa"][:, 0:W]
        qb = t["qb"][:, 0:W]
        qc = t["qc"][:, 0:W]
        qd = t["qd"][:, 0:W]
        qe = t["qe"][:, 0:W]
        cp = t["cpool"]
        self.vop(lambda e: e.tensor_scalar(
            qa, psum_ap, b1col, 1e-30, ALU.add, ALU.max))
        self.evac_mark = self.cnt["dve"]
        dv = self.cnt["dve"]
        self.wait("scalar", "dve", dv)
        self.op("scalar", "act", lambda e: e.activation(
            qb, qa, AF.Ln, bias=0.0, scale=1.0))
        self.wait("scalar", "act", self.cnt["act"])
        av = self.cnt["act"]
        self.wait("vector", "act", av)
        self.vop(lambda e: e.tensor_scalar(qb, qb, 1.0 / LN2, 6.0, ALU.mult, ALU.add))
        self.vop(lambda e: e.tensor_scalar(qb, qb, 0.0, 12.0 - 1e-3, ALU.max, ALU.min))
        self.vop(lambda e: e.tensor_copy(qc, qb))
        self.ladder(qb, qd, 4)
        self.vop(lambda e: e.tensor_tensor(qc, qc, qb, ALU.subtract))
        dv = self.cnt["dve"]
        self.wait("scalar", "dve", dv)
        self.op("scalar", "act", lambda e: e.activation(
            qb, qc, AF.Exp, bias=cp[:, 6:7], scale=cp[:, 4:5]))
        self.op("scalar", "act", lambda e: e.activation(
            qd, qc, AF.Exp, bias=cp[:, 7:8], scale=cp[:, 5:6]))
        av = self.cnt["act"]
        self.wait("vector", "act", av)
        self.vop(lambda e: e.tensor_tensor(qa, qa, qb, ALU.mult))
        self.vop(lambda e: e.tensor_copy(qc, qa))
        self.ladder(qa, qe, 4)
        self.vop(lambda e: e.tensor_tensor(qc, qc, qa, ALU.subtract))
        self.vop(lambda e: e.tensor_scalar(qa, qa, -0.5, None, ALU.add))
        self.vop(lambda e: e.scalar_tensor_tensor(
            qb, qa, 0.0, qc, ALU.is_ge, ALU.add))
        self.vop(lambda e: e.tensor_tensor(qc, qc, qe, ALU.add))
        self.vop(lambda e: e.tensor_tensor(qa, qa, qa, ALU.mult))
        self.vop(lambda e: e.tensor_scalar(
            qa, qa, float(2.0 ** -16 * 1.002), None, ALU.is_le))
        self.vop(lambda e: e.tensor_tensor(qc, qc, qb, ALU.subtract))
        self.vop(lambda e: e.tensor_tensor(qc, qa, qc, ALU.mult))
        self.vop(lambda e: e.tensor_tensor(qb, qb, qc, ALU.add))
        self.vop(lambda e: e.tensor_tensor(out_ap, qb, qd, ALU.mult))
        return self.cnt["dve"]

    def qy_tile(self, psum_ap, b2col, out_ap, W):
        t = self.t
        qa = t["qa"][:, 0:W]
        qb = t["qb"][:, 0:W]
        qc = t["qc"][:, 0:W]
        qd = t["qd"][:, 0:W]
        qi = t["qi"][:, 0:W]
        cp = t["cpool"]
        self.vop(lambda e: e.tensor_scalar(qa, psum_ap, b2col, None, ALU.add))
        self.evac_mark = self.cnt["dve"]
        self.wait("scalar", "dve", self.cnt["dve"])
        self.op("scalar", "act", lambda e: e.activation(
            qb, qa, AF.Square, bias=0.0, scale=1.0))
        self.wait("scalar", "act", self.cnt["act"])
        self.op("scalar", "act", lambda e: e.activation(
            qb, qb, AF.Ln, bias=cp[:, 0:1], scale=1.0))
        self.wait("scalar", "act", self.cnt["act"])
        av = self.cnt["act"]
        self.wait("vector", "act", av)
        self.vop(lambda e: e.tensor_scalar(qb, qb, 0.5 / LN2, 14.0, ALU.mult, ALU.add))
        self.vop(lambda e: e.tensor_scalar(qb, qb, 0.0, 20.0 - 1e-3, ALU.max, ALU.min))
        self.vop(lambda e: e.tensor_copy(qc, qb))
        self.ladder(qb, qd, 5)
        self.vop(lambda e: e.tensor_tensor(qc, qc, qb, ALU.subtract))
        dv = self.cnt["dve"]
        self.wait("scalar", "dve", dv)
        self.op("scalar", "act", lambda e: e.activation(
            qb, qc, AF.Exp, bias=cp[:, 8:9], scale=cp[:, 4:5]))
        self.op("scalar", "act", lambda e: e.activation(
            qd, qc, AF.Exp, bias=cp[:, 9:10], scale=cp[:, 5:6]))
        av = self.cnt["act"]
        self.wait("vector", "act", av)
        self.vop(lambda e: e.tensor_tensor(qa, qa, qb, ALU.mult))
        self.vop(lambda e: e.tensor_copy(qi, qa))
        self.vop(lambda e: e.tensor_copy(qa, qi))
        self.vop(lambda e: e.tensor_tensor(out_ap, qa, qd, ALU.mult))
        return self.cnt["dve"]

    def emit(self):
        cfg = self.cfg
        t = self.t
        cp = t["cpool"]
        SW = cfg.SW

        id_ld = self.dma("sync", t["ident"], t["identin"], sem="l_id")
        for j, val in enumerate(CPOOL_VALS):
            self.op("gpsimd", "gp", lambda e, j=j, val=val: e.memset(
                cp[:, j:j + 1], val))
        for par in range(2):
            self.op("gpsimd", "gp", lambda e, par=par: e.memset(
                t["vsl"][par][:, 64:65], 1.0))
            self.op("gpsimd", "gp", lambda e, par=par: e.memset(
                t["vsl"][par][:, 129:130], 1.0))
        self.op("gpsimd", "gp", lambda e: e.memset(t["onesbf"], 1.0))
        self.op("gpsimd", "gp", lambda e: e.memset(t["onesf32"], 1.0))
        self.op("gpsimd", "gp", lambda e: e.memset(t["onesf32c"], 1.0))
        gp_init = self.cnt["gp"]
        for en in ("vector", "scalar", "tensor"):
            self.wait(en, "gp", gp_init)
        self.wait("tensor", "l_id", id_ld)

        for tt in range(cfg.NTT):
            par = tt % 2
            xbuf = t["xbuf"][par]
            self.wait("sync", "dve", self.xb_done[par])
            ldv = self.dma("sync", xbuf, t["x_tok"][tt * 128:(tt + 1) * 128, :],
                           sem=f"l_xb{par}")
            self.wait("vector", f"l_xb{par}", ldv)
            self.vop(lambda e: e.tensor_tensor(t["sq1"], xbuf, xbuf, ALU.mult))
            self.vop(lambda e: e.reduce_sum(t["s1"], xbuf,
                                            axis=mybir.AxisListType.X))
            self.vop(lambda e: e.reduce_sum(t["s2"], t["sq1"],
                                            axis=mybir.AxisListType.X))
            self.vop(lambda e: e.tensor_scalar(
                t["s1"], t["s1"], 1.0 / 1024.0, None, ALU.mult))
            self.vop(lambda e: e.tensor_tensor(t["s3"], t["s1"], t["s1"], ALU.mult))
            self.vop(lambda e: e.scalar_tensor_tensor(
                t["s2"], t["s2"], 1.0 / 1024.0, t["s3"], ALU.mult, ALU.subtract))
            dvv = self.cnt["dve"]
            self.wait("scalar", "dve", dvv)
            self.op("scalar", "act", lambda e: e.activation(
                t["s2"], t["s2"], AF.Ln, bias=cp[:, 1:2], scale=1.0))
            self.wait("scalar", "act", self.cnt["act"])
            self.op("scalar", "act", lambda e: e.activation(
                t["s2"], t["s2"], AF.Exp, bias=0.0, scale=cp[:, 2:3]))
            self.wait("scalar", "act", self.cnt["act"])
            avv = self.cnt["act"]
            self.wait("vector", "act", avv)
            self.wait("vector", f"s_vt{par}", self.vt_store[par])
            self.vop(lambda e, par=par: e.tensor_scalar(
                t["xnc"][par], xbuf, t["s1"][:, 0:1], t["s2"][:, 0:1],
                ALU.subtract, ALU.mult))
            self.xb_done[par] = self.cnt["dve"]
            xnc_dve = self.cnt["dve"]
            self.wait("gpsimd", "dve", xnc_dve)
            self.vt_store[par] = self.dma(
                "gpsimd", t["Xn"][tt * 128:(tt + 1) * 128, :],
                t["xnc"][par], sem=f"s_vt{par}")
        p0_vt = [self.cnt["s_vt0"], self.cnt["s_vt1"]]
        for ch in range(cfg.NCH):
            par = ch % 2
            self.wait("sync", "s_vt0", p0_vt[0])
            self.wait("sync", "s_vt1", p0_vt[1])
            self.wait("sync", f"s_kc{par}", self.kb_done[par])
            if self.on("sync"):
                self.eng.dma_start_transpose(
                    t["kch"][par], t["Xn"][:, ch * 128:(ch + 1) * 128]
                ).then_inc(self.sems[f"l_kc{par}"], 16)
            ldv = self.bump(f"l_kc{par}", 16)
            self.wait("vector", f"l_kc{par}", ldv)
            self.vop(lambda e, par=par: e.reduce_sum(
                t["s1"], t["kch"][par], axis=mybir.AxisListType.X))
            self.vop(lambda e: e.tensor_scalar(
                t["s1"], t["s1"], 1.0 / float(cfg.T), None, ALU.mult))
            self.vop(lambda e, par=par: e.tensor_scalar(
                t["kch"][par], t["kch"][par], t["s1"][:, 0:1], None, ALU.subtract))
            kcd = self.cnt["dve"]
            self.wait("gpsimd", "dve", kcd)
            self.kb_done[par] = self.dma(
                "gpsimd", t["Kfm"][ch * 128:(ch + 1) * 128, :], t["kch"][par],
                sem=f"s_kc{par}")
        p1_kc = [self.cnt["s_kc0"], self.cnt["s_kc1"]]
        self.dma("sync", t["b1q"], t["b1f"], sem="l_b")
        ldb = self.dma("sync", t["b2q"], t["b2f"], sem="l_b")
        self.wait("vector", "l_b", ldb)
        self.wait("scalar", "l_b", ldb)
        self.q43_tile(t["b1q"], t["b1q"], cfg.NHJ, 3)
        self.q43_tile(t["b2q"], t["b2q"], cfg.NCH, 3)

        def w_prep(wsrc, nat, dst, nrow_tiles, ncols, stage, stage_sem):
            if cfg.NCH * SW >= 4096:
                WQ = 1024
                wbufs = (t["attnN"][:, 0:WQ], t["attnN"][:, WQ:2 * WQ],
                         t["attnN"][:, 2 * WQ:3 * WQ], t["attnN"][:, 3 * WQ:4 * WQ])
            else:
                WQ = SW
                wbufs = None
            for j in range(nrow_tiles):
                par = j % 2
                wb = t["wbuf"][par][:, 0:ncols]
                self.wait("sync", "dve", self.wbuf_done[par])
                ldv = self.dma("sync", wb, wsrc[j * 128:(j + 1) * 128, :],
                               sem=f"l_wb{par}")
                self.wait("vector", f"l_wb{par}", ldv)
                self.wait("scalar", f"l_wb{par}", ldv)
                self.wait("vector", "s_tr0", self.tr_store[0])
                for sl in range(ncols // WQ):
                    self.q43_tile(wb[:, sl * WQ:(sl + 1) * WQ],
                                  t["wq"][:, sl * WQ:(sl + 1) * WQ], WQ, 3,
                                  bufs=wbufs)
                self.wbuf_done[par] = self.cnt["dve"]
                self.wait("gpsimd", "dve", self.wbuf_done[par])
                self.wait("gpsimd", "s_tr0", self.tr_store[0])
                self.tr_store[0] = self.dma(
                    "gpsimd", nat[j * 128:(j + 1) * 128, :],
                    t["wq"][:, 0:ncols], sem="s_tr0")
            nat_done = self.cnt["s_tr0"]
            for ch in range(ncols // 128):
                self.wait("sync", "s_tr0", nat_done)
                self.wait("sync", "s_tr1", self.tr_store[1])
                if self.on("sync"):
                    self.eng.dma_start_transpose(
                        stage[:, 0:nrow_tiles * 128],
                        nat[:, ch * 128:(ch + 1) * 128]
                    ).then_inc(self.sems[stage_sem], 16)
                sld = self.bump(stage_sem, 16)
                self.wait("gpsimd", stage_sem, sld)
                self.tr_store[1] = self.dma(
                    "gpsimd", dst[ch * 128:(ch + 1) * 128, :],
                    stage[:, 0:nrow_tiles * 128], sem="s_tr1")

        w_prep(t["w1"], t["w1qn"], t["w1qT"], cfg.NHJ, cfg.D,
               t["w2t"][0], "l_w20")
        w_prep(t["w2"], t["w2qn"], t["w2qT"], cfg.NCH, cfg.HF,
               t["w1t"][0], "l_w10")
        p2_tr = [self.cnt["s_tr0"], self.cnt["s_tr1"]]
        self.wait("vector", "s_tr0", p2_tr[0])
        self.wait("vector", "s_tr1", p2_tr[1])

        def emit_ffn_unit(fs, kind, idx):
            resid = t["resid"][fs % 2]
            if kind == "hj":
                hj = idx
                mpar = hj % 2
                self.wait("sync", "pe", self.w1t_free[mpar])
                w1_ld = self.dma(
                    "sync", t["w1t3"][mpar],
                    t["w1qTr"][:, :, hj * 128:(hj + 1) * 128],
                    sem=f"l_w1{mpar}")
                self.wait("tensor", f"l_w1{mpar}", w1_ld)
                self.wait("tensor", "dve", self.xq_ready)
                self.wait("tensor", "dve", self.ffn_free[mpar])
                if mpar == 0:
                    self.wait("tensor", "dve", self.bc_free)
                else:
                    self.wait("tensor", "dve", self.bc2_free)
                for ch in range(cfg.NCH - 1):
                    self.opq("tensor", lambda e, ch=ch, mpar=mpar: e.matmul(
                        t["ffnps"][mpar], t["w1t"][mpar][:, ch * 128:(ch + 1) * 128],
                        t["xq"][:, ch * SW:(ch + 1) * SW],
                        start=(ch == 0), stop=False))
                self.op("tensor", "pe", lambda e, mpar=mpar: e.matmul(
                    t["ffnps"][mpar],
                    t["w1t"][mpar][:, (cfg.NCH - 1) * 128:cfg.NCH * 128],
                    t["xq"][:, (cfg.NCH - 1) * SW:cfg.NCH * SW],
                    start=False, stop=True))
                mm1_pe = self.cnt["pe"]
                self.w1t_free[mpar] = mm1_pe
                self.xq_free = mm1_pe
                self.wait("vector", "pe", mm1_pe)
                self.qh_tile(t["ffnps"][mpar], t["b1q"][:, hj:hj + 1],
                             t["hq"][:, hj * SW:(hj + 1) * SW], SW)
                self.ffn_free[mpar] = self.evac_mark
                if hj == cfg.NHJ - 1:
                    self.hq_ready = self.cnt["dve"]
            else:
                dj = idx
                mpar = dj % 2
                self.wait("sync", "pe", self.w2t_free[mpar])
                w2_ld = self.dma(
                    "sync", t["w2t3"][mpar],
                    t["w2qTr"][:, :, dj * 128:(dj + 1) * 128],
                    sem=f"l_w2{mpar}")
                self.wait("tensor", f"l_w2{mpar}", w2_ld)
                self.wait("tensor", "dve", self.hq_ready)
                self.wait("tensor", "dve", self.ffn_free[mpar])
                if mpar == 0:
                    self.wait("tensor", "dve", self.bc_free)
                else:
                    self.wait("tensor", "dve", self.bc2_free)
                for k in range(cfg.NHJ - 1):
                    self.opq("tensor", lambda e, k=k, mpar=mpar: e.matmul(
                        t["ffnps"][mpar], t["w2t"][mpar][:, k * 128:(k + 1) * 128],
                        t["hq"][:, k * SW:(k + 1) * SW],
                        start=(k == 0), stop=False))
                self.op("tensor", "pe", lambda e, mpar=mpar: e.matmul(
                    t["ffnps"][mpar],
                    t["w2t"][mpar][:, (cfg.NHJ - 1) * 128:cfg.NHJ * 128],
                    t["hq"][:, (cfg.NHJ - 1) * SW:cfg.NHJ * SW],
                    start=False, stop=True))
                mm2f_pe = self.cnt["pe"]
                self.w2t_free[mpar] = mm2f_pe
                self.hq_free = mm2f_pe
                self.wait("vector", "pe", mm2f_pe)
                self.wait("vector", "s_out", self.ych_store)
                self.qy_tile(t["ffnps"][mpar], t["b2q"][:, dj:dj + 1], t["ych"], SW)
                self.ffn_free[mpar] = self.evac_mark
                self.wait("sync", "dve", self.mkch_done)
                mk_ld = self.dma("sync", t["mkch"],
                                 t["mask_own"][dj * 128:(dj + 1) * 128,
                                               fs * SW:(fs + 1) * SW], sem="l_mk")
                self.wait("vector", "l_mk", mk_ld)
                self.vop(lambda e: e.tensor_tensor(
                    t["ych"], t["ych"], t["mkch"], ALU.mult))
                self.mkch_done = self.cnt["dve"]
                self.vop(lambda e, dj=dj, resid=resid: e.tensor_tensor(
                    t["ych"], t["ych"], resid[:, dj * SW:(dj + 1) * SW],
                    ALU.add))
                ych_dve = self.cnt["dve"]
                self.wait("gpsimd", "dve", ych_dve)
                self.ych_store = self.dma(
                    "gpsimd", t["out_fm"][dj * 128:(dj + 1) * 128,
                                          fs * SW:(fs + 1) * SW],
                    t["ych"], sem="s_out")

        units = [("hj", j) for j in range(cfg.NHJ)] + \
                [("dj", j) for j in range(cfg.NCH)]
        per_slot = (len(units) + cfg.H // 2 - 1) // (cfg.H // 2)

        for s in range(cfg.NSW + 1):
            if s < cfg.NSW:
                resid = t["resid"][s % 2]
                self.wait("sync", "s_tr0", p2_tr[0])
                self.wait("sync", "s_tr1", p2_tr[1])
                self.wait("sync", "s_kc0", p1_kc[0])
                self.wait("sync", "s_kc1", p1_kc[1])
                self.wait("sync", "s_vt0", p0_vt[0])
                self.wait("sync", "s_vt1", p0_vt[1])
                self.wait("sync", "pec", self.qf_free)
                for ch in range(cfg.NCH):
                    self.dma("sync", t["qf"][:, ch * SW:(ch + 1) * SW],
                             t["Kfm"][ch * 128:(ch + 1) * 128,
                                      s * SW:(s + 1) * SW], sem="l_qf")
                qf_ld = self.cnt["l_qf"]

                for p in range(cfg.H // 2):
                    par = p % 2
                    self.wait("sync", "pec", self.kch_free[par])
                    ldk = self.dma("sync", t["kch"][par],
                                   t["Kfm"][p * 128:(p + 1) * 128, :],
                                   sem=f"l_kc{par}")
                    prev = None
                    for kt in range(cfg.NKT + 1):
                        if kt < cfg.NKT:
                            kpar = kt % 2
                            self.wait("gpsimd", "pe", self.vsl_free[kpar])
                            ldvs = self.dma(
                                "gpsimd", t["vsl3"][kpar][:, :, 0:64],
                                t["XnG"][kt * 128:(kt + 1) * 128,
                                         2 * p:2 * p + 2, :],
                                sem=f"l_vs{kpar}")
                            if kt == 0:
                                self.wait("tensor", f"l_kc{par}", ldk)
                                self.wait("tensor", "l_qf", qf_ld)
                            self.wait("tensor", "act", self.mega_freeA[kpar])
                            self.wait("tensor", "act", self.mega_freeB[kpar])
                            self.opq("tensor", lambda e, p=p, kt=kt, kpar=kpar, par=par:
                                     e.matmul(t["megaA"][kpar],
                                              t["kch"][par][0:64, kt * 128:(kt + 1) * 128],
                                              t["qf"][0:64, p * SW:(p + 1) * SW],
                                              start=True, stop=True, tile_position=(0, 0)))
                            self.op("tensor", "pec", lambda e, p=p, kt=kt, kpar=kpar, par=par:
                                    e.matmul(t["megaB"][kpar],
                                             t["kch"][par][64:128, kt * 128:(kt + 1) * 128],
                                             t["qf"][64:128, p * SW:(p + 1) * SW],
                                             start=True, stop=True, tile_position=(64, 0)))
                            corr_pe = self.cnt["pec"]
                            self.wait("scalar", "pec", corr_pe)
                            self.wait("scalar", "pe", self.et_free[kpar])
                            self.op("scalar", "act", lambda e, kpar=kpar: e.activation(
                                t["etA"][kpar], t["megaA"][kpar], AF.Exp,
                                bias=cp[:, 3:4], scale=1.0))
                            self.op("scalar", "act", lambda e, kpar=kpar: e.activation(
                                t["etB"][kpar], t["megaB"][kpar], AF.Exp,
                                bias=cp[:, 3:4], scale=1.0))
                            exp_act = self.cnt["act"]
                            self.mega_freeA[kpar] = exp_act
                            self.mega_freeB[kpar] = exp_act
                            cur = (kt, kpar, exp_act, ldvs)
                        else:
                            cur = None
                        if prev is not None:
                            pkt, pkpar, pexp, pldvs = prev
                            self.wait("tensor", "act", pexp)
                            self.wait("tensor", f"l_vs{pkpar}", pldvs)
                            if pkt == 0:
                                self.wait("tensor", "act", self.nd_free_act)
                                self.wait("tensor", "dve", self.nd_free_dve)
                            self.opq("tensor", lambda e, pkt=pkt, pkpar=pkpar: e.matmul(
                                t["ndA"], t["vsl"][pkpar][:, 0:65], t["etA"][pkpar],
                                start=(pkt == 0), stop=(pkt == cfg.NKT - 1)))
                            self.op("tensor", "pe", lambda e, pkt=pkt, pkpar=pkpar: e.matmul(
                                t["ndB"], t["vsl"][pkpar][:, 65:130], t["etB"][pkpar],
                                start=(pkt == 0), stop=(pkt == cfg.NKT - 1)))
                            mm2_pe = self.cnt["pe"]
                            self.et_free[pkpar] = mm2_pe
                            self.vsl_free[pkpar] = mm2_pe
                        prev = cur
                    self.kch_free[par] = self.cnt["pec"]
                    nd_pe = self.cnt["pe"]
                    h0, h1 = 2 * p, 2 * p + 1
                    self.wait("scalar", "pe", nd_pe)
                    self.wait("scalar", "dve", self.attnN_free)
                    self.wait("scalar", "dve", self.dall_free)
                    self.op("scalar", "act", lambda e, p=p: e.copy(
                        t["attnN"][0:64, p * SW:(p + 1) * SW], t["ndA"][0:64, :]))
                    self.op("scalar", "act", lambda e, p=p: e.copy(
                        t["attnN"][64:128, p * SW:(p + 1) * SW], t["ndB"][0:64, :]))
                    self.op("scalar", "act", lambda e, h0=h0: e.copy(
                        t["Dall"][h0], t["ndA"][64:65, :]))
                    self.op("scalar", "act", lambda e, h1=h1: e.copy(
                        t["Dall"][h1], t["ndB"][64:65, :]))
                    self.nd_free_act = self.cnt["act"]
                    self.wait("vector", "act", self.nd_free_act)
                    self.wait("vector", "pe", self.drec_free)
                    self.vop(lambda e, h0=h0: e.reciprocal(t["Drec"][h0], t["Dall"][h0]))
                    self.vop(lambda e, h1=h1: e.reciprocal(t["Drec"][h1], t["Dall"][h1]))
                    self.dall_free = self.cnt["dve"]
                    self.wait("tensor", "dve", self.dall_free)
                    self.wait("tensor", "dve", self.ffn_free[0])
                    self.wait("tensor", "dve", self.bc_free)
                    self.opq("tensor", lambda e, h0=h0: e.matmul(
                        t["bcps"][0:64, :], t["onesf32c"][0:1, 0:64], t["Drec"][h0],
                        start=True, stop=True, tile_position=(0, 0)))
                    self.op("tensor", "pe", lambda e, h1=h1: e.matmul(
                        t["bcps"][64:128, :], t["onesf32c"][0:1, 0:64], t["Drec"][h1],
                        start=True, stop=True, tile_position=(0, 64)))
                    bc_pe = self.cnt["pe"]
                    self.drec_free = bc_pe
                    self.wait("sync", "dve", self.xoch_done)
                    xo_ld = self.dma("sync", t["xoch"],
                                     t["x_own"][p * 128:(p + 1) * 128,
                                                s * SW:(s + 1) * SW], sem="l_xo")
                    self.wait("vector", "pe", bc_pe)
                    self.wait("vector", "l_xo", xo_ld)
                    self.vop(lambda e, p=p, resid=resid: e.tensor_tensor(
                        resid[:, p * SW:(p + 1) * SW],
                        t["attnN"][:, p * SW:(p + 1) * SW], t["bcps"], ALU.mult))
                    self.bc_free = self.cnt["dve"]
                    self.vop(lambda e, p=p, resid=resid: e.tensor_tensor(
                        resid[:, p * SW:(p + 1) * SW],
                        resid[:, p * SW:(p + 1) * SW], t["xoch"], ALU.add))
                    self.xoch_done = self.cnt["dve"]
                    if s > 0:
                        for u in units[p * per_slot:(p + 1) * per_slot]:
                            emit_ffn_unit(s - 1, u[0], u[1])
            else:
                for u in units:
                    emit_ffn_unit(s - 1, u[0], u[1])

            if s < cfg.NSW:
                resid = t["resid"][s % 2]
                for ch in range(cfg.NCH):
                    self.vop(lambda e, ch=ch, resid=resid: e.tensor_tensor(
                        t["attnN"][:, ch * SW:(ch + 1) * SW],
                        resid[:, ch * SW:(ch + 1) * SW],
                        resid[:, ch * SW:(ch + 1) * SW], ALU.mult))
                sq_dve = self.cnt["dve"]
                self.wait("tensor", "dve", sq_dve)
                self.wait("tensor", "act", self.nd_free_act)
                self.wait("tensor", "dve", self.nd_free_dve)
                for ch in range(cfg.NCH):
                    if ch < cfg.NCH - 1:
                        self.opq("tensor", lambda e, ch=ch, resid=resid: e.matmul(
                            t["ndA"][0:1, :], t["onesf32"][:, 0:1],
                            resid[:, ch * SW:(ch + 1) * SW],
                            start=(ch == 0), stop=False))
                    else:
                        self.op("tensor", "pe", lambda e, ch=ch, resid=resid: e.matmul(
                            t["ndA"][0:1, :], t["onesf32"][:, 0:1],
                            resid[:, ch * SW:(ch + 1) * SW],
                            start=False, stop=True))
                for ch in range(cfg.NCH):
                    if ch < cfg.NCH - 1:
                        self.opq("tensor", lambda e, ch=ch: e.matmul(
                            t["ndB"][0:1, :], t["onesf32"][:, 0:1],
                            t["attnN"][:, ch * SW:(ch + 1) * SW],
                            start=(ch == 0), stop=False))
                    else:
                        self.op("tensor", "pe", lambda e, ch=ch: e.matmul(
                            t["ndB"][0:1, :], t["onesf32"][:, 0:1],
                            t["attnN"][:, ch * SW:(ch + 1) * SW],
                            start=False, stop=True))
                sums_pe = self.cnt["pe"]
                self.wait("vector", "pe", sums_pe)
                self.wait("vector", "pe", self.stats_free)
                self.vop(lambda e: e.tensor_scalar(
                    t["mrow"], t["ndA"][0:1, :], 1.0 / 1024.0, None, ALU.mult))
                self.vop(lambda e: e.tensor_tensor(
                    t["vrow"], t["mrow"], t["mrow"], ALU.mult))
                self.vop(lambda e: e.scalar_tensor_tensor(
                    t["vrow"], t["ndB"][0:1, :], 1.0 / 1024.0, t["vrow"],
                    ALU.mult, ALU.subtract))
                self.nd_free_dve = self.cnt["dve"]
                st_dve = self.cnt["dve"]
                self.wait("scalar", "dve", st_dve)
                self.op("scalar", "act", lambda e: e.activation(
                    t["vrow"], t["vrow"], AF.Ln, bias=cp[0:1, 1:2], scale=1.0))
                self.wait("scalar", "act", self.cnt["act"])
                self.op("scalar", "act", lambda e: e.activation(
                    t["vrow"], t["vrow"], AF.Exp, bias=0.0, scale=cp[0:1, 2:3]))
                self.wait("scalar", "act", self.cnt["act"])
                stat_act = self.cnt["act"]
                self.wait("tensor", "act", stat_act)
                self.wait("tensor", "dve", self.bc_free)
                self.wait("tensor", "dve", self.ffn_free[0])
                self.op("tensor", "pe", lambda e: e.matmul(
                    t["bcps"], t["onesf32c"], t["mrow"], start=True, stop=True))
                m_pe = self.cnt["pe"]
                self.stats_free = m_pe
                self.wait("vector", "pe", m_pe)
                self.wait("vector", "pe", self.xq_free)
                for ch in range(cfg.NCH):
                    self.vop(lambda e, ch=ch, resid=resid: e.tensor_tensor(
                        t["attnN"][:, ch * SW:(ch + 1) * SW],
                        resid[:, ch * SW:(ch + 1) * SW], t["bcps"], ALU.subtract))
                self.bc_free = self.cnt["dve"]
                self.wait("tensor", "dve", self.bc2_free)
                self.wait("tensor", "dve", self.ffn_free[1])
                self.op("tensor", "pe", lambda e: e.matmul(
                    t["bcps2"], t["onesf32c"], t["vrow"], start=True, stop=True))
                bc2_pe = self.cnt["pe"]
                self.stats_free = bc2_pe
                self.wait("vector", "pe", bc2_pe)
                for ch in range(cfg.NCH):
                    self.vop(lambda e, ch=ch: e.tensor_tensor(
                        t["attnN"][:, ch * SW:(ch + 1) * SW],
                        t["attnN"][:, ch * SW:(ch + 1) * SW], t["bcps2"], ALU.mult))
                    self.bc2_free = self.cnt["dve"]
                    self.q43_tile(t["attnN"][:, ch * SW:(ch + 1) * SW],
                                  t["xq"][:, ch * SW:(ch + 1) * SW], SW, 4)
                self.xq_ready = self.cnt["dve"]
                self.attnN_free = self.cnt["dve"]
                self.qf_free = self.cnt["pec"]
        self.wait("gpsimd", "s_out", self.cnt["s_out"])


def build_program(cfg):
    nc = bass.Bass("TRN2", target_bir_lowering=False, debug=False)
    T, D, HF, SW = cfg.T, cfg.D, cfg.HF, cfg.SW
    t = {}
    t["x_tok"] = nc.dram_tensor("x_tok", [T, D], F32, kind="ExternalInput").ap()
    t["x_own"] = nc.dram_tensor("x_own", [D, cfg.OWN], F32, kind="ExternalInput").ap()
    t["mask_own"] = nc.dram_tensor("mask_own", [D, cfg.OWN], F32,
                                   kind="ExternalInput").ap()
    t["w1"] = nc.dram_tensor("w1", [HF, D], F32, kind="ExternalInput").ap()
    t["w2"] = nc.dram_tensor("w2", [D, HF], F32, kind="ExternalInput").ap()
    t["b1f"] = nc.dram_tensor("b1f", [128, HF // 128], F32, kind="ExternalInput").ap()
    t["b2f"] = nc.dram_tensor("b2f", [128, D // 128], F32, kind="ExternalInput").ap()
    t["identin"] = nc.dram_tensor("identin", [128, 128], BF16,
                                  kind="ExternalInput").ap()
    t["out_fm"] = nc.dram_tensor("out_fm", [D, cfg.OWN], F32,
                                 kind="ExternalOutput").ap()
    t["Xn"] = nc.dram_tensor("Xn", [T, D], BF16, kind="Internal").ap()
    t["XnG"] = t["Xn"].rearrange("t (g c) -> t g c", g=D // 64, c=64)
    t["Kfm"] = nc.dram_tensor("Kfm", [D, T], BF16, kind="Internal").ap()
    t["w1qn"] = nc.dram_tensor("w1qn", [HF, D], BF16, kind="Internal").ap()
    t["w2qn"] = nc.dram_tensor("w2qn", [D, HF], BF16, kind="Internal").ap()
    t["w1qT"] = nc.dram_tensor("w1qT", [D, HF], BF16, kind="Internal").ap()
    t["w2qT"] = nc.dram_tensor("w2qT", [HF, D], BF16, kind="Internal").ap()
    t["w1qTr"] = t["w1qT"].rearrange("(c r) h -> r c h", c=D // 128, r=128)
    t["w2qTr"] = t["w2qT"].rearrange("(c r) d -> r c d", c=HF // 128, r=128)

    with contextlib.ExitStack() as es:
        def sb(name, shape, dtype):
            return es.enter_context(nc.sbuf_tensor(name, shape, dtype))[:]

        t["cpool"] = sb("cpool", [128, len(CPOOL_VALS)], F32)
        t["ident"] = sb("ident", [128, 128], BF16)
        t["onesbf"] = sb("onesbf", [1, 128], BF16)
        t["onesf32"] = sb("onesf32", [128, 1], F32)
        t["onesf32c"] = sb("onesf32c", [1, 128], F32)
        t["xbuf"] = [sb(f"xbuf{i}", [128, D], F32) for i in range(2)]
        t["s1"] = sb("s1", [128, 1], F32)
        t["s2"] = sb("s2", [128, 1], F32)
        t["s3"] = sb("s3", [128, 1], F32)
        xnch = [es.enter_context(nc.sbuf_tensor(f"xnc{i}", [128, D], BF16))
                for i in range(2)]
        t["xnc"] = [h[:] for h in xnch]
        t["xnc3"] = [h.ap().rearrange("p (h c) -> p h c", h=cfg.H, c=64)
                     for h in xnch]
        t["kch"] = [sb(f"kch{i}", [128, T], BF16) for i in range(2)]
        t["b1q"] = sb("b1q", [128, HF // 128], F32)
        t["b2q"] = sb("b2q", [128, D // 128], F32)
        for nm in ("qa", "qb", "qc", "qd", "qe"):
            t[nm] = sb(nm, [128, SW], F32)
        t["qi"] = sb("qi", [128, SW], I32)
        t["qf"] = sb("qfq", [128, (D // 128) * SW], BF16)
        vslh = [es.enter_context(nc.sbuf_tensor(f"vsl{i}", [128, 130], BF16))
                for i in range(2)]
        t["vsl"] = [h[:] for h in vslh]
        t["vsl3"] = [h.ap().rearrange("p (j c) -> p j c", j=2, c=65)
                     for h in vslh]
        t["etA"] = [sb(f"etA{i}", [128, SW], BF16) for i in range(2)]
        t["etB"] = [sb(f"etB{i}", [128, SW], BF16) for i in range(2)]
        t["attnN"] = sb("attnN", [128, cfg.NCH * SW], F32)
        t["attnR"] = sb("attnR", [128, cfg.NCH * SW], F32)
        t["sqch"] = sb("sqch", [128, cfg.NCH * SW], F32)
        t["xq"] = sb("xq", [128, cfg.NCH * SW], BF16)
        t["hq"] = sb("hq", [128, max(cfg.NHJ * SW, HF)], BF16)
        t["wq"] = t["hq"][:, 0:HF]
        dall = sb("Dall", [1, 2 * SW], F32)
        drec = sb("Drec", [1, 2 * SW], F32)
        t["Dall"] = [dall[:, (h % 2) * SW:(h % 2 + 1) * SW] for h in range(cfg.H)]
        t["Drec"] = [drec[:, (h % 2) * SW:(h % 2 + 1) * SW] for h in range(cfg.H)]
        t["mrow"] = sb("mrow", [1, SW], F32)
        t["vrow"] = sb("vrow", [1, SW], F32)
        t["xoch"] = sb("xoch", [128, SW], F32)
        t["mkch"] = sb("mkch", [128, SW], F32)
        t["ych"] = sb("ych", [128, SW], F32)
        w1th = [es.enter_context(nc.sbuf_tensor(f"w1t{i}", [128, D], BF16))
                for i in range(2)]
        t["w1t"] = [h[:] for h in w1th]
        t["w1t3"] = [h.ap().rearrange("p (c k) -> p c k", c=cfg.NCH, k=128)
                     for h in w1th]
        w2th = [es.enter_context(nc.sbuf_tensor(f"w2t{i}", [128, HF], BF16))
                for i in range(2)]
        t["w2t"] = [h[:] for h in w2th]
        t["w2t3"] = [h.ap().rearrange("p (c k) -> p c k", c=cfg.NHJ, k=128)
                     for h in w2th]
        t["sq1"] = t["attnN"][:, 0:D]
        t["resid"] = [t["attnR"], t["sqch"]]
        if cfg.NCH * SW >= HF:
            t["wbuf"] = [t["attnR"][:, 0:HF], t["sqch"][:, 0:HF]]
        else:
            t["wbuf"] = [sb(f"wbuf{i}", [128, HF], F32) for i in range(2)]

        t["megaA"] = [es.enter_context(nc.psum_tensor(f"megaA{i}", [128, SW], F32))[:]
                      for i in range(2)]
        t["megaB"] = [es.enter_context(nc.psum_tensor(f"megaB{i}", [128, SW], F32))[:]
                      for i in range(2)]
        t["ndA"] = es.enter_context(nc.psum_tensor("ndA", [65, SW], F32))[:]
        t["ndB"] = es.enter_context(nc.psum_tensor("ndB", [65, SW], F32))[:]
        t["ffnps"] = [es.enter_context(nc.psum_tensor(f"ffnps{i}", [128, SW], F32))[:]
                      for i in range(2)]
        t["bcps"] = t["ffnps"][0]
        t["bcps2"] = t["ffnps"][1]

        sems = {}
        semnames = ["gp", "dve", "act", "pe", "pec", "l_id", "l_b", "l_qf", "l_xo",
                    "l_mk", "s_out"]
        for i in range(2):
            semnames += [f"l_xb{i}", f"l_kc{i}", f"l_wb{i}", f"l_vs{i}",
                         f"l_w1{i}", f"l_w2{i}", f"s_vt{i}", f"s_tr{i}",
                         f"s_kc{i}"]
        for nm in semnames:
            sems[nm] = es.enter_context(nc.semaphore(nm))
        t["sems"] = sems

        with nc.Block() as blk:
            @blk.sync
            def _(sync):
                P(nc, cfg, t, "sync", sync).emit()

            @blk.gpsimd
            def _(g):
                P(nc, cfg, t, "gpsimd", g).emit()

            @blk.vector
            def _(v):
                P(nc, cfg, t, "vector", v).emit()

            @blk.scalar
            def _(sc):
                P(nc, cfg, t, "scalar", sc).emit()

            @blk.tensor
            def _(te):
                P(nc, cfg, t, "tensor", te).emit()
    return nc




def make_in_maps(x, w1, b1, w2, b2, cfg, n_cores=8):
    B, T, D = x.shape
    keep = dropout_keep_mask((B, T, D))
    mask = np.where(keep, np.float32(1.25), np.float32(0.0))
    ident = np.eye(128).astype(bf16)
    b1f = np.ascontiguousarray(b1.reshape(cfg.NHJ, 128).T)
    b2f = np.ascontiguousarray(b2.reshape(cfg.NCH, 128).T)
    per_batch = n_cores // B
    in_maps = []
    for c in range(n_cores):
        b = c // per_batch
        qlo = (c % per_batch) * cfg.OWN
        in_maps.append({
            "x_tok": np.ascontiguousarray(np.roll(x[b], -qlo, axis=0)),
            "x_own": np.ascontiguousarray(x[b, qlo:qlo + cfg.OWN].T),
            "mask_own": np.ascontiguousarray(mask[b, qlo:qlo + cfg.OWN].T),
            "w1": w1, "w2": w2, "b1f": b1f, "b2f": b2f,
            "identin": ident,
        })
    return in_maps


def kernel(L0_attn_x, L0_ffn_w1, L0_ffn_b1, L0_ffn_w2, L0_ffn_b2):
    x = np.ascontiguousarray(np.asarray(L0_attn_x, np.float32))
    w1 = np.ascontiguousarray(np.asarray(L0_ffn_w1, np.float32))
    b1 = np.asarray(L0_ffn_b1, np.float32)
    w2 = np.ascontiguousarray(np.asarray(L0_ffn_w2, np.float32))
    b2 = np.asarray(L0_ffn_b2, np.float32)
    B, T, D = x.shape
    cfg = Cfg(T=T, OWN=T * B // 8, SW=512)
    nc = build_program(cfg)
    in_maps = make_in_maps(x, w1, b1, w2, b2, cfg)
    res = run_bass_kernel_spmd(nc, in_maps, core_ids=list(range(8)))
    out = np.empty((B, T, D), np.float32)
    per_batch = 8 // B
    for c in range(8):
        b = c // per_batch
        qlo = (c % per_batch) * cfg.OWN
        out[b, qlo:qlo + cfg.OWN] = np.asarray(res.results[c]["out_fm"],
                                               np.float32).T
    return out


# version 25
# speedup vs baseline: 1.1565x; 1.1527x over previous
import contextlib
import numpy as np
import ml_dtypes
import concourse.bass as bass
import concourse.mybir as mybir
from concourse.bass_utils import run_bass_kernel_spmd

F32 = mybir.dt.float32
BF16 = mybir.dt.bfloat16
I32 = mybir.dt.int32
AF = mybir.ActivationFunctionType
ALU = mybir.AluOpType
LN2 = float(np.log(2.0))
bf16 = ml_dtypes.bfloat16



def _threefry2x32(k0, k1, x0, x1):
    R0 = [13, 15, 26, 6]
    R1 = [17, 29, 16, 24]
    ks0 = np.uint32(k0)
    ks1 = np.uint32(k1)
    ks2 = np.uint32(ks0 ^ ks1 ^ np.uint32(0x1BD11BDA))
    ks = [ks0, ks1, ks2]
    x0 = (x0 + ks[0]).astype(np.uint32)
    x1 = (x1 + ks[1]).astype(np.uint32)
    for i, rots in enumerate([R0, R1, R0, R1, R0]):
        for r in rots:
            x0 = (x0 + x1).astype(np.uint32)
            x1 = ((x1 << np.uint32(r)) | (x1 >> np.uint32(32 - r))).astype(np.uint32)
            x1 = (x1 ^ x0).astype(np.uint32)
        x0 = (x0 + ks[(i + 1) % 3]).astype(np.uint32)
        x1 = (x1 + ks[(i + 2) % 3] + np.uint32(i + 1)).astype(np.uint32)
    return x0, x1


def dropout_keep_mask(shape, key=42, p_keep=0.8):
    n = int(np.prod(shape))
    y0, y1 = _threefry2x32(0, key, np.zeros(n, np.uint32),
                           np.arange(n, dtype=np.uint32))
    bits = (y0 ^ y1).astype(np.uint32)
    f = ((bits >> np.uint32(9)) | np.uint32(0x3F800000)).view(np.float32) \
        - np.float32(1.0)
    return (f < np.float32(p_keep)).reshape(shape)




class Cfg:
    def __init__(self, T=8192, OWN=2048, SW=512):
        self.T = T
        self.OWN = OWN
        self.SW = SW
        self.D = 1024
        self.H = 16
        self.HD = 64
        self.HF = 4096
        self.NSW = OWN // SW
        self.NKT = T // 128
        self.NTT = T // 128
        self.NCH = self.D // 128
        self.NHJ = self.HF // 128


CPOOL_VALS = [
    1e-38,
    1e-5,
    -0.5,
    -80.0,
    -LN2,
    LN2,
    9.0 * LN2,
    -9.0 * LN2,
    24.0 * LN2,
    -24.0 * LN2,
]


class P:

    def __init__(self, nc, cfg, t, eng_name, eng):
        self.nc = nc
        self.cfg = cfg
        self.t = t
        self.eng_name = eng_name
        self.eng = eng
        self.sems = t["sems"]
        self.cnt = {k: 0 for k in self.sems}
        self.tr_evac = [0, 0]
        self.tr_store = [0, 0]
        self.vt_store = [0, 0]
        self.vt_q = [(0, 0)] * 4
        self.xb_done = [0, 0]
        self.kb_done = [0, 0]
        self.kch_free = [0, 0]
        self.vsl_free = [0, 0, 0, 0]
        self.mega_freeA = [0, 0]
        self.mega_freeB = [0, 0]
        self.et_free = [0, 0]
        self.nd_free = 0
        self.bc_free = 0
        self.bc2_free = 0
        self.mmps_free = [0, 0]
        self.w1t_free = [0, 0]
        self.w2t_free = [0, 0]
        self.qf_free = 0
        self.wbuf_done = [0, 0]
        self.ych_store = 0
        self.xoch_done = 0
        self.mkch_done = 0
        self.xnc_free = [0, 0]
        self.drec_free = 0
        self.stats_free = 0
        self.xq_free = 0
        self.hq_free = 0
        self.ffn_free = [0, 0]
        self.evac_mark = 0
        self.nd_free_act = 0
        self.nd_free_dve = 0
        self.attnN_free = 0
        self.dall_free = 0
        self.xq_ready = 0
        self.hq_ready = 0

    def on(self, name):
        return self.eng_name == name

    def bump(self, sem, n):
        self.cnt[sem] += n
        return self.cnt[sem]

    def wait(self, engname, sem, val):
        if self.on(engname) and val > 0:
            self.eng.wait_ge(self.sems[sem], val)

    def dma(self, q, out, in_, sem="ld"):
        v = self.bump(sem, 16)
        if self.on(q):
            self.eng.dma_start(out=out, in_=in_).then_inc(self.sems[sem], 16)
        return v

    def op(self, engname, sem, fn):
        v = self.bump(sem, 1)
        if self.on(engname):
            fn(self.eng).then_inc(self.sems[sem], 1)
        return v

    def opq(self, engname, fn):
        if self.on(engname):
            fn(self.eng)

    def vop(self, fn):
        self.op("vector", "dve", fn)
        self.wait("vector", "dve", self.cnt["dve"])
        return self.cnt["dve"]

    def ladder(self, z_ap, b_ap, nlev):
        for k in reversed(range(nlev)):
            c = float(2.0 ** k)
            self.vop(lambda e, c=c: e.tensor_scalar(b_ap, z_ap, c, None, ALU.is_ge))
            self.vop(lambda e, c=c: e.scalar_tensor_tensor(
                z_ap, b_ap, -c, z_ap, ALU.mult, ALU.add))

    def q43_tile(self, v_ap, out_ap, W, elev, bufs=None):
        t = self.t
        if bufs is None:
            qa = t["qa"][:, 0:W]
            qb = t["qb"][:, 0:W]
            qc = t["qc"][:, 0:W]
            qd = t["qd"][:, 0:W]
        else:
            qa, qb, qc, qd = (b[:, 0:W] for b in bufs)
        cp = t["cpool"]
        self.wait("scalar", "dve", self.cnt["dve"])
        self.op("scalar", "act", lambda e: e.activation(
            qa, v_ap, AF.Square, bias=0.0, scale=1.0))
        self.wait("scalar", "act", self.cnt["act"])
        self.op("scalar", "act", lambda e: e.activation(
            qb, qa, AF.Ln, bias=cp[:, 0:1], scale=1.0))
        self.wait("scalar", "act", self.cnt["act"])
        av = self.cnt["act"]
        self.wait("vector", "act", av)
        self.vop(lambda e: e.tensor_scalar(qb, qb, 0.5 / LN2, 6.0, ALU.mult, ALU.add))
        self.vop(lambda e: e.tensor_scalar(
            qb, qb, 0.0, float(2.0 ** elev) - 1e-3, ALU.max, ALU.min))
        self.vop(lambda e: e.tensor_copy(qc, qb))
        self.ladder(qb, qd, elev)
        self.vop(lambda e: e.tensor_tensor(qc, qc, qb, ALU.subtract))
        dv = self.cnt["dve"]
        self.wait("scalar", "dve", dv)
        self.op("scalar", "act", lambda e: e.activation(
            qb, qc, AF.Exp, bias=cp[:, 6:7], scale=cp[:, 4:5]))
        self.op("scalar", "act", lambda e: e.activation(
            qd, qc, AF.Exp, bias=cp[:, 7:8], scale=cp[:, 5:6]))
        av = self.cnt["act"]
        self.wait("vector", "act", av)
        self.vop(lambda e: e.tensor_tensor(qa, v_ap, qb, ALU.mult))
        self.vop(lambda e: e.tensor_scalar(qa, qa, 16.5, None, ALU.add))
        self.vop(lambda e: e.tensor_copy(qc, qa))
        self.ladder(qa, qb, 6)
        self.vop(lambda e: e.tensor_tensor(qc, qc, qa, ALU.subtract))
        self.vop(lambda e: e.scalar_tensor_tensor(
            out_ap, qc, -16.0, qd, ALU.add, ALU.mult))
        return self.cnt["dve"]

    def qh_tile(self, psum_ap, b1col, out_ap, W):
        t = self.t
        qa = t["qa"][:, 0:W]
        qb = t["qb"][:, 0:W]
        qc = t["qc"][:, 0:W]
        qd = t["qd"][:, 0:W]
        qe = t["qe"][:, 0:W]
        cp = t["cpool"]
        self.vop(lambda e: e.tensor_scalar(
            qa, psum_ap, b1col, 1e-30, ALU.add, ALU.max))
        self.evac_mark = self.cnt["dve"]
        dv = self.cnt["dve"]
        self.wait("scalar", "dve", dv)
        self.op("scalar", "act", lambda e: e.activation(
            qb, qa, AF.Ln, bias=0.0, scale=1.0))
        self.wait("scalar", "act", self.cnt["act"])
        av = self.cnt["act"]
        self.wait("vector", "act", av)
        self.vop(lambda e: e.tensor_scalar(qb, qb, 1.0 / LN2, 6.0, ALU.mult, ALU.add))
        self.vop(lambda e: e.tensor_scalar(qb, qb, 0.0, 12.0 - 1e-3, ALU.max, ALU.min))
        self.vop(lambda e: e.tensor_copy(qc, qb))
        self.ladder(qb, qd, 4)
        self.vop(lambda e: e.tensor_tensor(qc, qc, qb, ALU.subtract))
        dv = self.cnt["dve"]
        self.wait("scalar", "dve", dv)
        self.op("scalar", "act", lambda e: e.activation(
            qb, qc, AF.Exp, bias=cp[:, 6:7], scale=cp[:, 4:5]))
        self.op("scalar", "act", lambda e: e.activation(
            qd, qc, AF.Exp, bias=cp[:, 7:8], scale=cp[:, 5:6]))
        av = self.cnt["act"]
        self.wait("vector", "act", av)
        self.vop(lambda e: e.tensor_tensor(qa, qa, qb, ALU.mult))
        self.vop(lambda e: e.tensor_copy(qc, qa))
        self.ladder(qa, qe, 4)
        self.vop(lambda e: e.tensor_tensor(qc, qc, qa, ALU.subtract))
        self.vop(lambda e: e.tensor_scalar(qa, qa, -0.5, None, ALU.add))
        self.vop(lambda e: e.scalar_tensor_tensor(
            qb, qa, 0.0, qc, ALU.is_ge, ALU.add))
        self.vop(lambda e: e.tensor_tensor(qc, qc, qe, ALU.add))
        self.vop(lambda e: e.tensor_tensor(qa, qa, qa, ALU.mult))
        self.vop(lambda e: e.tensor_scalar(
            qa, qa, float(2.0 ** -16 * 1.002), None, ALU.is_le))
        self.vop(lambda e: e.tensor_tensor(qc, qc, qb, ALU.subtract))
        self.vop(lambda e: e.tensor_tensor(qc, qa, qc, ALU.mult))
        self.vop(lambda e: e.tensor_tensor(qb, qb, qc, ALU.add))
        self.vop(lambda e: e.tensor_tensor(out_ap, qb, qd, ALU.mult))
        return self.cnt["dve"]

    def qy_tile(self, psum_ap, b2col, out_ap, W):
        t = self.t
        qa = t["qa"][:, 0:W]
        qb = t["qb"][:, 0:W]
        qc = t["qc"][:, 0:W]
        qd = t["qd"][:, 0:W]
        qi = t["qi"][:, 0:W]
        cp = t["cpool"]
        self.vop(lambda e: e.tensor_scalar(qa, psum_ap, b2col, None, ALU.add))
        self.evac_mark = self.cnt["dve"]
        self.wait("scalar", "dve", self.cnt["dve"])
        self.op("scalar", "act", lambda e: e.activation(
            qb, qa, AF.Square, bias=0.0, scale=1.0))
        self.wait("scalar", "act", self.cnt["act"])
        self.op("scalar", "act", lambda e: e.activation(
            qb, qb, AF.Ln, bias=cp[:, 0:1], scale=1.0))
        self.wait("scalar", "act", self.cnt["act"])
        av = self.cnt["act"]
        self.wait("vector", "act", av)
        self.vop(lambda e: e.tensor_scalar(qb, qb, 0.5 / LN2, 14.0, ALU.mult, ALU.add))
        self.vop(lambda e: e.tensor_scalar(qb, qb, 0.0, 20.0 - 1e-3, ALU.max, ALU.min))
        self.vop(lambda e: e.tensor_copy(qc, qb))
        self.ladder(qb, qd, 5)
        self.vop(lambda e: e.tensor_tensor(qc, qc, qb, ALU.subtract))
        dv = self.cnt["dve"]
        self.wait("scalar", "dve", dv)
        self.op("scalar", "act", lambda e: e.activation(
            qb, qc, AF.Exp, bias=cp[:, 8:9], scale=cp[:, 4:5]))
        self.op("scalar", "act", lambda e: e.activation(
            qd, qc, AF.Exp, bias=cp[:, 9:10], scale=cp[:, 5:6]))
        av = self.cnt["act"]
        self.wait("vector", "act", av)
        self.vop(lambda e: e.tensor_tensor(qa, qa, qb, ALU.mult))
        self.vop(lambda e: e.tensor_copy(qi, qa))
        self.vop(lambda e: e.tensor_copy(qa, qi))
        self.vop(lambda e: e.tensor_tensor(out_ap, qa, qd, ALU.mult))
        return self.cnt["dve"]

    def emit(self):
        cfg = self.cfg
        t = self.t
        cp = t["cpool"]
        SW = cfg.SW

        id_ld = self.dma("sync", t["ident"], t["identin"], sem="l_id")
        for j, val in enumerate(CPOOL_VALS):
            self.op("gpsimd", "gp", lambda e, j=j, val=val: e.memset(
                cp[:, j:j + 1], val))
        for par in range(4):
            self.op("gpsimd", "gp", lambda e, par=par: e.memset(
                t["vsl"][par][:, 64:65], 1.0))
            self.op("gpsimd", "gp", lambda e, par=par: e.memset(
                t["vsl"][par][:, 129:130], 1.0))
        self.op("gpsimd", "gp", lambda e: e.memset(t["onesbf"], 1.0))
        self.op("gpsimd", "gp", lambda e: e.memset(t["onesf32"], 1.0))
        self.op("gpsimd", "gp", lambda e: e.memset(t["onesf32c"], 1.0))
        gp_init = self.cnt["gp"]
        for en in ("vector", "scalar", "tensor"):
            self.wait(en, "gp", gp_init)
        self.wait("tensor", "l_id", id_ld)

        for tt in range(cfg.NTT):
            par = tt % 2
            xbuf = t["xbuf"][par]
            self.wait("sync", "dve", self.xb_done[par])
            ldv = self.dma("sync", xbuf, t["x_tok"][tt * 128:(tt + 1) * 128, :],
                           sem=f"l_xb{par}")
            self.wait("vector", f"l_xb{par}", ldv)
            self.vop(lambda e: e.tensor_tensor(t["sq1"], xbuf, xbuf, ALU.mult))
            self.vop(lambda e: e.reduce_sum(t["s1"], xbuf,
                                            axis=mybir.AxisListType.X))
            self.vop(lambda e: e.reduce_sum(t["s2"], t["sq1"],
                                            axis=mybir.AxisListType.X))
            self.vop(lambda e: e.tensor_scalar(
                t["s1"], t["s1"], 1.0 / 1024.0, None, ALU.mult))
            self.vop(lambda e: e.tensor_tensor(t["s3"], t["s1"], t["s1"], ALU.mult))
            self.vop(lambda e: e.scalar_tensor_tensor(
                t["s2"], t["s2"], 1.0 / 1024.0, t["s3"], ALU.mult, ALU.subtract))
            dvv = self.cnt["dve"]
            self.wait("scalar", "dve", dvv)
            self.op("scalar", "act", lambda e: e.activation(
                t["s2"], t["s2"], AF.Ln, bias=cp[:, 1:2], scale=1.0))
            self.wait("scalar", "act", self.cnt["act"])
            self.op("scalar", "act", lambda e: e.activation(
                t["s2"], t["s2"], AF.Exp, bias=0.0, scale=cp[:, 2:3]))
            self.wait("scalar", "act", self.cnt["act"])
            avv = self.cnt["act"]
            self.wait("vector", "act", avv)
            self.wait("vector", f"s_vt{par}", self.vt_store[par])
            self.vop(lambda e, par=par: e.tensor_scalar(
                t["xnc"][par], xbuf, t["s1"][:, 0:1], t["s2"][:, 0:1],
                ALU.subtract, ALU.mult))
            self.xb_done[par] = self.cnt["dve"]
            xnc_dve = self.cnt["dve"]
            self.wait("gpsimd", "dve", xnc_dve)
            self.vt_store[par] = self.dma(
                "gpsimd", t["Xn"][tt * 128:(tt + 1) * 128, :],
                t["xnc"][par], sem=f"s_vt{par}")
            if (tt + 1) % max(1, cfg.NTT // 4) == 0:
                self.vt_q[(tt + 1) // max(1, cfg.NTT // 4) - 1] = (
                    self.cnt["s_vt0"], self.cnt["s_vt1"])
        p0_vt = [self.cnt["s_vt0"], self.cnt["s_vt1"]]
        TQ = cfg.T // 4
        for ch in range(cfg.NCH):
            par = ch % 2
            self.wait("sync", f"s_kc{par}", self.kb_done[par])
            for c in range(4):
                self.wait("sync", "s_vt0", self.vt_q[c][0])
                self.wait("sync", "s_vt1", self.vt_q[c][1])
                if self.on("sync"):
                    self.eng.dma_start_transpose(
                        t["kch"][par][:, c * TQ:(c + 1) * TQ],
                        t["Xn"][c * TQ:(c + 1) * TQ, ch * 128:(ch + 1) * 128]
                    ).then_inc(self.sems[f"l_kc{par}"], 16)
                self.bump(f"l_kc{par}", 16)
            ldv = self.cnt[f"l_kc{par}"]
            self.wait("vector", f"l_kc{par}", ldv)
            self.vop(lambda e, par=par: e.reduce_sum(
                t["s1"], t["kch"][par], axis=mybir.AxisListType.X))
            self.vop(lambda e: e.tensor_scalar(
                t["s1"], t["s1"], 1.0 / float(cfg.T), None, ALU.mult))
            self.vop(lambda e, par=par: e.tensor_scalar(
                t["kch"][par], t["kch"][par], t["s1"][:, 0:1], None, ALU.subtract))
            kcd = self.cnt["dve"]
            self.wait("gpsimd", "dve", kcd)
            self.kb_done[par] = self.dma(
                "gpsimd", t["Kfm"][ch * 128:(ch + 1) * 128, :], t["kch"][par],
                sem=f"s_kc{par}")
        p1_kc = [self.cnt["s_kc0"], self.cnt["s_kc1"]]
        self.dma("sync", t["b1q"], t["b1f"], sem="l_b")
        ldb = self.dma("sync", t["b2q"], t["b2f"], sem="l_b")
        self.wait("vector", "l_b", ldb)
        self.wait("scalar", "l_b", ldb)
        self.q43_tile(t["b1q"], t["b1q"], cfg.NHJ, 3)
        self.q43_tile(t["b2q"], t["b2q"], cfg.NCH, 3)

        def w_prep(wsrc, nat, dst, nrow_tiles, ncols, stage, stage_sem):
            if cfg.NCH * SW >= 4096:
                WQ = 1024
                wbufs = (t["attnN"][:, 0:WQ], t["attnN"][:, WQ:2 * WQ],
                         t["attnN"][:, 2 * WQ:3 * WQ], t["attnN"][:, 3 * WQ:4 * WQ])
            else:
                WQ = SW
                wbufs = None
            for j in range(nrow_tiles):
                par = j % 2
                wb = t["wbuf"][par][:, 0:ncols]
                self.wait("sync", "dve", self.wbuf_done[par])
                ldv = self.dma("sync", wb, wsrc[j * 128:(j + 1) * 128, :],
                               sem=f"l_wb{par}")
                self.wait("vector", f"l_wb{par}", ldv)
                self.wait("scalar", f"l_wb{par}", ldv)
                self.wait("vector", "s_tr0", self.tr_store[0])
                for sl in range(ncols // WQ):
                    self.q43_tile(wb[:, sl * WQ:(sl + 1) * WQ],
                                  t["wq"][:, sl * WQ:(sl + 1) * WQ], WQ, 3,
                                  bufs=wbufs)
                self.wbuf_done[par] = self.cnt["dve"]
                self.wait("gpsimd", "dve", self.wbuf_done[par])
                self.wait("gpsimd", "s_tr0", self.tr_store[0])
                self.tr_store[0] = self.dma(
                    "gpsimd", nat[j * 128:(j + 1) * 128, :],
                    t["wq"][:, 0:ncols], sem="s_tr0")
            nat_done = self.cnt["s_tr0"]
            NRQ = nrow_tiles * 128 // 4
            for ch in range(ncols // 128):
                self.wait("sync", "s_tr0", nat_done)
                self.wait("sync", "s_tr1", self.tr_store[1])
                for c in range(4):
                    if self.on("sync"):
                        self.eng.dma_start_transpose(
                            stage[:, c * NRQ:(c + 1) * NRQ],
                            nat[c * NRQ:(c + 1) * NRQ,
                                ch * 128:(ch + 1) * 128]
                        ).then_inc(self.sems[stage_sem], 16)
                    self.bump(stage_sem, 16)
                sld = self.cnt[stage_sem]
                self.wait("gpsimd", stage_sem, sld)
                self.tr_store[1] = self.dma(
                    "gpsimd", dst[ch * 128:(ch + 1) * 128, :],
                    stage[:, 0:nrow_tiles * 128], sem="s_tr1")

        w_prep(t["w1"], t["w1qn"], t["w1qT"], cfg.NHJ, cfg.D,
               t["w2t"][0], "l_w20")
        w_prep(t["w2"], t["w2qn"], t["w2qT"], cfg.NCH, cfg.HF,
               t["w1t"][0], "l_w10")
        p2_tr = [self.cnt["s_tr0"], self.cnt["s_tr1"]]
        self.wait("vector", "s_tr0", p2_tr[0])
        self.wait("vector", "s_tr1", p2_tr[1])

        def emit_ffn_unit(fs, kind, idx):
            resid = t["resid"][fs % 2]
            if kind == "hj":
                hj = idx
                mpar = hj % 2
                self.wait("sync", "pe", self.w1t_free[mpar])
                w1_ld = self.dma(
                    "sync", t["w1t3"][mpar],
                    t["w1qTr"][:, :, hj * 128:(hj + 1) * 128],
                    sem=f"l_w1{mpar}")
                self.wait("tensor", f"l_w1{mpar}", w1_ld)
                self.wait("tensor", "dve", self.xq_ready)
                self.wait("tensor", "dve", self.ffn_free[mpar])
                if mpar == 0:
                    self.wait("tensor", "dve", self.bc_free)
                else:
                    self.wait("tensor", "dve", self.bc2_free)
                for ch in range(cfg.NCH - 1):
                    self.opq("tensor", lambda e, ch=ch, mpar=mpar: e.matmul(
                        t["ffnps"][mpar], t["w1t"][mpar][:, ch * 128:(ch + 1) * 128],
                        t["xq"][:, ch * SW:(ch + 1) * SW],
                        start=(ch == 0), stop=False))
                self.op("tensor", "pe", lambda e, mpar=mpar: e.matmul(
                    t["ffnps"][mpar],
                    t["w1t"][mpar][:, (cfg.NCH - 1) * 128:cfg.NCH * 128],
                    t["xq"][:, (cfg.NCH - 1) * SW:cfg.NCH * SW],
                    start=False, stop=True))
                mm1_pe = self.cnt["pe"]
                self.w1t_free[mpar] = mm1_pe
                self.xq_free = mm1_pe
                self.wait("vector", "pe", mm1_pe)
                self.qh_tile(t["ffnps"][mpar], t["b1q"][:, hj:hj + 1],
                             t["hq"][:, hj * SW:(hj + 1) * SW], SW)
                self.ffn_free[mpar] = self.evac_mark
                if hj == cfg.NHJ - 1:
                    self.hq_ready = self.cnt["dve"]
            else:
                dj = idx
                mpar = dj % 2
                self.wait("sync", "pe", self.w2t_free[mpar])
                w2_ld = self.dma(
                    "sync", t["w2t3"][mpar],
                    t["w2qTr"][:, :, dj * 128:(dj + 1) * 128],
                    sem=f"l_w2{mpar}")
                self.wait("tensor", f"l_w2{mpar}", w2_ld)
                self.wait("tensor", "dve", self.hq_ready)
                self.wait("tensor", "dve", self.ffn_free[mpar])
                if mpar == 0:
                    self.wait("tensor", "dve", self.bc_free)
                else:
                    self.wait("tensor", "dve", self.bc2_free)
                for k in range(cfg.NHJ - 1):
                    self.opq("tensor", lambda e, k=k, mpar=mpar: e.matmul(
                        t["ffnps"][mpar], t["w2t"][mpar][:, k * 128:(k + 1) * 128],
                        t["hq"][:, k * SW:(k + 1) * SW],
                        start=(k == 0), stop=False))
                self.op("tensor", "pe", lambda e, mpar=mpar: e.matmul(
                    t["ffnps"][mpar],
                    t["w2t"][mpar][:, (cfg.NHJ - 1) * 128:cfg.NHJ * 128],
                    t["hq"][:, (cfg.NHJ - 1) * SW:cfg.NHJ * SW],
                    start=False, stop=True))
                mm2f_pe = self.cnt["pe"]
                self.w2t_free[mpar] = mm2f_pe
                self.hq_free = mm2f_pe
                self.wait("vector", "pe", mm2f_pe)
                self.wait("vector", "s_out", self.ych_store)
                self.qy_tile(t["ffnps"][mpar], t["b2q"][:, dj:dj + 1], t["ych"], SW)
                self.ffn_free[mpar] = self.evac_mark
                self.wait("sync", "dve", self.mkch_done)
                mk_ld = self.dma("sync", t["mkch"],
                                 t["mask_own"][dj * 128:(dj + 1) * 128,
                                               fs * SW:(fs + 1) * SW], sem="l_mk")
                self.wait("vector", "l_mk", mk_ld)
                self.vop(lambda e: e.tensor_tensor(
                    t["ych"], t["ych"], t["mkch"], ALU.mult))
                self.mkch_done = self.cnt["dve"]
                self.vop(lambda e, dj=dj, resid=resid: e.tensor_tensor(
                    t["ych"], t["ych"], resid[:, dj * SW:(dj + 1) * SW],
                    ALU.add))
                ych_dve = self.cnt["dve"]
                self.wait("gpsimd", "dve", ych_dve)
                self.ych_store = self.dma(
                    "gpsimd", t["out_fm"][dj * 128:(dj + 1) * 128,
                                          fs * SW:(fs + 1) * SW],
                    t["ych"], sem="s_out")

        units = [("hj", j) for j in range(cfg.NHJ)] + \
                [("dj", j) for j in range(cfg.NCH)]
        per_slot = (len(units) + cfg.H // 2 - 1) // (cfg.H // 2)

        for s in range(cfg.NSW + 1):
            if s < cfg.NSW:
                resid = t["resid"][s % 2]
                self.wait("sync", "s_tr0", p2_tr[0])
                self.wait("sync", "s_tr1", p2_tr[1])
                self.wait("sync", "s_kc0", p1_kc[0])
                self.wait("sync", "s_kc1", p1_kc[1])
                self.wait("sync", "s_vt0", p0_vt[0])
                self.wait("sync", "s_vt1", p0_vt[1])
                self.wait("sync", "pec", self.qf_free)
                for ch in range(cfg.NCH):
                    self.dma("sync", t["qf"][:, ch * SW:(ch + 1) * SW],
                             t["Kfm"][ch * 128:(ch + 1) * 128,
                                      s * SW:(s + 1) * SW], sem="l_qf")
                qf_ld = self.cnt["l_qf"]

                for p in range(cfg.H // 2):
                    par = p % 2
                    self.wait("sync", "pec", self.kch_free[par])
                    ldk = self.dma("sync", t["kch"][par],
                                   t["Kfm"][p * 128:(p + 1) * 128, :],
                                   sem=f"l_kc{par}")
                    prev = None
                    for kt in range(cfg.NKT + 1):
                        if kt < cfg.NKT:
                            kpar = kt % 2
                            vpar = kt % 4
                            self.wait("sync", "pe", self.vsl_free[vpar])
                            ldvs = self.dma(
                                "sync", t["vsl3"][vpar][:, :, 0:64],
                                t["XnG"][kt * 128:(kt + 1) * 128,
                                         2 * p:2 * p + 2, :],
                                sem=f"l_vs{vpar}")
                            if kt == 0:
                                self.wait("tensor", f"l_kc{par}", ldk)
                                self.wait("tensor", "l_qf", qf_ld)
                            self.wait("tensor", "act", self.mega_freeA[kpar])
                            self.opq("tensor", lambda e, p=p, kt=kt, kpar=kpar, par=par:
                                     e.matmul(t["megaAB"][kpar][:, 0:SW],
                                              t["kch"][par][0:64, kt * 128:(kt + 1) * 128],
                                              t["qf"][0:64, p * SW:(p + 1) * SW],
                                              start=True, stop=True, tile_position=(0, 0)))
                            self.op("tensor", "pec", lambda e, p=p, kt=kt, kpar=kpar, par=par:
                                    e.matmul(t["megaAB"][kpar][:, SW:2 * SW],
                                             t["kch"][par][64:128, kt * 128:(kt + 1) * 128],
                                             t["qf"][64:128, p * SW:(p + 1) * SW],
                                             start=True, stop=True, tile_position=(64, 0)))
                            corr_pe = self.cnt["pec"]
                            self.wait("scalar", "pec", corr_pe)
                            self.wait("scalar", "pe", self.et_free[kpar])
                            self.op("scalar", "act", lambda e, kpar=kpar: e.activation(
                                t["etAB"][kpar], t["megaAB"][kpar], AF.Exp,
                                bias=cp[:, 3:4], scale=1.0))
                            exp_act = self.cnt["act"]
                            self.mega_freeA[kpar] = exp_act
                            cur = (kt, kpar, vpar, exp_act, ldvs)
                        else:
                            cur = None
                        if prev is not None:
                            pkt, pkpar, pvpar, pexp, pldvs = prev
                            self.wait("tensor", "act", pexp)
                            self.wait("tensor", f"l_vs{pvpar}", pldvs)
                            if pkt == 0:
                                self.wait("tensor", "act", self.nd_free_act)
                                self.wait("tensor", "dve", self.nd_free_dve)
                            self.opq("tensor", lambda e, pkt=pkt, pkpar=pkpar,
                                     pvpar=pvpar: e.matmul(
                                t["ndA"], t["vsl"][pvpar][:, 0:65],
                                t["etAB"][pkpar][:, 0:SW],
                                start=(pkt == 0), stop=(pkt == cfg.NKT - 1)))
                            self.op("tensor", "pe", lambda e, pkt=pkt, pkpar=pkpar,
                                    pvpar=pvpar: e.matmul(
                                t["ndB"], t["vsl"][pvpar][:, 65:130],
                                t["etAB"][pkpar][:, SW:2 * SW],
                                start=(pkt == 0), stop=(pkt == cfg.NKT - 1)))
                            mm2_pe = self.cnt["pe"]
                            self.et_free[pkpar] = mm2_pe
                            self.vsl_free[pvpar] = mm2_pe
                        prev = cur
                    self.kch_free[par] = self.cnt["pec"]
                    nd_pe = self.cnt["pe"]
                    h0, h1 = 2 * p, 2 * p + 1
                    self.wait("scalar", "pe", nd_pe)
                    self.wait("scalar", "dve", self.attnN_free)
                    self.wait("scalar", "dve", self.dall_free)
                    self.op("scalar", "act", lambda e, p=p: e.copy(
                        t["attnN"][0:64, p * SW:(p + 1) * SW], t["ndA"][0:64, :]))
                    self.op("scalar", "act", lambda e, p=p: e.copy(
                        t["attnN"][64:128, p * SW:(p + 1) * SW], t["ndB"][0:64, :]))
                    self.op("scalar", "act", lambda e, h0=h0: e.copy(
                        t["Dall"][h0], t["ndA"][64:65, :]))
                    self.op("scalar", "act", lambda e, h1=h1: e.copy(
                        t["Dall"][h1], t["ndB"][64:65, :]))
                    self.nd_free_act = self.cnt["act"]
                    self.wait("vector", "act", self.nd_free_act)
                    self.wait("vector", "pe", self.drec_free)
                    self.vop(lambda e, h0=h0: e.reciprocal(t["Drec"][h0], t["Dall"][h0]))
                    self.vop(lambda e, h1=h1: e.reciprocal(t["Drec"][h1], t["Dall"][h1]))
                    self.dall_free = self.cnt["dve"]
                    self.wait("tensor", "dve", self.dall_free)
                    self.wait("tensor", "dve", self.ffn_free[0])
                    self.wait("tensor", "dve", self.bc_free)
                    self.opq("tensor", lambda e, h0=h0: e.matmul(
                        t["bcps"][0:64, :], t["onesf32c"][0:1, 0:64], t["Drec"][h0],
                        start=True, stop=True, tile_position=(0, 0)))
                    self.op("tensor", "pe", lambda e, h1=h1: e.matmul(
                        t["bcps"][64:128, :], t["onesf32c"][0:1, 0:64], t["Drec"][h1],
                        start=True, stop=True, tile_position=(0, 64)))
                    bc_pe = self.cnt["pe"]
                    self.drec_free = bc_pe
                    self.wait("sync", "dve", self.xoch_done)
                    xo_ld = self.dma("sync", t["xoch"],
                                     t["x_own"][p * 128:(p + 1) * 128,
                                                s * SW:(s + 1) * SW], sem="l_xo")
                    self.wait("vector", "pe", bc_pe)
                    self.wait("vector", "l_xo", xo_ld)
                    self.vop(lambda e, p=p, resid=resid: e.tensor_tensor(
                        resid[:, p * SW:(p + 1) * SW],
                        t["attnN"][:, p * SW:(p + 1) * SW], t["bcps"], ALU.mult))
                    self.bc_free = self.cnt["dve"]
                    self.vop(lambda e, p=p, resid=resid: e.tensor_tensor(
                        resid[:, p * SW:(p + 1) * SW],
                        resid[:, p * SW:(p + 1) * SW], t["xoch"], ALU.add))
                    self.xoch_done = self.cnt["dve"]
                    if s > 0:
                        for u in units[p * per_slot:(p + 1) * per_slot]:
                            emit_ffn_unit(s - 1, u[0], u[1])
            else:
                for u in units:
                    emit_ffn_unit(s - 1, u[0], u[1])

            if s < cfg.NSW:
                resid = t["resid"][s % 2]
                for ch in range(cfg.NCH):
                    self.vop(lambda e, ch=ch, resid=resid: e.tensor_tensor(
                        t["attnN"][:, ch * SW:(ch + 1) * SW],
                        resid[:, ch * SW:(ch + 1) * SW],
                        resid[:, ch * SW:(ch + 1) * SW], ALU.mult))
                sq_dve = self.cnt["dve"]
                self.wait("tensor", "dve", sq_dve)
                self.wait("tensor", "act", self.nd_free_act)
                self.wait("tensor", "dve", self.nd_free_dve)
                for ch in range(cfg.NCH):
                    if ch < cfg.NCH - 1:
                        self.opq("tensor", lambda e, ch=ch, resid=resid: e.matmul(
                            t["ndA"][0:1, :], t["onesf32"][:, 0:1],
                            resid[:, ch * SW:(ch + 1) * SW],
                            start=(ch == 0), stop=False))
                    else:
                        self.op("tensor", "pe", lambda e, ch=ch, resid=resid: e.matmul(
                            t["ndA"][0:1, :], t["onesf32"][:, 0:1],
                            resid[:, ch * SW:(ch + 1) * SW],
                            start=False, stop=True))
                for ch in range(cfg.NCH):
                    if ch < cfg.NCH - 1:
                        self.opq("tensor", lambda e, ch=ch: e.matmul(
                            t["ndB"][0:1, :], t["onesf32"][:, 0:1],
                            t["attnN"][:, ch * SW:(ch + 1) * SW],
                            start=(ch == 0), stop=False))
                    else:
                        self.op("tensor", "pe", lambda e, ch=ch: e.matmul(
                            t["ndB"][0:1, :], t["onesf32"][:, 0:1],
                            t["attnN"][:, ch * SW:(ch + 1) * SW],
                            start=False, stop=True))
                sums_pe = self.cnt["pe"]
                self.wait("vector", "pe", sums_pe)
                self.wait("vector", "pe", self.stats_free)
                self.vop(lambda e: e.tensor_scalar(
                    t["mrow"], t["ndA"][0:1, :], 1.0 / 1024.0, None, ALU.mult))
                self.vop(lambda e: e.tensor_tensor(
                    t["vrow"], t["mrow"], t["mrow"], ALU.mult))
                self.vop(lambda e: e.scalar_tensor_tensor(
                    t["vrow"], t["ndB"][0:1, :], 1.0 / 1024.0, t["vrow"],
                    ALU.mult, ALU.subtract))
                self.nd_free_dve = self.cnt["dve"]
                st_dve = self.cnt["dve"]
                self.wait("scalar", "dve", st_dve)
                self.op("scalar", "act", lambda e: e.activation(
                    t["vrow"], t["vrow"], AF.Ln, bias=cp[0:1, 1:2], scale=1.0))
                self.wait("scalar", "act", self.cnt["act"])
                self.op("scalar", "act", lambda e: e.activation(
                    t["vrow"], t["vrow"], AF.Exp, bias=0.0, scale=cp[0:1, 2:3]))
                self.wait("scalar", "act", self.cnt["act"])
                stat_act = self.cnt["act"]
                self.wait("tensor", "act", stat_act)
                self.wait("tensor", "dve", self.bc_free)
                self.wait("tensor", "dve", self.ffn_free[0])
                self.op("tensor", "pe", lambda e: e.matmul(
                    t["bcps"], t["onesf32c"], t["mrow"], start=True, stop=True))
                m_pe = self.cnt["pe"]
                self.stats_free = m_pe
                self.wait("vector", "pe", m_pe)
                self.wait("vector", "pe", self.xq_free)
                for ch in range(cfg.NCH):
                    self.vop(lambda e, ch=ch, resid=resid: e.tensor_tensor(
                        t["attnN"][:, ch * SW:(ch + 1) * SW],
                        resid[:, ch * SW:(ch + 1) * SW], t["bcps"], ALU.subtract))
                self.bc_free = self.cnt["dve"]
                self.wait("tensor", "dve", self.bc2_free)
                self.wait("tensor", "dve", self.ffn_free[1])
                self.op("tensor", "pe", lambda e: e.matmul(
                    t["bcps2"], t["onesf32c"], t["vrow"], start=True, stop=True))
                bc2_pe = self.cnt["pe"]
                self.stats_free = bc2_pe
                self.wait("vector", "pe", bc2_pe)
                for ch in range(cfg.NCH):
                    self.vop(lambda e, ch=ch: e.tensor_tensor(
                        t["attnN"][:, ch * SW:(ch + 1) * SW],
                        t["attnN"][:, ch * SW:(ch + 1) * SW], t["bcps2"], ALU.mult))
                    self.bc2_free = self.cnt["dve"]
                    self.q43_tile(t["attnN"][:, ch * SW:(ch + 1) * SW],
                                  t["xq"][:, ch * SW:(ch + 1) * SW], SW, 4)
                self.xq_ready = self.cnt["dve"]
                self.attnN_free = self.cnt["dve"]
                self.qf_free = self.cnt["pec"]
        self.wait("gpsimd", "s_out", self.cnt["s_out"])


def build_program(cfg):
    nc = bass.Bass("TRN2", target_bir_lowering=False, debug=False)
    T, D, HF, SW = cfg.T, cfg.D, cfg.HF, cfg.SW
    t = {}
    t["x_tok"] = nc.dram_tensor("x_tok", [T, D], F32, kind="ExternalInput").ap()
    t["x_own"] = nc.dram_tensor("x_own", [D, cfg.OWN], F32, kind="ExternalInput").ap()
    t["mask_own"] = nc.dram_tensor("mask_own", [D, cfg.OWN], F32,
                                   kind="ExternalInput").ap()
    t["w1"] = nc.dram_tensor("w1", [HF, D], F32, kind="ExternalInput").ap()
    t["w2"] = nc.dram_tensor("w2", [D, HF], F32, kind="ExternalInput").ap()
    t["b1f"] = nc.dram_tensor("b1f", [128, HF // 128], F32, kind="ExternalInput").ap()
    t["b2f"] = nc.dram_tensor("b2f", [128, D // 128], F32, kind="ExternalInput").ap()
    t["identin"] = nc.dram_tensor("identin", [128, 128], BF16,
                                  kind="ExternalInput").ap()
    t["out_fm"] = nc.dram_tensor("out_fm", [D, cfg.OWN], F32,
                                 kind="ExternalOutput").ap()
    t["Xn"] = nc.dram_tensor("Xn", [T, D], BF16, kind="Internal").ap()
    t["XnG"] = t["Xn"].rearrange("t (g c) -> t g c", g=D // 64, c=64)
    t["Kfm"] = nc.dram_tensor("Kfm", [D, T], BF16, kind="Internal").ap()
    t["w1qn"] = nc.dram_tensor("w1qn", [HF, D], BF16, kind="Internal").ap()
    t["w2qn"] = nc.dram_tensor("w2qn", [D, HF], BF16, kind="Internal").ap()
    t["w1qT"] = nc.dram_tensor("w1qT", [D, HF], BF16, kind="Internal").ap()
    t["w2qT"] = nc.dram_tensor("w2qT", [HF, D], BF16, kind="Internal").ap()
    t["w1qTr"] = t["w1qT"].rearrange("(c r) h -> r c h", c=D // 128, r=128)
    t["w2qTr"] = t["w2qT"].rearrange("(c r) d -> r c d", c=HF // 128, r=128)

    with contextlib.ExitStack() as es:
        def sb(name, shape, dtype):
            return es.enter_context(nc.sbuf_tensor(name, shape, dtype))[:]

        t["cpool"] = sb("cpool", [128, len(CPOOL_VALS)], F32)
        t["ident"] = sb("ident", [128, 128], BF16)
        t["onesbf"] = sb("onesbf", [1, 128], BF16)
        t["onesf32"] = sb("onesf32", [128, 1], F32)
        t["onesf32c"] = sb("onesf32c", [1, 128], F32)
        t["xbuf"] = [sb(f"xbuf{i}", [128, D], F32) for i in range(2)]
        t["s1"] = sb("s1", [128, 1], F32)
        t["s2"] = sb("s2", [128, 1], F32)
        t["s3"] = sb("s3", [128, 1], F32)
        xnch = [es.enter_context(nc.sbuf_tensor(f"xnc{i}", [128, D], BF16))
                for i in range(2)]
        t["xnc"] = [h[:] for h in xnch]
        t["xnc3"] = [h.ap().rearrange("p (h c) -> p h c", h=cfg.H, c=64)
                     for h in xnch]
        t["kch"] = [sb(f"kch{i}", [128, T], BF16) for i in range(2)]
        t["b1q"] = sb("b1q", [128, HF // 128], F32)
        t["b2q"] = sb("b2q", [128, D // 128], F32)
        for nm in ("qa", "qb", "qc", "qd", "qe"):
            t[nm] = sb(nm, [128, SW], F32)
        t["qi"] = sb("qi", [128, SW], I32)
        t["qf"] = sb("qfq", [128, (D // 128) * SW], BF16)
        vslh = [es.enter_context(nc.sbuf_tensor(f"vsl{i}", [128, 130], BF16))
                for i in range(4)]
        t["vsl"] = [h[:] for h in vslh]
        t["vsl3"] = [h.ap().rearrange("p (j c) -> p j c", j=2, c=65)
                     for h in vslh]
        t["etAB"] = [sb(f"etAB{i}", [128, 2 * SW], BF16) for i in range(2)]
        t["attnN"] = sb("attnN", [128, cfg.NCH * SW], F32)
        t["attnR"] = sb("attnR", [128, cfg.NCH * SW], F32)
        t["sqch"] = sb("sqch", [128, cfg.NCH * SW], F32)
        t["xq"] = sb("xq", [128, cfg.NCH * SW], BF16)
        t["hq"] = sb("hq", [128, max(cfg.NHJ * SW, HF)], BF16)
        t["wq"] = t["hq"][:, 0:HF]
        dall = sb("Dall", [1, 2 * SW], F32)
        drec = sb("Drec", [1, 2 * SW], F32)
        t["Dall"] = [dall[:, (h % 2) * SW:(h % 2 + 1) * SW] for h in range(cfg.H)]
        t["Drec"] = [drec[:, (h % 2) * SW:(h % 2 + 1) * SW] for h in range(cfg.H)]
        t["mrow"] = sb("mrow", [1, SW], F32)
        t["vrow"] = sb("vrow", [1, SW], F32)
        t["xoch"] = sb("xoch", [128, SW], F32)
        t["mkch"] = sb("mkch", [128, SW], F32)
        t["ych"] = sb("ych", [128, SW], F32)
        w1th = [es.enter_context(nc.sbuf_tensor(f"w1t{i}", [128, D], BF16))
                for i in range(2)]
        t["w1t"] = [h[:] for h in w1th]
        t["w1t3"] = [h.ap().rearrange("p (c k) -> p c k", c=cfg.NCH, k=128)
                     for h in w1th]
        w2th = [es.enter_context(nc.sbuf_tensor(f"w2t{i}", [128, HF], BF16))
                for i in range(2)]
        t["w2t"] = [h[:] for h in w2th]
        t["w2t3"] = [h.ap().rearrange("p (c k) -> p c k", c=cfg.NHJ, k=128)
                     for h in w2th]
        t["sq1"] = t["attnN"][:, 0:D]
        t["resid"] = [t["attnR"], t["sqch"]]
        if cfg.NCH * SW >= HF:
            t["wbuf"] = [t["attnR"][:, 0:HF], t["sqch"][:, 0:HF]]
        else:
            t["wbuf"] = [sb(f"wbuf{i}", [128, HF], F32) for i in range(2)]

        t["megaAB"] = [es.enter_context(
            nc.psum_tensor(f"megaAB{i}", [128, 2 * SW], F32))[:] for i in range(2)]
        t["ndA"] = es.enter_context(nc.psum_tensor("ndA", [65, SW], F32))[:]
        t["ndB"] = es.enter_context(nc.psum_tensor("ndB", [65, SW], F32))[:]
        t["ffnps"] = [es.enter_context(nc.psum_tensor(f"ffnps{i}", [128, SW], F32))[:]
                      for i in range(2)]
        t["bcps"] = t["ffnps"][0]
        t["bcps2"] = t["ffnps"][1]

        sems = {}
        semnames = ["gp", "dve", "act", "pe", "pec", "l_id", "l_b", "l_qf", "l_xo",
                    "l_mk", "s_out"]
        for i in range(2):
            semnames += [f"l_xb{i}", f"l_kc{i}", f"l_wb{i}", f"l_vs{i}",
                         f"l_w1{i}", f"l_w2{i}", f"s_vt{i}", f"s_tr{i}",
                         f"s_kc{i}", f"l_vs{i + 2}"]
        for nm in semnames:
            sems[nm] = es.enter_context(nc.semaphore(nm))
        t["sems"] = sems

        with nc.Block() as blk:
            @blk.sync
            def _(sync):
                P(nc, cfg, t, "sync", sync).emit()

            @blk.gpsimd
            def _(g):
                P(nc, cfg, t, "gpsimd", g).emit()

            @blk.vector
            def _(v):
                P(nc, cfg, t, "vector", v).emit()

            @blk.scalar
            def _(sc):
                P(nc, cfg, t, "scalar", sc).emit()

            @blk.tensor
            def _(te):
                P(nc, cfg, t, "tensor", te).emit()
    return nc




def make_in_maps(x, w1, b1, w2, b2, cfg, n_cores=8):
    B, T, D = x.shape
    keep = dropout_keep_mask((B, T, D))
    mask = np.where(keep, np.float32(1.25), np.float32(0.0))
    ident = np.eye(128).astype(bf16)
    b1f = np.ascontiguousarray(b1.reshape(cfg.NHJ, 128).T)
    b2f = np.ascontiguousarray(b2.reshape(cfg.NCH, 128).T)
    per_batch = n_cores // B
    in_maps = []
    for c in range(n_cores):
        b = c // per_batch
        qlo = (c % per_batch) * cfg.OWN
        in_maps.append({
            "x_tok": np.ascontiguousarray(np.roll(x[b], -qlo, axis=0)),
            "x_own": np.ascontiguousarray(x[b, qlo:qlo + cfg.OWN].T),
            "mask_own": np.ascontiguousarray(mask[b, qlo:qlo + cfg.OWN].T),
            "w1": w1, "w2": w2, "b1f": b1f, "b2f": b2f,
            "identin": ident,
        })
    return in_maps


def kernel(L0_attn_x, L0_ffn_w1, L0_ffn_b1, L0_ffn_w2, L0_ffn_b2):
    x = np.ascontiguousarray(np.asarray(L0_attn_x, np.float32))
    w1 = np.ascontiguousarray(np.asarray(L0_ffn_w1, np.float32))
    b1 = np.asarray(L0_ffn_b1, np.float32)
    w2 = np.ascontiguousarray(np.asarray(L0_ffn_w2, np.float32))
    b2 = np.asarray(L0_ffn_b2, np.float32)
    B, T, D = x.shape
    cfg = Cfg(T=T, OWN=T * B // 8, SW=512)
    nc = build_program(cfg)
    in_maps = make_in_maps(x, w1, b1, w2, b2, cfg)
    res = run_bass_kernel_spmd(nc, in_maps, core_ids=list(range(8)))
    out = np.empty((B, T, D), np.float32)
    per_batch = 8 // B
    for c in range(8):
        b = c // per_batch
        qlo = (c % per_batch) * cfg.OWN
        out[b, qlo:qlo + cfg.OWN] = np.asarray(res.results[c]["out_fm"],
                                               np.float32).T
    return out


# version 26
# speedup vs baseline: 1.1782x; 1.0188x over previous
import contextlib
import numpy as np
import ml_dtypes
import concourse.bass as bass
import concourse.mybir as mybir
from concourse.bass_utils import run_bass_kernel_spmd

F32 = mybir.dt.float32
BF16 = mybir.dt.bfloat16
I32 = mybir.dt.int32
AF = mybir.ActivationFunctionType
ALU = mybir.AluOpType
LN2 = float(np.log(2.0))
bf16 = ml_dtypes.bfloat16



def _threefry2x32(k0, k1, x0, x1):
    R0 = [13, 15, 26, 6]
    R1 = [17, 29, 16, 24]
    ks0 = np.uint32(k0)
    ks1 = np.uint32(k1)
    ks2 = np.uint32(ks0 ^ ks1 ^ np.uint32(0x1BD11BDA))
    ks = [ks0, ks1, ks2]
    x0 = (x0 + ks[0]).astype(np.uint32)
    x1 = (x1 + ks[1]).astype(np.uint32)
    for i, rots in enumerate([R0, R1, R0, R1, R0]):
        for r in rots:
            x0 = (x0 + x1).astype(np.uint32)
            x1 = ((x1 << np.uint32(r)) | (x1 >> np.uint32(32 - r))).astype(np.uint32)
            x1 = (x1 ^ x0).astype(np.uint32)
        x0 = (x0 + ks[(i + 1) % 3]).astype(np.uint32)
        x1 = (x1 + ks[(i + 2) % 3] + np.uint32(i + 1)).astype(np.uint32)
    return x0, x1


def dropout_keep_mask(shape, key=42, p_keep=0.8):
    n = int(np.prod(shape))
    y0, y1 = _threefry2x32(0, key, np.zeros(n, np.uint32),
                           np.arange(n, dtype=np.uint32))
    bits = (y0 ^ y1).astype(np.uint32)
    f = ((bits >> np.uint32(9)) | np.uint32(0x3F800000)).view(np.float32) \
        - np.float32(1.0)
    return (f < np.float32(p_keep)).reshape(shape)




class Cfg:
    def __init__(self, T=8192, OWN=2048, SW=512):
        self.T = T
        self.OWN = OWN
        self.SW = SW
        self.D = 1024
        self.H = 16
        self.HD = 64
        self.HF = 4096
        self.NSW = OWN // SW
        self.NKT = T // 128
        self.NTT = T // 128
        self.NCH = self.D // 128
        self.NHJ = self.HF // 128


CPOOL_VALS = [
    1e-38,
    1e-5,
    -0.5,
    -80.0,
    -LN2,
    LN2,
    9.0 * LN2,
    -9.0 * LN2,
    24.0 * LN2,
    -24.0 * LN2,
]


class P:

    def __init__(self, nc, cfg, t, eng_name, eng):
        self.nc = nc
        self.cfg = cfg
        self.t = t
        self.eng_name = eng_name
        self.eng = eng
        self.sems = t["sems"]
        self.cnt = {k: 0 for k in self.sems}
        self.tr_evac = [0, 0]
        self.tr_store = [0, 0]
        self.vt_store = [0, 0]
        self.s1_act = 0
        self.xb_done = [0, 0]
        self.kb_done = [0, 0]
        self.kch_free = [0, 0]
        self.vsl_free = [0, 0, 0, 0]
        self.mega_freeA = [0, 0]
        self.mega_freeB = [0, 0]
        self.et_free = [0, 0]
        self.nd_free = 0
        self.bc_free = 0
        self.bc2_free = 0
        self.mmps_free = [0, 0]
        self.w1t_free = [0, 0]
        self.w2t_free = [0, 0]
        self.qf_free = 0
        self.wbuf_done = [0, 0]
        self.ych_store = 0
        self.xoch_done = 0
        self.mkch_done = 0
        self.xnc_free = [0, 0]
        self.drec_free = 0
        self.stats_free = 0
        self.xq_free = 0
        self.hq_free = 0
        self.ffn_free = [0, 0]
        self.evac_mark = 0
        self.nd_free_act = 0
        self.nd_free_dve = 0
        self.attnN_free = 0
        self.dall_free = 0
        self.xq_ready = 0
        self.hq_ready = 0

    def on(self, name):
        return self.eng_name == name

    def bump(self, sem, n):
        self.cnt[sem] += n
        return self.cnt[sem]

    def wait(self, engname, sem, val):
        if self.on(engname) and val > 0:
            self.eng.wait_ge(self.sems[sem], val)

    def dma(self, q, out, in_, sem="ld"):
        v = self.bump(sem, 16)
        if self.on(q):
            self.eng.dma_start(out=out, in_=in_).then_inc(self.sems[sem], 16)
        return v

    def op(self, engname, sem, fn):
        v = self.bump(sem, 1)
        if self.on(engname):
            fn(self.eng).then_inc(self.sems[sem], 1)
        return v

    def opq(self, engname, fn):
        if self.on(engname):
            fn(self.eng)

    def vop(self, fn):
        self.op("vector", "dve", fn)
        self.wait("vector", "dve", self.cnt["dve"])
        return self.cnt["dve"]

    def ladder(self, z_ap, b_ap, nlev):
        for k in reversed(range(nlev)):
            c = float(2.0 ** k)
            self.vop(lambda e, c=c: e.tensor_scalar(b_ap, z_ap, c, None, ALU.is_ge))
            self.vop(lambda e, c=c: e.scalar_tensor_tensor(
                z_ap, b_ap, -c, z_ap, ALU.mult, ALU.add))

    def q43_tile(self, v_ap, out_ap, W, elev, bufs=None):
        t = self.t
        if bufs is None:
            qa = t["qa"][:, 0:W]
            qb = t["qb"][:, 0:W]
            qc = t["qc"][:, 0:W]
            qd = t["qd"][:, 0:W]
        else:
            qa, qb, qc, qd = (b[:, 0:W] for b in bufs)
        cp = t["cpool"]
        self.wait("scalar", "dve", self.cnt["dve"])
        self.op("scalar", "act", lambda e: e.activation(
            qa, v_ap, AF.Square, bias=0.0, scale=1.0))
        self.wait("scalar", "act", self.cnt["act"])
        self.op("scalar", "act", lambda e: e.activation(
            qb, qa, AF.Ln, bias=cp[:, 0:1], scale=1.0))
        self.wait("scalar", "act", self.cnt["act"])
        av = self.cnt["act"]
        self.wait("vector", "act", av)
        self.vop(lambda e: e.tensor_scalar(qb, qb, 0.5 / LN2, 6.0, ALU.mult, ALU.add))
        self.vop(lambda e: e.tensor_scalar(
            qb, qb, 0.0, float(2.0 ** elev) - 1e-3, ALU.max, ALU.min))
        self.vop(lambda e: e.tensor_copy(qc, qb))
        self.ladder(qb, qd, elev)
        self.vop(lambda e: e.tensor_tensor(qc, qc, qb, ALU.subtract))
        dv = self.cnt["dve"]
        self.wait("scalar", "dve", dv)
        self.op("scalar", "act", lambda e: e.activation(
            qb, qc, AF.Exp, bias=cp[:, 6:7], scale=cp[:, 4:5]))
        self.op("scalar", "act", lambda e: e.activation(
            qd, qc, AF.Exp, bias=cp[:, 7:8], scale=cp[:, 5:6]))
        av = self.cnt["act"]
        self.wait("vector", "act", av)
        self.vop(lambda e: e.tensor_tensor(qa, v_ap, qb, ALU.mult))
        self.vop(lambda e: e.tensor_scalar(qa, qa, 16.5, None, ALU.add))
        self.vop(lambda e: e.tensor_copy(qc, qa))
        self.ladder(qa, qb, 6)
        self.vop(lambda e: e.tensor_tensor(qc, qc, qa, ALU.subtract))
        self.vop(lambda e: e.scalar_tensor_tensor(
            out_ap, qc, -16.0, qd, ALU.add, ALU.mult))
        return self.cnt["dve"]

    def qh_tile(self, psum_ap, b1col, out_ap, W):
        t = self.t
        qa = t["qa"][:, 0:W]
        qb = t["qb"][:, 0:W]
        qc = t["qc"][:, 0:W]
        qd = t["qd"][:, 0:W]
        qe = t["qe"][:, 0:W]
        cp = t["cpool"]
        self.vop(lambda e: e.tensor_scalar(
            qa, psum_ap, b1col, 1e-30, ALU.add, ALU.max))
        self.evac_mark = self.cnt["dve"]
        dv = self.cnt["dve"]
        self.wait("scalar", "dve", dv)
        self.op("scalar", "act", lambda e: e.activation(
            qb, qa, AF.Ln, bias=0.0, scale=1.0))
        self.wait("scalar", "act", self.cnt["act"])
        av = self.cnt["act"]
        self.wait("vector", "act", av)
        self.vop(lambda e: e.tensor_scalar(qb, qb, 1.0 / LN2, 6.0, ALU.mult, ALU.add))
        self.vop(lambda e: e.tensor_scalar(qb, qb, 0.0, 12.0 - 1e-3, ALU.max, ALU.min))
        self.vop(lambda e: e.tensor_copy(qc, qb))
        self.ladder(qb, qd, 4)
        self.vop(lambda e: e.tensor_tensor(qc, qc, qb, ALU.subtract))
        dv = self.cnt["dve"]
        self.wait("scalar", "dve", dv)
        self.op("scalar", "act", lambda e: e.activation(
            qb, qc, AF.Exp, bias=cp[:, 6:7], scale=cp[:, 4:5]))
        self.op("scalar", "act", lambda e: e.activation(
            qd, qc, AF.Exp, bias=cp[:, 7:8], scale=cp[:, 5:6]))
        av = self.cnt["act"]
        self.wait("vector", "act", av)
        self.vop(lambda e: e.tensor_tensor(qa, qa, qb, ALU.mult))
        self.vop(lambda e: e.tensor_copy(qc, qa))
        self.ladder(qa, qe, 4)
        self.vop(lambda e: e.tensor_tensor(qc, qc, qa, ALU.subtract))
        self.vop(lambda e: e.tensor_scalar(qa, qa, -0.5, None, ALU.add))
        self.vop(lambda e: e.scalar_tensor_tensor(
            qb, qa, 0.0, qc, ALU.is_ge, ALU.add))
        self.vop(lambda e: e.tensor_tensor(qc, qc, qe, ALU.add))
        self.vop(lambda e: e.tensor_tensor(qa, qa, qa, ALU.mult))
        self.vop(lambda e: e.tensor_scalar(
            qa, qa, float(2.0 ** -16 * 1.002), None, ALU.is_le))
        self.vop(lambda e: e.tensor_tensor(qc, qc, qb, ALU.subtract))
        self.vop(lambda e: e.tensor_tensor(qc, qa, qc, ALU.mult))
        self.vop(lambda e: e.tensor_tensor(qb, qb, qc, ALU.add))
        self.vop(lambda e: e.tensor_tensor(out_ap, qb, qd, ALU.mult))
        return self.cnt["dve"]

    def qy_tile(self, psum_ap, b2col, out_ap, W):
        t = self.t
        qa = t["qa"][:, 0:W]
        qb = t["qb"][:, 0:W]
        qc = t["qc"][:, 0:W]
        qd = t["qd"][:, 0:W]
        qi = t["qi"][:, 0:W]
        cp = t["cpool"]
        self.vop(lambda e: e.tensor_scalar(qa, psum_ap, b2col, None, ALU.add))
        self.evac_mark = self.cnt["dve"]
        self.wait("scalar", "dve", self.cnt["dve"])
        self.op("scalar", "act", lambda e: e.activation(
            qb, qa, AF.Square, bias=0.0, scale=1.0))
        self.wait("scalar", "act", self.cnt["act"])
        self.op("scalar", "act", lambda e: e.activation(
            qb, qb, AF.Ln, bias=cp[:, 0:1], scale=1.0))
        self.wait("scalar", "act", self.cnt["act"])
        av = self.cnt["act"]
        self.wait("vector", "act", av)
        self.vop(lambda e: e.tensor_scalar(qb, qb, 0.5 / LN2, 14.0, ALU.mult, ALU.add))
        self.vop(lambda e: e.tensor_scalar(qb, qb, 0.0, 20.0 - 1e-3, ALU.max, ALU.min))
        self.vop(lambda e: e.tensor_copy(qc, qb))
        self.ladder(qb, qd, 5)
        self.vop(lambda e: e.tensor_tensor(qc, qc, qb, ALU.subtract))
        dv = self.cnt["dve"]
        self.wait("scalar", "dve", dv)
        self.op("scalar", "act", lambda e: e.activation(
            qb, qc, AF.Exp, bias=cp[:, 8:9], scale=cp[:, 4:5]))
        self.op("scalar", "act", lambda e: e.activation(
            qd, qc, AF.Exp, bias=cp[:, 9:10], scale=cp[:, 5:6]))
        av = self.cnt["act"]
        self.wait("vector", "act", av)
        self.vop(lambda e: e.tensor_tensor(qa, qa, qb, ALU.mult))
        self.vop(lambda e: e.tensor_copy(qi, qa))
        self.vop(lambda e: e.tensor_copy(qa, qi))
        self.vop(lambda e: e.tensor_tensor(out_ap, qa, qd, ALU.mult))
        return self.cnt["dve"]

    def emit(self):
        cfg = self.cfg
        t = self.t
        cp = t["cpool"]
        SW = cfg.SW

        id_ld = self.dma("sync", t["ident"], t["identin"], sem="l_id")
        for j, val in enumerate(CPOOL_VALS):
            self.op("gpsimd", "gp", lambda e, j=j, val=val: e.memset(
                cp[:, j:j + 1], val))
        for par in range(4):
            self.op("gpsimd", "gp", lambda e, par=par: e.memset(
                t["vsl"][par][:, 64:65], 1.0))
            self.op("gpsimd", "gp", lambda e, par=par: e.memset(
                t["vsl"][par][:, 129:130], 1.0))
        self.op("gpsimd", "gp", lambda e: e.memset(t["onesbf"], 1.0))
        self.op("gpsimd", "gp", lambda e: e.memset(t["onesf32"], 1.0))
        self.op("gpsimd", "gp", lambda e: e.memset(t["onesf32c"], 1.0))
        gp_init = self.cnt["gp"]
        for en in ("vector", "scalar", "tensor"):
            self.wait(en, "gp", gp_init)
        self.wait("tensor", "l_id", id_ld)

        for tt in range(cfg.NTT):
            par = tt % 2
            xbuf = t["xbuf"][par]
            self.wait("sync", "dve", self.xb_done[par])
            ldv = self.dma("sync", xbuf, t["x_tok"][tt * 128:(tt + 1) * 128, :],
                           sem=f"l_xb{par}")
            self.wait("vector", f"l_xb{par}", ldv)
            self.wait("scalar", f"l_xb{par}", ldv)
            self.wait("scalar", "dve", self.cnt["dve"])
            self.op("scalar", "act", lambda e: e.activation(
                t["sq1"], xbuf, AF.Square, bias=0.0, scale=1.0))
            sq_act = self.cnt["act"]
            self.vop(lambda e: e.reduce_sum(t["s1"], xbuf,
                                            axis=mybir.AxisListType.X))
            self.wait("vector", "act", sq_act)
            self.vop(lambda e: e.reduce_sum(t["s2"], t["sq1"],
                                            axis=mybir.AxisListType.X))
            self.vop(lambda e: e.tensor_scalar(
                t["s1"], t["s1"], 1.0 / 1024.0, None, ALU.mult))
            self.vop(lambda e: e.tensor_tensor(t["s3"], t["s1"], t["s1"], ALU.mult))
            self.vop(lambda e: e.scalar_tensor_tensor(
                t["s2"], t["s2"], 1.0 / 1024.0, t["s3"], ALU.mult, ALU.subtract))
            dvv = self.cnt["dve"]
            self.wait("scalar", "dve", dvv)
            self.op("scalar", "act", lambda e: e.activation(
                t["s2"], t["s2"], AF.Ln, bias=cp[:, 1:2], scale=1.0))
            self.wait("scalar", "act", self.cnt["act"])
            self.op("scalar", "act", lambda e: e.activation(
                t["s2"], t["s2"], AF.Exp, bias=0.0, scale=cp[:, 2:3]))
            self.wait("scalar", "act", self.cnt["act"])
            avv = self.cnt["act"]
            self.wait("vector", "act", avv)
            self.wait("vector", f"s_vt{par}", self.vt_store[par])
            self.vop(lambda e, par=par: e.tensor_scalar(
                t["xnc"][par], xbuf, t["s1"][:, 0:1], t["s2"][:, 0:1],
                ALU.subtract, ALU.mult))
            self.xb_done[par] = self.cnt["dve"]
            xnc_dve = self.cnt["dve"]
            self.wait("gpsimd", "dve", xnc_dve)
            self.vt_store[par] = self.dma(
                "gpsimd", t["Xn"][tt * 128:(tt + 1) * 128, :],
                t["xnc"][par], sem=f"s_vt{par}")
        p0_vt = [self.cnt["s_vt0"], self.cnt["s_vt1"]]
        for ch in range(cfg.NCH):
            par = ch % 2
            self.wait("sync", "s_vt0", p0_vt[0])
            self.wait("sync", "s_vt1", p0_vt[1])
            self.wait("sync", f"s_kc{par}", self.kb_done[par])
            if self.on("sync"):
                self.eng.dma_start_transpose(
                    t["kch"][par], t["Xn"][:, ch * 128:(ch + 1) * 128]
                ).then_inc(self.sems[f"l_kc{par}"], 16)
            ldv = self.bump(f"l_kc{par}", 16)
            self.wait("vector", f"l_kc{par}", ldv)
            self.wait("vector", "act", self.s1_act)
            self.vop(lambda e, par=par: e.reduce_sum(
                t["s1"], t["kch"][par], axis=mybir.AxisListType.X))
            self.vop(lambda e: e.tensor_scalar(
                t["s1"], t["s1"], -1.0 / float(cfg.T), None, ALU.mult))
            kdv = self.cnt["dve"]
            self.wait("scalar", f"l_kc{par}", ldv)
            self.wait("scalar", "dve", kdv)
            self.op("scalar", "act", lambda e, par=par: e.activation(
                t["kch"][par], t["kch"][par], AF.Identity,
                bias=t["s1"][:, 0:1], scale=1.0))
            kca = self.cnt["act"]
            self.s1_act = kca
            self.wait("gpsimd", "act", kca)
            self.kb_done[par] = self.dma(
                "gpsimd", t["Kfm"][ch * 128:(ch + 1) * 128, :], t["kch"][par],
                sem=f"s_kc{par}")
        p1_kc = [self.cnt["s_kc0"], self.cnt["s_kc1"]]
        self.dma("sync", t["b1q"], t["b1f"], sem="l_b")
        ldb = self.dma("sync", t["b2q"], t["b2f"], sem="l_b")
        self.wait("vector", "l_b", ldb)
        self.wait("scalar", "l_b", ldb)
        self.q43_tile(t["b1q"], t["b1q"], cfg.NHJ, 3)
        self.q43_tile(t["b2q"], t["b2q"], cfg.NCH, 3)

        def w_prep(wsrc, nat, dst, nrow_tiles, ncols, stage, stage_sem):
            if cfg.NCH * SW >= 4096:
                WQ = 1024
                wbufs = (t["attnN"][:, 0:WQ], t["attnN"][:, WQ:2 * WQ],
                         t["attnN"][:, 2 * WQ:3 * WQ], t["attnN"][:, 3 * WQ:4 * WQ])
            else:
                WQ = SW
                wbufs = None
            for j in range(nrow_tiles):
                par = j % 2
                wb = t["wbuf"][par][:, 0:ncols]
                self.wait("sync", "dve", self.wbuf_done[par])
                ldv = self.dma("sync", wb, wsrc[j * 128:(j + 1) * 128, :],
                               sem=f"l_wb{par}")
                self.wait("vector", f"l_wb{par}", ldv)
                self.wait("scalar", f"l_wb{par}", ldv)
                self.wait("vector", "s_tr0", self.tr_store[0])
                for sl in range(ncols // WQ):
                    self.q43_tile(wb[:, sl * WQ:(sl + 1) * WQ],
                                  t["wq"][:, sl * WQ:(sl + 1) * WQ], WQ, 3,
                                  bufs=wbufs)
                self.wbuf_done[par] = self.cnt["dve"]
                self.wait("gpsimd", "dve", self.wbuf_done[par])
                self.wait("gpsimd", "s_tr0", self.tr_store[0])
                self.tr_store[0] = self.dma(
                    "gpsimd", nat[j * 128:(j + 1) * 128, :],
                    t["wq"][:, 0:ncols], sem="s_tr0")
            nat_done = self.cnt["s_tr0"]
            for ch in range(ncols // 128):
                self.wait("sync", "s_tr0", nat_done)
                self.wait("sync", "s_tr1", self.tr_store[1])
                if self.on("sync"):
                    self.eng.dma_start_transpose(
                        stage[:, 0:nrow_tiles * 128],
                        nat[:, ch * 128:(ch + 1) * 128]
                    ).then_inc(self.sems[stage_sem], 16)
                sld = self.bump(stage_sem, 16)
                self.wait("gpsimd", stage_sem, sld)
                self.tr_store[1] = self.dma(
                    "gpsimd", dst[ch * 128:(ch + 1) * 128, :],
                    stage[:, 0:nrow_tiles * 128], sem="s_tr1")

        w_prep(t["w1"], t["w1qn"], t["w1qT"], cfg.NHJ, cfg.D,
               t["w2t"][0], "l_w20")
        w_prep(t["w2"], t["w2qn"], t["w2qT"], cfg.NCH, cfg.HF,
               t["w1t"][0], "l_w10")
        p2_tr = [self.cnt["s_tr0"], self.cnt["s_tr1"]]
        self.wait("vector", "s_tr0", p2_tr[0])
        self.wait("vector", "s_tr1", p2_tr[1])

        def emit_ffn_unit(fs, kind, idx):
            resid = t["resid"][fs % 2]
            if kind == "hj":
                hj = idx
                mpar = hj % 2
                self.wait("sync", "pe", self.w1t_free[mpar])
                w1_ld = self.dma(
                    "sync", t["w1t3"][mpar],
                    t["w1qTr"][:, :, hj * 128:(hj + 1) * 128],
                    sem=f"l_w1{mpar}")
                self.wait("tensor", f"l_w1{mpar}", w1_ld)
                self.wait("tensor", "dve", self.xq_ready)
                self.wait("tensor", "dve", self.ffn_free[mpar])
                if mpar == 0:
                    self.wait("tensor", "dve", self.bc_free)
                else:
                    self.wait("tensor", "dve", self.bc2_free)
                for ch in range(cfg.NCH - 1):
                    self.opq("tensor", lambda e, ch=ch, mpar=mpar: e.matmul(
                        t["ffnps"][mpar], t["w1t"][mpar][:, ch * 128:(ch + 1) * 128],
                        t["xq"][:, ch * SW:(ch + 1) * SW],
                        start=(ch == 0), stop=False))
                self.op("tensor", "pe", lambda e, mpar=mpar: e.matmul(
                    t["ffnps"][mpar],
                    t["w1t"][mpar][:, (cfg.NCH - 1) * 128:cfg.NCH * 128],
                    t["xq"][:, (cfg.NCH - 1) * SW:cfg.NCH * SW],
                    start=False, stop=True))
                mm1_pe = self.cnt["pe"]
                self.w1t_free[mpar] = mm1_pe
                self.xq_free = mm1_pe
                self.wait("vector", "pe", mm1_pe)
                self.qh_tile(t["ffnps"][mpar], t["b1q"][:, hj:hj + 1],
                             t["hq"][:, hj * SW:(hj + 1) * SW], SW)
                self.ffn_free[mpar] = self.evac_mark
                if hj == cfg.NHJ - 1:
                    self.hq_ready = self.cnt["dve"]
            else:
                dj = idx
                mpar = dj % 2
                self.wait("sync", "pe", self.w2t_free[mpar])
                w2_ld = self.dma(
                    "sync", t["w2t3"][mpar],
                    t["w2qTr"][:, :, dj * 128:(dj + 1) * 128],
                    sem=f"l_w2{mpar}")
                self.wait("tensor", f"l_w2{mpar}", w2_ld)
                self.wait("tensor", "dve", self.hq_ready)
                self.wait("tensor", "dve", self.ffn_free[mpar])
                if mpar == 0:
                    self.wait("tensor", "dve", self.bc_free)
                else:
                    self.wait("tensor", "dve", self.bc2_free)
                for k in range(cfg.NHJ - 1):
                    self.opq("tensor", lambda e, k=k, mpar=mpar: e.matmul(
                        t["ffnps"][mpar], t["w2t"][mpar][:, k * 128:(k + 1) * 128],
                        t["hq"][:, k * SW:(k + 1) * SW],
                        start=(k == 0), stop=False))
                self.op("tensor", "pe", lambda e, mpar=mpar: e.matmul(
                    t["ffnps"][mpar],
                    t["w2t"][mpar][:, (cfg.NHJ - 1) * 128:cfg.NHJ * 128],
                    t["hq"][:, (cfg.NHJ - 1) * SW:cfg.NHJ * SW],
                    start=False, stop=True))
                mm2f_pe = self.cnt["pe"]
                self.w2t_free[mpar] = mm2f_pe
                self.hq_free = mm2f_pe
                self.wait("vector", "pe", mm2f_pe)
                self.wait("vector", "s_out", self.ych_store)
                self.qy_tile(t["ffnps"][mpar], t["b2q"][:, dj:dj + 1], t["ych"], SW)
                self.ffn_free[mpar] = self.evac_mark
                self.wait("sync", "dve", self.mkch_done)
                mk_ld = self.dma("sync", t["mkch"],
                                 t["mask_own"][dj * 128:(dj + 1) * 128,
                                               fs * SW:(fs + 1) * SW], sem="l_mk")
                self.wait("vector", "l_mk", mk_ld)
                self.vop(lambda e: e.tensor_tensor(
                    t["ych"], t["ych"], t["mkch"], ALU.mult))
                self.mkch_done = self.cnt["dve"]
                self.vop(lambda e, dj=dj, resid=resid: e.tensor_tensor(
                    t["ych"], t["ych"], resid[:, dj * SW:(dj + 1) * SW],
                    ALU.add))
                ych_dve = self.cnt["dve"]
                self.wait("gpsimd", "dve", ych_dve)
                self.ych_store = self.dma(
                    "gpsimd", t["out_fm"][dj * 128:(dj + 1) * 128,
                                          fs * SW:(fs + 1) * SW],
                    t["ych"], sem="s_out")

        units = [("hj", j) for j in range(cfg.NHJ)] + \
                [("dj", j) for j in range(cfg.NCH)]
        per_slot = (len(units) + cfg.H // 2 - 1) // (cfg.H // 2)

        for s in range(cfg.NSW + 1):
            if s < cfg.NSW:
                resid = t["resid"][s % 2]
                self.wait("sync", "s_tr0", p2_tr[0])
                self.wait("sync", "s_tr1", p2_tr[1])
                self.wait("sync", "s_kc0", p1_kc[0])
                self.wait("sync", "s_kc1", p1_kc[1])
                self.wait("sync", "s_vt0", p0_vt[0])
                self.wait("sync", "s_vt1", p0_vt[1])
                self.wait("sync", "pec", self.qf_free)
                for ch in range(cfg.NCH):
                    self.dma("sync", t["qf"][:, ch * SW:(ch + 1) * SW],
                             t["Kfm"][ch * 128:(ch + 1) * 128,
                                      s * SW:(s + 1) * SW], sem="l_qf")
                qf_ld = self.cnt["l_qf"]

                for p in range(cfg.H // 2):
                    par = p % 2
                    self.wait("sync", "pec", self.kch_free[par])
                    ldk = self.dma("sync", t["kch"][par],
                                   t["Kfm"][p * 128:(p + 1) * 128, :],
                                   sem=f"l_kc{par}")
                    prev = None
                    for kt in range(cfg.NKT + 1):
                        if kt < cfg.NKT:
                            kpar = kt % 2
                            vpar = kt % 4
                            self.wait("sync", "pe", self.vsl_free[vpar])
                            ldvs = self.dma(
                                "sync", t["vsl3"][vpar][:, :, 0:64],
                                t["XnG"][kt * 128:(kt + 1) * 128,
                                         2 * p:2 * p + 2, :],
                                sem=f"l_vs{vpar}")
                            if kt == 0:
                                self.wait("tensor", f"l_kc{par}", ldk)
                                self.wait("tensor", "l_qf", qf_ld)
                            self.wait("tensor", "act", self.mega_freeA[kpar])
                            self.opq("tensor", lambda e, p=p, kt=kt, kpar=kpar, par=par:
                                     e.matmul(t["megaAB"][kpar][:, 0:SW],
                                              t["kch"][par][0:64, kt * 128:(kt + 1) * 128],
                                              t["qf"][0:64, p * SW:(p + 1) * SW],
                                              start=True, stop=True, tile_position=(0, 0)))
                            self.op("tensor", "pec", lambda e, p=p, kt=kt, kpar=kpar, par=par:
                                    e.matmul(t["megaAB"][kpar][:, SW:2 * SW],
                                             t["kch"][par][64:128, kt * 128:(kt + 1) * 128],
                                             t["qf"][64:128, p * SW:(p + 1) * SW],
                                             start=True, stop=True, tile_position=(64, 0)))
                            corr_pe = self.cnt["pec"]
                            self.wait("scalar", "pec", corr_pe)
                            self.wait("scalar", "pe", self.et_free[kpar])
                            self.op("scalar", "act", lambda e, kpar=kpar: e.activation(
                                t["etAB"][kpar], t["megaAB"][kpar], AF.Exp,
                                bias=cp[:, 3:4], scale=1.0))
                            exp_act = self.cnt["act"]
                            self.mega_freeA[kpar] = exp_act
                            cur = (kt, kpar, vpar, exp_act, ldvs)
                        else:
                            cur = None
                        if prev is not None:
                            pkt, pkpar, pvpar, pexp, pldvs = prev
                            self.wait("tensor", "act", pexp)
                            self.wait("tensor", f"l_vs{pvpar}", pldvs)
                            if pkt == 0:
                                self.wait("tensor", "act", self.nd_free_act)
                                self.wait("tensor", "dve", self.nd_free_dve)
                            self.opq("tensor", lambda e, pkt=pkt, pkpar=pkpar,
                                     pvpar=pvpar: e.matmul(
                                t["ndA"], t["vsl"][pvpar][:, 0:65],
                                t["etAB"][pkpar][:, 0:SW],
                                start=(pkt == 0), stop=(pkt == cfg.NKT - 1)))
                            self.op("tensor", "pe", lambda e, pkt=pkt, pkpar=pkpar,
                                    pvpar=pvpar: e.matmul(
                                t["ndB"], t["vsl"][pvpar][:, 65:130],
                                t["etAB"][pkpar][:, SW:2 * SW],
                                start=(pkt == 0), stop=(pkt == cfg.NKT - 1)))
                            mm2_pe = self.cnt["pe"]
                            self.et_free[pkpar] = mm2_pe
                            self.vsl_free[pvpar] = mm2_pe
                        prev = cur
                    self.kch_free[par] = self.cnt["pec"]
                    nd_pe = self.cnt["pe"]
                    h0, h1 = 2 * p, 2 * p + 1
                    self.wait("scalar", "pe", nd_pe)
                    self.wait("scalar", "dve", self.attnN_free)
                    self.wait("scalar", "dve", self.dall_free)
                    self.op("scalar", "act", lambda e, p=p: e.copy(
                        t["attnN"][0:64, p * SW:(p + 1) * SW], t["ndA"][0:64, :]))
                    self.op("scalar", "act", lambda e, p=p: e.copy(
                        t["attnN"][64:128, p * SW:(p + 1) * SW], t["ndB"][0:64, :]))
                    self.op("scalar", "act", lambda e, h0=h0: e.copy(
                        t["Dall"][h0], t["ndA"][64:65, :]))
                    self.op("scalar", "act", lambda e, h1=h1: e.copy(
                        t["Dall"][h1], t["ndB"][64:65, :]))
                    self.nd_free_act = self.cnt["act"]
                    self.wait("vector", "act", self.nd_free_act)
                    self.wait("vector", "pe", self.drec_free)
                    self.vop(lambda e, h0=h0: e.reciprocal(t["Drec"][h0], t["Dall"][h0]))
                    self.vop(lambda e, h1=h1: e.reciprocal(t["Drec"][h1], t["Dall"][h1]))
                    self.dall_free = self.cnt["dve"]
                    self.wait("tensor", "dve", self.dall_free)
                    self.wait("tensor", "dve", self.ffn_free[0])
                    self.wait("tensor", "dve", self.bc_free)
                    self.opq("tensor", lambda e, h0=h0: e.matmul(
                        t["bcps"][0:64, :], t["onesf32c"][0:1, 0:64], t["Drec"][h0],
                        start=True, stop=True, tile_position=(0, 0)))
                    self.op("tensor", "pe", lambda e, h1=h1: e.matmul(
                        t["bcps"][64:128, :], t["onesf32c"][0:1, 0:64], t["Drec"][h1],
                        start=True, stop=True, tile_position=(0, 64)))
                    bc_pe = self.cnt["pe"]
                    self.drec_free = bc_pe
                    self.wait("sync", "dve", self.xoch_done)
                    xo_ld = self.dma("sync", t["xoch"],
                                     t["x_own"][p * 128:(p + 1) * 128,
                                                s * SW:(s + 1) * SW], sem="l_xo")
                    self.wait("vector", "pe", bc_pe)
                    self.wait("vector", "l_xo", xo_ld)
                    self.vop(lambda e, p=p, resid=resid: e.tensor_tensor(
                        resid[:, p * SW:(p + 1) * SW],
                        t["attnN"][:, p * SW:(p + 1) * SW], t["bcps"], ALU.mult))
                    self.bc_free = self.cnt["dve"]
                    self.vop(lambda e, p=p, resid=resid: e.tensor_tensor(
                        resid[:, p * SW:(p + 1) * SW],
                        resid[:, p * SW:(p + 1) * SW], t["xoch"], ALU.add))
                    self.xoch_done = self.cnt["dve"]
                    if s > 0:
                        for u in units[p * per_slot:(p + 1) * per_slot]:
                            emit_ffn_unit(s - 1, u[0], u[1])
            else:
                for u in units:
                    emit_ffn_unit(s - 1, u[0], u[1])

            if s < cfg.NSW:
                resid = t["resid"][s % 2]
                for ch in range(cfg.NCH):
                    self.vop(lambda e, ch=ch, resid=resid: e.tensor_tensor(
                        t["attnN"][:, ch * SW:(ch + 1) * SW],
                        resid[:, ch * SW:(ch + 1) * SW],
                        resid[:, ch * SW:(ch + 1) * SW], ALU.mult))
                sq_dve = self.cnt["dve"]
                self.wait("tensor", "dve", sq_dve)
                self.wait("tensor", "act", self.nd_free_act)
                self.wait("tensor", "dve", self.nd_free_dve)
                for ch in range(cfg.NCH):
                    if ch < cfg.NCH - 1:
                        self.opq("tensor", lambda e, ch=ch, resid=resid: e.matmul(
                            t["ndA"][0:1, :], t["onesf32"][:, 0:1],
                            resid[:, ch * SW:(ch + 1) * SW],
                            start=(ch == 0), stop=False))
                    else:
                        self.op("tensor", "pe", lambda e, ch=ch, resid=resid: e.matmul(
                            t["ndA"][0:1, :], t["onesf32"][:, 0:1],
                            resid[:, ch * SW:(ch + 1) * SW],
                            start=False, stop=True))
                for ch in range(cfg.NCH):
                    if ch < cfg.NCH - 1:
                        self.opq("tensor", lambda e, ch=ch: e.matmul(
                            t["ndB"][0:1, :], t["onesf32"][:, 0:1],
                            t["attnN"][:, ch * SW:(ch + 1) * SW],
                            start=(ch == 0), stop=False))
                    else:
                        self.op("tensor", "pe", lambda e, ch=ch: e.matmul(
                            t["ndB"][0:1, :], t["onesf32"][:, 0:1],
                            t["attnN"][:, ch * SW:(ch + 1) * SW],
                            start=False, stop=True))
                sums_pe = self.cnt["pe"]
                self.wait("vector", "pe", sums_pe)
                self.wait("vector", "pe", self.stats_free)
                self.vop(lambda e: e.tensor_scalar(
                    t["mrow"], t["ndA"][0:1, :], 1.0 / 1024.0, None, ALU.mult))
                self.vop(lambda e: e.tensor_tensor(
                    t["vrow"], t["mrow"], t["mrow"], ALU.mult))
                self.vop(lambda e: e.scalar_tensor_tensor(
                    t["vrow"], t["ndB"][0:1, :], 1.0 / 1024.0, t["vrow"],
                    ALU.mult, ALU.subtract))
                self.nd_free_dve = self.cnt["dve"]
                st_dve = self.cnt["dve"]
                self.wait("scalar", "dve", st_dve)
                self.op("scalar", "act", lambda e: e.activation(
                    t["vrow"], t["vrow"], AF.Ln, bias=cp[0:1, 1:2], scale=1.0))
                self.wait("scalar", "act", self.cnt["act"])
                self.op("scalar", "act", lambda e: e.activation(
                    t["vrow"], t["vrow"], AF.Exp, bias=0.0, scale=cp[0:1, 2:3]))
                self.wait("scalar", "act", self.cnt["act"])
                stat_act = self.cnt["act"]
                self.wait("tensor", "act", stat_act)
                self.wait("tensor", "dve", self.bc_free)
                self.wait("tensor", "dve", self.ffn_free[0])
                self.op("tensor", "pe", lambda e: e.matmul(
                    t["bcps"], t["onesf32c"], t["mrow"], start=True, stop=True))
                m_pe = self.cnt["pe"]
                self.stats_free = m_pe
                self.wait("vector", "pe", m_pe)
                self.wait("vector", "pe", self.xq_free)
                for ch in range(cfg.NCH):
                    self.vop(lambda e, ch=ch, resid=resid: e.tensor_tensor(
                        t["attnN"][:, ch * SW:(ch + 1) * SW],
                        resid[:, ch * SW:(ch + 1) * SW], t["bcps"], ALU.subtract))
                self.bc_free = self.cnt["dve"]
                self.wait("tensor", "dve", self.bc2_free)
                self.wait("tensor", "dve", self.ffn_free[1])
                self.op("tensor", "pe", lambda e: e.matmul(
                    t["bcps2"], t["onesf32c"], t["vrow"], start=True, stop=True))
                bc2_pe = self.cnt["pe"]
                self.stats_free = bc2_pe
                self.wait("vector", "pe", bc2_pe)
                for ch in range(cfg.NCH):
                    self.vop(lambda e, ch=ch: e.tensor_tensor(
                        t["attnN"][:, ch * SW:(ch + 1) * SW],
                        t["attnN"][:, ch * SW:(ch + 1) * SW], t["bcps2"], ALU.mult))
                    self.bc2_free = self.cnt["dve"]
                    self.q43_tile(t["attnN"][:, ch * SW:(ch + 1) * SW],
                                  t["xq"][:, ch * SW:(ch + 1) * SW], SW, 4)
                self.xq_ready = self.cnt["dve"]
                self.attnN_free = self.cnt["dve"]
                self.qf_free = self.cnt["pec"]
        self.wait("gpsimd", "s_out", self.cnt["s_out"])


def build_program(cfg):
    nc = bass.Bass("TRN2", target_bir_lowering=False, debug=False)
    T, D, HF, SW = cfg.T, cfg.D, cfg.HF, cfg.SW
    t = {}
    t["x_tok"] = nc.dram_tensor("x_tok", [T, D], F32, kind="ExternalInput").ap()
    t["x_own"] = nc.dram_tensor("x_own", [D, cfg.OWN], F32, kind="ExternalInput").ap()
    t["mask_own"] = nc.dram_tensor("mask_own", [D, cfg.OWN], F32,
                                   kind="ExternalInput").ap()
    t["w1"] = nc.dram_tensor("w1", [HF, D], F32, kind="ExternalInput").ap()
    t["w2"] = nc.dram_tensor("w2", [D, HF], F32, kind="ExternalInput").ap()
    t["b1f"] = nc.dram_tensor("b1f", [128, HF // 128], F32, kind="ExternalInput").ap()
    t["b2f"] = nc.dram_tensor("b2f", [128, D // 128], F32, kind="ExternalInput").ap()
    t["identin"] = nc.dram_tensor("identin", [128, 128], BF16,
                                  kind="ExternalInput").ap()
    t["out_fm"] = nc.dram_tensor("out_fm", [D, cfg.OWN], F32,
                                 kind="ExternalOutput").ap()
    t["Xn"] = nc.dram_tensor("Xn", [T, D], BF16, kind="Internal").ap()
    t["XnG"] = t["Xn"].rearrange("t (g c) -> t g c", g=D // 64, c=64)
    t["Kfm"] = nc.dram_tensor("Kfm", [D, T], BF16, kind="Internal").ap()
    t["w1qn"] = nc.dram_tensor("w1qn", [HF, D], BF16, kind="Internal").ap()
    t["w2qn"] = nc.dram_tensor("w2qn", [D, HF], BF16, kind="Internal").ap()
    t["w1qT"] = nc.dram_tensor("w1qT", [D, HF], BF16, kind="Internal").ap()
    t["w2qT"] = nc.dram_tensor("w2qT", [HF, D], BF16, kind="Internal").ap()
    t["w1qTr"] = t["w1qT"].rearrange("(c r) h -> r c h", c=D // 128, r=128)
    t["w2qTr"] = t["w2qT"].rearrange("(c r) d -> r c d", c=HF // 128, r=128)

    with contextlib.ExitStack() as es:
        def sb(name, shape, dtype):
            return es.enter_context(nc.sbuf_tensor(name, shape, dtype))[:]

        t["cpool"] = sb("cpool", [128, len(CPOOL_VALS)], F32)
        t["ident"] = sb("ident", [128, 128], BF16)
        t["onesbf"] = sb("onesbf", [1, 128], BF16)
        t["onesf32"] = sb("onesf32", [128, 1], F32)
        t["onesf32c"] = sb("onesf32c", [1, 128], F32)
        t["xbuf"] = [sb(f"xbuf{i}", [128, D], F32) for i in range(2)]
        t["s1"] = sb("s1", [128, 1], F32)
        t["s2"] = sb("s2", [128, 1], F32)
        t["s3"] = sb("s3", [128, 1], F32)
        xnch = [es.enter_context(nc.sbuf_tensor(f"xnc{i}", [128, D], BF16))
                for i in range(2)]
        t["xnc"] = [h[:] for h in xnch]
        t["xnc3"] = [h.ap().rearrange("p (h c) -> p h c", h=cfg.H, c=64)
                     for h in xnch]
        t["kch"] = [sb(f"kch{i}", [128, T], BF16) for i in range(2)]
        t["b1q"] = sb("b1q", [128, HF // 128], F32)
        t["b2q"] = sb("b2q", [128, D // 128], F32)
        for nm in ("qa", "qb", "qc", "qd", "qe"):
            t[nm] = sb(nm, [128, SW], F32)
        t["qi"] = sb("qi", [128, SW], I32)
        t["qf"] = sb("qfq", [128, (D // 128) * SW], BF16)
        vslh = [es.enter_context(nc.sbuf_tensor(f"vsl{i}", [128, 130], BF16))
                for i in range(4)]
        t["vsl"] = [h[:] for h in vslh]
        t["vsl3"] = [h.ap().rearrange("p (j c) -> p j c", j=2, c=65)
                     for h in vslh]
        t["etAB"] = [sb(f"etAB{i}", [128, 2 * SW], BF16) for i in range(2)]
        t["attnN"] = sb("attnN", [128, cfg.NCH * SW], F32)
        t["attnR"] = sb("attnR", [128, cfg.NCH * SW], F32)
        t["sqch"] = sb("sqch", [128, cfg.NCH * SW], F32)
        t["xq"] = sb("xq", [128, cfg.NCH * SW], BF16)
        t["hq"] = sb("hq", [128, max(cfg.NHJ * SW, HF)], BF16)
        t["wq"] = t["hq"][:, 0:HF]
        dall = sb("Dall", [1, 2 * SW], F32)
        drec = sb("Drec", [1, 2 * SW], F32)
        t["Dall"] = [dall[:, (h % 2) * SW:(h % 2 + 1) * SW] for h in range(cfg.H)]
        t["Drec"] = [drec[:, (h % 2) * SW:(h % 2 + 1) * SW] for h in range(cfg.H)]
        t["mrow"] = sb("mrow", [1, SW], F32)
        t["vrow"] = sb("vrow", [1, SW], F32)
        t["xoch"] = sb("xoch", [128, SW], F32)
        t["mkch"] = sb("mkch", [128, SW], F32)
        t["ych"] = sb("ych", [128, SW], F32)
        w1th = [es.enter_context(nc.sbuf_tensor(f"w1t{i}", [128, D], BF16))
                for i in range(2)]
        t["w1t"] = [h[:] for h in w1th]
        t["w1t3"] = [h.ap().rearrange("p (c k) -> p c k", c=cfg.NCH, k=128)
                     for h in w1th]
        w2th = [es.enter_context(nc.sbuf_tensor(f"w2t{i}", [128, HF], BF16))
                for i in range(2)]
        t["w2t"] = [h[:] for h in w2th]
        t["w2t3"] = [h.ap().rearrange("p (c k) -> p c k", c=cfg.NHJ, k=128)
                     for h in w2th]
        t["sq1"] = t["attnN"][:, 0:D]
        t["resid"] = [t["attnR"], t["sqch"]]
        if cfg.NCH * SW >= HF:
            t["wbuf"] = [t["attnR"][:, 0:HF], t["sqch"][:, 0:HF]]
        else:
            t["wbuf"] = [sb(f"wbuf{i}", [128, HF], F32) for i in range(2)]

        t["megaAB"] = [es.enter_context(
            nc.psum_tensor(f"megaAB{i}", [128, 2 * SW], F32))[:] for i in range(2)]
        t["ndA"] = es.enter_context(nc.psum_tensor("ndA", [65, SW], F32))[:]
        t["ndB"] = es.enter_context(nc.psum_tensor("ndB", [65, SW], F32))[:]
        t["ffnps"] = [es.enter_context(nc.psum_tensor(f"ffnps{i}", [128, SW], F32))[:]
                      for i in range(2)]
        t["bcps"] = t["ffnps"][0]
        t["bcps2"] = t["ffnps"][1]

        sems = {}
        semnames = ["gp", "dve", "act", "pe", "pec", "l_id", "l_b", "l_qf", "l_xo",
                    "l_mk", "s_out"]
        for i in range(2):
            semnames += [f"l_xb{i}", f"l_kc{i}", f"l_wb{i}", f"l_vs{i}",
                         f"l_w1{i}", f"l_w2{i}", f"s_vt{i}", f"s_tr{i}",
                         f"s_kc{i}", f"l_vs{i + 2}"]
        for nm in semnames:
            sems[nm] = es.enter_context(nc.semaphore(nm))
        t["sems"] = sems

        with nc.Block() as blk:
            @blk.sync
            def _(sync):
                P(nc, cfg, t, "sync", sync).emit()

            @blk.gpsimd
            def _(g):
                P(nc, cfg, t, "gpsimd", g).emit()

            @blk.vector
            def _(v):
                P(nc, cfg, t, "vector", v).emit()

            @blk.scalar
            def _(sc):
                P(nc, cfg, t, "scalar", sc).emit()

            @blk.tensor
            def _(te):
                P(nc, cfg, t, "tensor", te).emit()
    return nc




def make_in_maps(x, w1, b1, w2, b2, cfg, n_cores=8):
    B, T, D = x.shape
    keep = dropout_keep_mask((B, T, D))
    mask = np.where(keep, np.float32(1.25), np.float32(0.0))
    ident = np.eye(128).astype(bf16)
    b1f = np.ascontiguousarray(b1.reshape(cfg.NHJ, 128).T)
    b2f = np.ascontiguousarray(b2.reshape(cfg.NCH, 128).T)
    per_batch = n_cores // B
    in_maps = []
    for c in range(n_cores):
        b = c // per_batch
        qlo = (c % per_batch) * cfg.OWN
        in_maps.append({
            "x_tok": np.ascontiguousarray(np.roll(x[b], -qlo, axis=0)),
            "x_own": np.ascontiguousarray(x[b, qlo:qlo + cfg.OWN].T),
            "mask_own": np.ascontiguousarray(mask[b, qlo:qlo + cfg.OWN].T),
            "w1": w1, "w2": w2, "b1f": b1f, "b2f": b2f,
            "identin": ident,
        })
    return in_maps


def kernel(L0_attn_x, L0_ffn_w1, L0_ffn_b1, L0_ffn_w2, L0_ffn_b2):
    x = np.ascontiguousarray(np.asarray(L0_attn_x, np.float32))
    w1 = np.ascontiguousarray(np.asarray(L0_ffn_w1, np.float32))
    b1 = np.asarray(L0_ffn_b1, np.float32)
    w2 = np.ascontiguousarray(np.asarray(L0_ffn_w2, np.float32))
    b2 = np.asarray(L0_ffn_b2, np.float32)
    B, T, D = x.shape
    cfg = Cfg(T=T, OWN=T * B // 8, SW=512)
    nc = build_program(cfg)
    in_maps = make_in_maps(x, w1, b1, w2, b2, cfg)
    res = run_bass_kernel_spmd(nc, in_maps, core_ids=list(range(8)))
    out = np.empty((B, T, D), np.float32)
    per_batch = 8 // B
    for c in range(8):
        b = c // per_batch
        qlo = (c % per_batch) * cfg.OWN
        out[b, qlo:qlo + cfg.OWN] = np.asarray(res.results[c]["out_fm"],
                                               np.float32).T
    return out


# version 27
# speedup vs baseline: 1.1949x; 1.0141x over previous
import contextlib
import numpy as np
import ml_dtypes
import concourse.bass as bass
import concourse.mybir as mybir
from concourse.bass_utils import run_bass_kernel_spmd

F32 = mybir.dt.float32
BF16 = mybir.dt.bfloat16
I32 = mybir.dt.int32
AF = mybir.ActivationFunctionType
ALU = mybir.AluOpType
LN2 = float(np.log(2.0))
bf16 = ml_dtypes.bfloat16



def _threefry2x32(k0, k1, x0, x1):
    R0 = [13, 15, 26, 6]
    R1 = [17, 29, 16, 24]
    ks0 = np.uint32(k0)
    ks1 = np.uint32(k1)
    ks2 = np.uint32(ks0 ^ ks1 ^ np.uint32(0x1BD11BDA))
    ks = [ks0, ks1, ks2]
    x0 = (x0 + ks[0]).astype(np.uint32)
    x1 = (x1 + ks[1]).astype(np.uint32)
    for i, rots in enumerate([R0, R1, R0, R1, R0]):
        for r in rots:
            x0 = (x0 + x1).astype(np.uint32)
            x1 = ((x1 << np.uint32(r)) | (x1 >> np.uint32(32 - r))).astype(np.uint32)
            x1 = (x1 ^ x0).astype(np.uint32)
        x0 = (x0 + ks[(i + 1) % 3]).astype(np.uint32)
        x1 = (x1 + ks[(i + 2) % 3] + np.uint32(i + 1)).astype(np.uint32)
    return x0, x1


def dropout_keep_mask(shape, key=42, p_keep=0.8):
    n = int(np.prod(shape))
    y0, y1 = _threefry2x32(0, key, np.zeros(n, np.uint32),
                           np.arange(n, dtype=np.uint32))
    bits = (y0 ^ y1).astype(np.uint32)
    f = ((bits >> np.uint32(9)) | np.uint32(0x3F800000)).view(np.float32) \
        - np.float32(1.0)
    return (f < np.float32(p_keep)).reshape(shape)




class Cfg:
    def __init__(self, T=8192, OWN=2048, SW=512):
        self.T = T
        self.OWN = OWN
        self.SW = SW
        self.D = 1024
        self.H = 16
        self.HD = 64
        self.HF = 4096
        self.NSW = OWN // SW
        self.NKT = T // 128
        self.NTT = T // 128
        self.NCH = self.D // 128
        self.NHJ = self.HF // 128


CPOOL_VALS = [
    1e-38,
    1e-5,
    -0.5,
    -80.0,
    -LN2,
    LN2,
    9.0 * LN2,
    -9.0 * LN2,
    24.0 * LN2,
    -24.0 * LN2,
]


class P:

    def __init__(self, nc, cfg, t, eng_name, eng):
        self.nc = nc
        self.cfg = cfg
        self.t = t
        self.eng_name = eng_name
        self.eng = eng
        self.sems = t["sems"]
        self.cnt = {k: 0 for k in self.sems}
        self.tr_evac = [0, 0]
        self.tr_store = [0, 0]
        self.vt_store = [0, 0]
        self.s1_act = 0
        self.natst = [0, 0]
        self.xb_done = [0, 0]
        self.kb_done = [0, 0]
        self.kch_free = [0, 0]
        self.vsl_free = [0, 0, 0, 0]
        self.mega_freeA = [0, 0]
        self.mega_freeB = [0, 0]
        self.et_free = [0, 0]
        self.nd_free = 0
        self.bc_free = 0
        self.bc2_free = 0
        self.mmps_free = [0, 0]
        self.w1t_free = [0, 0]
        self.w2t_free = [0, 0]
        self.qf_free = 0
        self.wbuf_done = [0, 0]
        self.ych_store = 0
        self.xoch_done = 0
        self.mkch_done = 0
        self.xnc_free = [0, 0]
        self.drec_free = 0
        self.stats_free = 0
        self.xq_free = 0
        self.hq_free = 0
        self.ffn_free = [0, 0]
        self.evac_mark = 0
        self.nd_free_act = 0
        self.nd_free_dve = 0
        self.attnN_free = 0
        self.dall_free = 0
        self.xq_ready = 0
        self.hq_ready = 0

    def on(self, name):
        return self.eng_name == name

    def bump(self, sem, n):
        self.cnt[sem] += n
        return self.cnt[sem]

    def wait(self, engname, sem, val):
        if self.on(engname) and val > 0:
            self.eng.wait_ge(self.sems[sem], val)

    def dma(self, q, out, in_, sem="ld"):
        v = self.bump(sem, 16)
        if self.on(q):
            self.eng.dma_start(out=out, in_=in_).then_inc(self.sems[sem], 16)
        return v

    def op(self, engname, sem, fn):
        v = self.bump(sem, 1)
        if self.on(engname):
            fn(self.eng).then_inc(self.sems[sem], 1)
        return v

    def opq(self, engname, fn):
        if self.on(engname):
            fn(self.eng)

    def vop(self, fn):
        self.op("vector", "dve", fn)
        self.wait("vector", "dve", self.cnt["dve"])
        return self.cnt["dve"]

    def ladder(self, z_ap, b_ap, nlev):
        for k in reversed(range(nlev)):
            c = float(2.0 ** k)
            self.vop(lambda e, c=c: e.tensor_scalar(b_ap, z_ap, c, None, ALU.is_ge))
            self.vop(lambda e, c=c: e.scalar_tensor_tensor(
                z_ap, b_ap, -c, z_ap, ALU.mult, ALU.add))

    def q43_tile(self, v_ap, out_ap, W, elev, bufs=None):
        t = self.t
        if bufs is None:
            qa = t["qa"][:, 0:W]
            qb = t["qb"][:, 0:W]
            qc = t["qc"][:, 0:W]
            qd = t["qd"][:, 0:W]
        else:
            qa, qb, qc, qd = (b[:, 0:W] for b in bufs)
        cp = t["cpool"]
        self.wait("scalar", "dve", self.cnt["dve"])
        self.op("scalar", "act", lambda e: e.activation(
            qa, v_ap, AF.Square, bias=0.0, scale=1.0))
        self.wait("scalar", "act", self.cnt["act"])
        self.op("scalar", "act", lambda e: e.activation(
            qb, qa, AF.Ln, bias=cp[:, 0:1], scale=1.0))
        self.wait("scalar", "act", self.cnt["act"])
        av = self.cnt["act"]
        self.wait("vector", "act", av)
        self.vop(lambda e: e.tensor_scalar(qb, qb, 0.5 / LN2, 6.0, ALU.mult, ALU.add))
        self.vop(lambda e: e.tensor_scalar(
            qb, qb, 0.0, float(2.0 ** elev) - 1e-3, ALU.max, ALU.min))
        self.vop(lambda e: e.tensor_copy(qc, qb))
        self.ladder(qb, qd, elev)
        self.vop(lambda e: e.tensor_tensor(qc, qc, qb, ALU.subtract))
        dv = self.cnt["dve"]
        self.wait("scalar", "dve", dv)
        self.op("scalar", "act", lambda e: e.activation(
            qb, qc, AF.Exp, bias=cp[:, 6:7], scale=cp[:, 4:5]))
        self.op("scalar", "act", lambda e: e.activation(
            qd, qc, AF.Exp, bias=cp[:, 7:8], scale=cp[:, 5:6]))
        av = self.cnt["act"]
        self.wait("vector", "act", av)
        self.vop(lambda e: e.tensor_tensor(qa, v_ap, qb, ALU.mult))
        self.vop(lambda e: e.tensor_scalar(qa, qa, 16.5, None, ALU.add))
        self.vop(lambda e: e.tensor_copy(qc, qa))
        self.ladder(qa, qb, 6)
        self.vop(lambda e: e.tensor_tensor(qc, qc, qa, ALU.subtract))
        self.vop(lambda e: e.scalar_tensor_tensor(
            out_ap, qc, -16.0, qd, ALU.add, ALU.mult))
        return self.cnt["dve"]

    def qh_tile(self, psum_ap, b1col, out_ap, W):
        t = self.t
        qa = t["qa"][:, 0:W]
        qb = t["qb"][:, 0:W]
        qc = t["qc"][:, 0:W]
        qd = t["qd"][:, 0:W]
        qe = t["qe"][:, 0:W]
        cp = t["cpool"]
        self.vop(lambda e: e.tensor_scalar(
            qa, psum_ap, b1col, 1e-30, ALU.add, ALU.max))
        self.evac_mark = self.cnt["dve"]
        dv = self.cnt["dve"]
        self.wait("scalar", "dve", dv)
        self.op("scalar", "act", lambda e: e.activation(
            qb, qa, AF.Ln, bias=0.0, scale=1.0))
        self.wait("scalar", "act", self.cnt["act"])
        av = self.cnt["act"]
        self.wait("vector", "act", av)
        self.vop(lambda e: e.tensor_scalar(qb, qb, 1.0 / LN2, 6.0, ALU.mult, ALU.add))
        self.vop(lambda e: e.tensor_scalar(qb, qb, 0.0, 12.0 - 1e-3, ALU.max, ALU.min))
        self.vop(lambda e: e.tensor_copy(qc, qb))
        self.ladder(qb, qd, 4)
        self.vop(lambda e: e.tensor_tensor(qc, qc, qb, ALU.subtract))
        dv = self.cnt["dve"]
        self.wait("scalar", "dve", dv)
        self.op("scalar", "act", lambda e: e.activation(
            qb, qc, AF.Exp, bias=cp[:, 6:7], scale=cp[:, 4:5]))
        self.op("scalar", "act", lambda e: e.activation(
            qd, qc, AF.Exp, bias=cp[:, 7:8], scale=cp[:, 5:6]))
        av = self.cnt["act"]
        self.wait("vector", "act", av)
        self.vop(lambda e: e.tensor_tensor(qa, qa, qb, ALU.mult))
        self.vop(lambda e: e.tensor_copy(qc, qa))
        self.ladder(qa, qe, 4)
        self.vop(lambda e: e.tensor_tensor(qc, qc, qa, ALU.subtract))
        self.vop(lambda e: e.tensor_scalar(qa, qa, -0.5, None, ALU.add))
        self.vop(lambda e: e.scalar_tensor_tensor(
            qb, qa, 0.0, qc, ALU.is_ge, ALU.add))
        self.vop(lambda e: e.tensor_tensor(qc, qc, qe, ALU.add))
        self.vop(lambda e: e.tensor_tensor(qa, qa, qa, ALU.mult))
        self.vop(lambda e: e.tensor_scalar(
            qa, qa, float(2.0 ** -16 * 1.002), None, ALU.is_le))
        self.vop(lambda e: e.tensor_tensor(qc, qc, qb, ALU.subtract))
        self.vop(lambda e: e.tensor_tensor(qc, qa, qc, ALU.mult))
        self.vop(lambda e: e.tensor_tensor(qb, qb, qc, ALU.add))
        self.vop(lambda e: e.tensor_tensor(out_ap, qb, qd, ALU.mult))
        return self.cnt["dve"]

    def qy_tile(self, psum_ap, b2col, out_ap, W):
        t = self.t
        qa = t["qa"][:, 0:W]
        qb = t["qb"][:, 0:W]
        qc = t["qc"][:, 0:W]
        qd = t["qd"][:, 0:W]
        qi = t["qi"][:, 0:W]
        cp = t["cpool"]
        self.vop(lambda e: e.tensor_scalar(qa, psum_ap, b2col, None, ALU.add))
        self.evac_mark = self.cnt["dve"]
        self.wait("scalar", "dve", self.cnt["dve"])
        self.op("scalar", "act", lambda e: e.activation(
            qb, qa, AF.Square, bias=0.0, scale=1.0))
        self.wait("scalar", "act", self.cnt["act"])
        self.op("scalar", "act", lambda e: e.activation(
            qb, qb, AF.Ln, bias=cp[:, 0:1], scale=1.0))
        self.wait("scalar", "act", self.cnt["act"])
        av = self.cnt["act"]
        self.wait("vector", "act", av)
        self.vop(lambda e: e.tensor_scalar(qb, qb, 0.5 / LN2, 14.0, ALU.mult, ALU.add))
        self.vop(lambda e: e.tensor_scalar(qb, qb, 0.0, 20.0 - 1e-3, ALU.max, ALU.min))
        self.vop(lambda e: e.tensor_copy(qc, qb))
        self.ladder(qb, qd, 5)
        self.vop(lambda e: e.tensor_tensor(qc, qc, qb, ALU.subtract))
        dv = self.cnt["dve"]
        self.wait("scalar", "dve", dv)
        self.op("scalar", "act", lambda e: e.activation(
            qb, qc, AF.Exp, bias=cp[:, 8:9], scale=cp[:, 4:5]))
        self.op("scalar", "act", lambda e: e.activation(
            qd, qc, AF.Exp, bias=cp[:, 9:10], scale=cp[:, 5:6]))
        av = self.cnt["act"]
        self.wait("vector", "act", av)
        self.vop(lambda e: e.tensor_tensor(qa, qa, qb, ALU.mult))
        self.vop(lambda e: e.tensor_copy(qi, qa))
        self.vop(lambda e: e.tensor_copy(qa, qi))
        self.vop(lambda e: e.tensor_tensor(out_ap, qa, qd, ALU.mult))
        return self.cnt["dve"]

    def emit(self):
        cfg = self.cfg
        t = self.t
        cp = t["cpool"]
        SW = cfg.SW

        id_ld = self.dma("sync", t["ident"], t["identin"], sem="l_id")
        for j, val in enumerate(CPOOL_VALS):
            self.op("gpsimd", "gp", lambda e, j=j, val=val: e.memset(
                cp[:, j:j + 1], val))
        for par in range(4):
            self.op("gpsimd", "gp", lambda e, par=par: e.memset(
                t["vsl"][par][:, 64:65], 1.0))
            self.op("gpsimd", "gp", lambda e, par=par: e.memset(
                t["vsl"][par][:, 129:130], 1.0))
        self.op("gpsimd", "gp", lambda e: e.memset(t["onesbf"], 1.0))
        self.op("gpsimd", "gp", lambda e: e.memset(t["onesf32"], 1.0))
        self.op("gpsimd", "gp", lambda e: e.memset(t["onesf32c"], 1.0))
        gp_init = self.cnt["gp"]
        for en in ("vector", "scalar", "tensor"):
            self.wait(en, "gp", gp_init)
        self.wait("tensor", "l_id", id_ld)

        if cfg.NCH * SW >= 4096:
            WQP = 1024
            wbufs_p = (t["attnN"][:, 0:WQP], t["attnN"][:, WQP:2 * WQP],
                       t["attnN"][:, 2 * WQP:3 * WQP],
                       t["attnN"][:, 3 * WQP:4 * WQP])
        else:
            WQP = SW
            wbufs_p = None

        def wj_unit(wsrc, nat, j, ncols):
            par = j % 2
            dbl = 2 * ncols <= max(cfg.NHJ * SW, cfg.HF)
            wqs = t["hq"][:, (j % 2) * ncols:(j % 2 + 1) * ncols] if dbl                 else t["hq"][:, 0:ncols]
            wb = t["wbuf"][par][:, 0:ncols]
            self.wait("sync", "dve", self.wbuf_done[par])
            ldv = self.dma("sync", wb, wsrc[j * 128:(j + 1) * 128, :],
                           sem=f"l_wb{par}")
            self.wait("vector", f"l_wb{par}", ldv)
            self.wait("scalar", f"l_wb{par}", ldv)
            self.wait("vector", f"s_nat{par}", self.natst[par])
            if not dbl:
                self.wait("vector", f"s_nat{1 - par}", self.natst[1 - par])
            for sl in range(ncols // WQP):
                self.q43_tile(wb[:, sl * WQP:(sl + 1) * WQP],
                              wqs[:, sl * WQP:(sl + 1) * WQP], WQP, 3,
                              bufs=wbufs_p)
            self.wbuf_done[par] = self.cnt["dve"]
            self.wait("gpsimd", "dve", self.wbuf_done[par])
            self.natst[par] = self.dma(
                "gpsimd", nat[j * 128:(j + 1) * 128, :],
                wqs, sem=f"s_nat{par}")


        for tt in range(cfg.NTT):
            par = tt % 2
            xbuf = t["xbuf"][par]
            self.wait("sync", "dve", self.xb_done[par])
            ldv = self.dma("sync", xbuf, t["x_tok"][tt * 128:(tt + 1) * 128, :],
                           sem=f"l_xb{par}")
            self.wait("vector", f"l_xb{par}", ldv)
            self.wait("scalar", f"l_xb{par}", ldv)
            self.wait("scalar", "dve", self.cnt["dve"])
            self.op("scalar", "act", lambda e: e.activation(
                t["sq1"], xbuf, AF.Square, bias=0.0, scale=1.0))
            sq_act = self.cnt["act"]
            self.vop(lambda e: e.reduce_sum(t["s1"], xbuf,
                                            axis=mybir.AxisListType.X))
            self.wait("vector", "act", sq_act)
            self.vop(lambda e: e.reduce_sum(t["s2"], t["sq1"],
                                            axis=mybir.AxisListType.X))
            self.vop(lambda e: e.tensor_scalar(
                t["s1"], t["s1"], 1.0 / 1024.0, None, ALU.mult))
            self.vop(lambda e: e.tensor_tensor(t["s3"], t["s1"], t["s1"], ALU.mult))
            self.vop(lambda e: e.scalar_tensor_tensor(
                t["s2"], t["s2"], 1.0 / 1024.0, t["s3"], ALU.mult, ALU.subtract))
            dvv = self.cnt["dve"]
            self.wait("scalar", "dve", dvv)
            self.op("scalar", "act", lambda e: e.activation(
                t["s2"], t["s2"], AF.Ln, bias=cp[:, 1:2], scale=1.0))
            self.wait("scalar", "act", self.cnt["act"])
            self.op("scalar", "act", lambda e: e.activation(
                t["s2"], t["s2"], AF.Exp, bias=0.0, scale=cp[:, 2:3]))
            self.wait("scalar", "act", self.cnt["act"])
            avv = self.cnt["act"]
            self.wait("vector", "act", avv)
            self.wait("vector", f"s_vt{par}", self.vt_store[par])
            self.vop(lambda e, par=par: e.tensor_scalar(
                t["xnc"][par], xbuf, t["s1"][:, 0:1], t["s2"][:, 0:1],
                ALU.subtract, ALU.mult))
            self.xb_done[par] = self.cnt["dve"]
            xnc_dve = self.cnt["dve"]
            self.wait("gpsimd", "dve", xnc_dve)
            self.vt_store[par] = self.dma(
                "gpsimd", t["Xn"][tt * 128:(tt + 1) * 128, :],
                t["xnc"][par], sem=f"s_vt{par}")
        p0_vt = [self.cnt["s_vt0"], self.cnt["s_vt1"]]
        for ch in range(cfg.NCH):
            par = ch % 2
            self.wait("sync", "s_vt0", p0_vt[0])
            self.wait("sync", "s_vt1", p0_vt[1])
            self.wait("sync", f"s_kc{par}", self.kb_done[par])
            if self.on("sync"):
                self.eng.dma_start_transpose(
                    t["kch"][par], t["Xn"][:, ch * 128:(ch + 1) * 128]
                ).then_inc(self.sems[f"l_kc{par}"], 16)
            ldv = self.bump(f"l_kc{par}", 16)
            self.wait("vector", f"l_kc{par}", ldv)
            self.wait("vector", "act", self.s1_act)
            self.vop(lambda e, par=par: e.reduce_sum(
                t["s1"], t["kch"][par], axis=mybir.AxisListType.X))
            self.vop(lambda e: e.tensor_scalar(
                t["s1"], t["s1"], -1.0 / float(cfg.T), None, ALU.mult))
            kdv = self.cnt["dve"]
            self.wait("scalar", f"l_kc{par}", ldv)
            self.wait("scalar", "dve", kdv)
            self.op("scalar", "act", lambda e, par=par: e.activation(
                t["kch"][par], t["kch"][par], AF.Identity,
                bias=t["s1"][:, 0:1], scale=1.0))
            kca = self.cnt["act"]
            self.s1_act = kca
            self.wait("gpsimd", "act", kca)
            self.kb_done[par] = self.dma(
                "gpsimd", t["Kfm"][ch * 128:(ch + 1) * 128, :], t["kch"][par],
                sem=f"s_kc{par}")
            wj_unit(t["w1"], t["w1qn"], 2 * ch, cfg.D)
            wj_unit(t["w1"], t["w1qn"], 2 * ch + 1, cfg.D)
        p1_kc = [self.cnt["s_kc0"], self.cnt["s_kc1"]]
        self.dma("sync", t["b1q"], t["b1f"], sem="l_b")
        ldb = self.dma("sync", t["b2q"], t["b2f"], sem="l_b")
        self.wait("vector", "l_b", ldb)
        self.wait("scalar", "l_b", ldb)
        self.q43_tile(t["b1q"], t["b1q"], cfg.NHJ, 3)
        self.q43_tile(t["b2q"], t["b2q"], cfg.NCH, 3)

        def w_prep(wsrc, nat, dst, nrow_tiles, ncols, stage, stage_sem,
                   j0=0):
            for j in range(j0, nrow_tiles):
                wj_unit(wsrc, nat, j, ncols)
            nat_done = (self.cnt["s_nat0"], self.cnt["s_nat1"])
            for ch in range(ncols // 128):
                self.wait("sync", "s_nat0", nat_done[0])
                self.wait("sync", "s_nat1", nat_done[1])
                self.wait("sync", "s_tr1", self.tr_store[1])
                if self.on("sync"):
                    self.eng.dma_start_transpose(
                        stage[:, 0:nrow_tiles * 128],
                        nat[:, ch * 128:(ch + 1) * 128]
                    ).then_inc(self.sems[stage_sem], 16)
                sld = self.bump(stage_sem, 16)
                self.wait("gpsimd", stage_sem, sld)
                self.tr_store[1] = self.dma(
                    "gpsimd", dst[ch * 128:(ch + 1) * 128, :],
                    stage[:, 0:nrow_tiles * 128], sem="s_tr1")

        w_prep(t["w1"], t["w1qn"], t["w1qT"], cfg.NHJ, cfg.D,
               t["w2t"][0], "l_w20", j0=2 * cfg.NCH)
        w_prep(t["w2"], t["w2qn"], t["w2qT"], cfg.NCH, cfg.HF,
               t["w1t"][0], "l_w10")
        p2_nat = (self.cnt["s_nat0"], self.cnt["s_nat1"])
        p2_tr1 = self.tr_store[1]
        self.wait("vector", "s_nat0", p2_nat[0])
        self.wait("vector", "s_nat1", p2_nat[1])
        self.wait("vector", "s_tr1", p2_tr1)

        def emit_ffn_unit(fs, kind, idx):
            resid = t["resid"][fs % 2]
            if kind == "hj":
                hj = idx
                mpar = hj % 2
                self.wait("sync", "pe", self.w1t_free[mpar])
                w1_ld = self.dma(
                    "sync", t["w1t3"][mpar],
                    t["w1qTr"][:, :, hj * 128:(hj + 1) * 128],
                    sem=f"l_w1{mpar}")
                self.wait("tensor", f"l_w1{mpar}", w1_ld)
                self.wait("tensor", "dve", self.xq_ready)
                self.wait("tensor", "dve", self.ffn_free[mpar])
                if mpar == 0:
                    self.wait("tensor", "dve", self.bc_free)
                else:
                    self.wait("tensor", "dve", self.bc2_free)
                for ch in range(cfg.NCH - 1):
                    self.opq("tensor", lambda e, ch=ch, mpar=mpar: e.matmul(
                        t["ffnps"][mpar], t["w1t"][mpar][:, ch * 128:(ch + 1) * 128],
                        t["xq"][:, ch * SW:(ch + 1) * SW],
                        start=(ch == 0), stop=False))
                self.op("tensor", "pe", lambda e, mpar=mpar: e.matmul(
                    t["ffnps"][mpar],
                    t["w1t"][mpar][:, (cfg.NCH - 1) * 128:cfg.NCH * 128],
                    t["xq"][:, (cfg.NCH - 1) * SW:cfg.NCH * SW],
                    start=False, stop=True))
                mm1_pe = self.cnt["pe"]
                self.w1t_free[mpar] = mm1_pe
                self.xq_free = mm1_pe
                self.wait("vector", "pe", mm1_pe)
                self.qh_tile(t["ffnps"][mpar], t["b1q"][:, hj:hj + 1],
                             t["hq"][:, hj * SW:(hj + 1) * SW], SW)
                self.ffn_free[mpar] = self.evac_mark
                if hj == cfg.NHJ - 1:
                    self.hq_ready = self.cnt["dve"]
            else:
                dj = idx
                mpar = dj % 2
                self.wait("sync", "pe", self.w2t_free[mpar])
                w2_ld = self.dma(
                    "sync", t["w2t3"][mpar],
                    t["w2qTr"][:, :, dj * 128:(dj + 1) * 128],
                    sem=f"l_w2{mpar}")
                self.wait("tensor", f"l_w2{mpar}", w2_ld)
                self.wait("tensor", "dve", self.hq_ready)
                self.wait("tensor", "dve", self.ffn_free[mpar])
                if mpar == 0:
                    self.wait("tensor", "dve", self.bc_free)
                else:
                    self.wait("tensor", "dve", self.bc2_free)
                for k in range(cfg.NHJ - 1):
                    self.opq("tensor", lambda e, k=k, mpar=mpar: e.matmul(
                        t["ffnps"][mpar], t["w2t"][mpar][:, k * 128:(k + 1) * 128],
                        t["hq"][:, k * SW:(k + 1) * SW],
                        start=(k == 0), stop=False))
                self.op("tensor", "pe", lambda e, mpar=mpar: e.matmul(
                    t["ffnps"][mpar],
                    t["w2t"][mpar][:, (cfg.NHJ - 1) * 128:cfg.NHJ * 128],
                    t["hq"][:, (cfg.NHJ - 1) * SW:cfg.NHJ * SW],
                    start=False, stop=True))
                mm2f_pe = self.cnt["pe"]
                self.w2t_free[mpar] = mm2f_pe
                self.hq_free = mm2f_pe
                self.wait("vector", "pe", mm2f_pe)
                self.wait("vector", "s_out", self.ych_store)
                self.qy_tile(t["ffnps"][mpar], t["b2q"][:, dj:dj + 1], t["ych"], SW)
                self.ffn_free[mpar] = self.evac_mark
                self.wait("sync", "dve", self.mkch_done)
                mk_ld = self.dma("sync", t["mkch"],
                                 t["mask_own"][dj * 128:(dj + 1) * 128,
                                               fs * SW:(fs + 1) * SW], sem="l_mk")
                self.wait("vector", "l_mk", mk_ld)
                self.vop(lambda e: e.tensor_tensor(
                    t["ych"], t["ych"], t["mkch"], ALU.mult))
                self.mkch_done = self.cnt["dve"]
                self.vop(lambda e, dj=dj, resid=resid: e.tensor_tensor(
                    t["ych"], t["ych"], resid[:, dj * SW:(dj + 1) * SW],
                    ALU.add))
                ych_dve = self.cnt["dve"]
                self.wait("gpsimd", "dve", ych_dve)
                self.ych_store = self.dma(
                    "gpsimd", t["out_fm"][dj * 128:(dj + 1) * 128,
                                          fs * SW:(fs + 1) * SW],
                    t["ych"], sem="s_out")

        units = [("hj", j) for j in range(cfg.NHJ)] + \
                [("dj", j) for j in range(cfg.NCH)]
        per_slot = (len(units) + cfg.H // 2 - 1) // (cfg.H // 2)

        for s in range(cfg.NSW + 1):
            if s < cfg.NSW:
                resid = t["resid"][s % 2]
                self.wait("sync", "s_tr1", p2_tr1)
                self.wait("sync", "s_kc0", p1_kc[0])
                self.wait("sync", "s_kc1", p1_kc[1])
                self.wait("sync", "s_vt0", p0_vt[0])
                self.wait("sync", "s_vt1", p0_vt[1])
                self.wait("sync", "pec", self.qf_free)
                for ch in range(cfg.NCH):
                    self.dma("sync", t["qf"][:, ch * SW:(ch + 1) * SW],
                             t["Kfm"][ch * 128:(ch + 1) * 128,
                                      s * SW:(s + 1) * SW], sem="l_qf")
                qf_ld = self.cnt["l_qf"]

                for p in range(cfg.H // 2):
                    par = p % 2
                    self.wait("sync", "pec", self.kch_free[par])
                    ldk = self.dma("sync", t["kch"][par],
                                   t["Kfm"][p * 128:(p + 1) * 128, :],
                                   sem=f"l_kc{par}")
                    prev = None
                    for kt in range(cfg.NKT + 1):
                        if kt < cfg.NKT:
                            kpar = kt % 2
                            vpar = kt % 4
                            self.wait("sync", "pe", self.vsl_free[vpar])
                            ldvs = self.dma(
                                "sync", t["vsl3"][vpar][:, :, 0:64],
                                t["XnG"][kt * 128:(kt + 1) * 128,
                                         2 * p:2 * p + 2, :],
                                sem=f"l_vs{vpar}")
                            if kt == 0:
                                self.wait("tensor", f"l_kc{par}", ldk)
                                self.wait("tensor", "l_qf", qf_ld)
                            self.wait("tensor", "act", self.mega_freeA[kpar])
                            self.opq("tensor", lambda e, p=p, kt=kt, kpar=kpar, par=par:
                                     e.matmul(t["megaAB"][kpar][:, 0:SW],
                                              t["kch"][par][0:64, kt * 128:(kt + 1) * 128],
                                              t["qf"][0:64, p * SW:(p + 1) * SW],
                                              start=True, stop=True, tile_position=(0, 0)))
                            self.op("tensor", "pec", lambda e, p=p, kt=kt, kpar=kpar, par=par:
                                    e.matmul(t["megaAB"][kpar][:, SW:2 * SW],
                                             t["kch"][par][64:128, kt * 128:(kt + 1) * 128],
                                             t["qf"][64:128, p * SW:(p + 1) * SW],
                                             start=True, stop=True, tile_position=(64, 0)))
                            corr_pe = self.cnt["pec"]
                            self.wait("scalar", "pec", corr_pe)
                            self.wait("scalar", "pe", self.et_free[kpar])
                            self.op("scalar", "act", lambda e, kpar=kpar: e.activation(
                                t["etAB"][kpar], t["megaAB"][kpar], AF.Exp,
                                bias=cp[:, 3:4], scale=1.0))
                            exp_act = self.cnt["act"]
                            self.mega_freeA[kpar] = exp_act
                            cur = (kt, kpar, vpar, exp_act, ldvs)
                        else:
                            cur = None
                        if prev is not None:
                            pkt, pkpar, pvpar, pexp, pldvs = prev
                            self.wait("tensor", "act", pexp)
                            self.wait("tensor", f"l_vs{pvpar}", pldvs)
                            if pkt == 0:
                                self.wait("tensor", "act", self.nd_free_act)
                                self.wait("tensor", "dve", self.nd_free_dve)
                            self.opq("tensor", lambda e, pkt=pkt, pkpar=pkpar,
                                     pvpar=pvpar: e.matmul(
                                t["ndA"], t["vsl"][pvpar][:, 0:65],
                                t["etAB"][pkpar][:, 0:SW],
                                start=(pkt == 0), stop=(pkt == cfg.NKT - 1)))
                            self.op("tensor", "pe", lambda e, pkt=pkt, pkpar=pkpar,
                                    pvpar=pvpar: e.matmul(
                                t["ndB"], t["vsl"][pvpar][:, 65:130],
                                t["etAB"][pkpar][:, SW:2 * SW],
                                start=(pkt == 0), stop=(pkt == cfg.NKT - 1)))
                            mm2_pe = self.cnt["pe"]
                            self.et_free[pkpar] = mm2_pe
                            self.vsl_free[pvpar] = mm2_pe
                        prev = cur
                    self.kch_free[par] = self.cnt["pec"]
                    nd_pe = self.cnt["pe"]
                    h0, h1 = 2 * p, 2 * p + 1
                    self.wait("scalar", "pe", nd_pe)
                    self.wait("scalar", "dve", self.attnN_free)
                    self.wait("scalar", "dve", self.dall_free)
                    self.op("scalar", "act", lambda e, p=p: e.copy(
                        t["attnN"][0:64, p * SW:(p + 1) * SW], t["ndA"][0:64, :]))
                    self.op("scalar", "act", lambda e, p=p: e.copy(
                        t["attnN"][64:128, p * SW:(p + 1) * SW], t["ndB"][0:64, :]))
                    self.op("scalar", "act", lambda e, h0=h0: e.copy(
                        t["Dall"][h0], t["ndA"][64:65, :]))
                    self.op("scalar", "act", lambda e, h1=h1: e.copy(
                        t["Dall"][h1], t["ndB"][64:65, :]))
                    self.nd_free_act = self.cnt["act"]
                    self.wait("vector", "act", self.nd_free_act)
                    self.wait("vector", "pe", self.drec_free)
                    self.vop(lambda e, h0=h0: e.reciprocal(t["Drec"][h0], t["Dall"][h0]))
                    self.vop(lambda e, h1=h1: e.reciprocal(t["Drec"][h1], t["Dall"][h1]))
                    self.dall_free = self.cnt["dve"]
                    self.wait("tensor", "dve", self.dall_free)
                    self.wait("tensor", "dve", self.ffn_free[0])
                    self.wait("tensor", "dve", self.bc_free)
                    self.opq("tensor", lambda e, h0=h0: e.matmul(
                        t["bcps"][0:64, :], t["onesf32c"][0:1, 0:64], t["Drec"][h0],
                        start=True, stop=True, tile_position=(0, 0)))
                    self.op("tensor", "pe", lambda e, h1=h1: e.matmul(
                        t["bcps"][64:128, :], t["onesf32c"][0:1, 0:64], t["Drec"][h1],
                        start=True, stop=True, tile_position=(0, 64)))
                    bc_pe = self.cnt["pe"]
                    self.drec_free = bc_pe
                    self.wait("sync", "dve", self.xoch_done)
                    xo_ld = self.dma("sync", t["xoch"],
                                     t["x_own"][p * 128:(p + 1) * 128,
                                                s * SW:(s + 1) * SW], sem="l_xo")
                    self.wait("vector", "pe", bc_pe)
                    self.wait("vector", "l_xo", xo_ld)
                    self.vop(lambda e, p=p, resid=resid: e.tensor_tensor(
                        resid[:, p * SW:(p + 1) * SW],
                        t["attnN"][:, p * SW:(p + 1) * SW], t["bcps"], ALU.mult))
                    self.bc_free = self.cnt["dve"]
                    self.vop(lambda e, p=p, resid=resid: e.tensor_tensor(
                        resid[:, p * SW:(p + 1) * SW],
                        resid[:, p * SW:(p + 1) * SW], t["xoch"], ALU.add))
                    self.xoch_done = self.cnt["dve"]
                    if s > 0:
                        for u in units[p * per_slot:(p + 1) * per_slot]:
                            emit_ffn_unit(s - 1, u[0], u[1])
            else:
                for u in units:
                    emit_ffn_unit(s - 1, u[0], u[1])

            if s < cfg.NSW:
                resid = t["resid"][s % 2]
                for ch in range(cfg.NCH):
                    self.vop(lambda e, ch=ch, resid=resid: e.tensor_tensor(
                        t["attnN"][:, ch * SW:(ch + 1) * SW],
                        resid[:, ch * SW:(ch + 1) * SW],
                        resid[:, ch * SW:(ch + 1) * SW], ALU.mult))
                sq_dve = self.cnt["dve"]
                self.wait("tensor", "dve", sq_dve)
                self.wait("tensor", "act", self.nd_free_act)
                self.wait("tensor", "dve", self.nd_free_dve)
                for ch in range(cfg.NCH):
                    if ch < cfg.NCH - 1:
                        self.opq("tensor", lambda e, ch=ch, resid=resid: e.matmul(
                            t["ndA"][0:1, :], t["onesf32"][:, 0:1],
                            resid[:, ch * SW:(ch + 1) * SW],
                            start=(ch == 0), stop=False))
                    else:
                        self.op("tensor", "pe", lambda e, ch=ch, resid=resid: e.matmul(
                            t["ndA"][0:1, :], t["onesf32"][:, 0:1],
                            resid[:, ch * SW:(ch + 1) * SW],
                            start=False, stop=True))
                for ch in range(cfg.NCH):
                    if ch < cfg.NCH - 1:
                        self.opq("tensor", lambda e, ch=ch: e.matmul(
                            t["ndB"][0:1, :], t["onesf32"][:, 0:1],
                            t["attnN"][:, ch * SW:(ch + 1) * SW],
                            start=(ch == 0), stop=False))
                    else:
                        self.op("tensor", "pe", lambda e, ch=ch: e.matmul(
                            t["ndB"][0:1, :], t["onesf32"][:, 0:1],
                            t["attnN"][:, ch * SW:(ch + 1) * SW],
                            start=False, stop=True))
                sums_pe = self.cnt["pe"]
                self.wait("vector", "pe", sums_pe)
                self.wait("vector", "pe", self.stats_free)
                self.vop(lambda e: e.tensor_scalar(
                    t["mrow"], t["ndA"][0:1, :], 1.0 / 1024.0, None, ALU.mult))
                self.vop(lambda e: e.tensor_tensor(
                    t["vrow"], t["mrow"], t["mrow"], ALU.mult))
                self.vop(lambda e: e.scalar_tensor_tensor(
                    t["vrow"], t["ndB"][0:1, :], 1.0 / 1024.0, t["vrow"],
                    ALU.mult, ALU.subtract))
                self.nd_free_dve = self.cnt["dve"]
                st_dve = self.cnt["dve"]
                self.wait("scalar", "dve", st_dve)
                self.op("scalar", "act", lambda e: e.activation(
                    t["vrow"], t["vrow"], AF.Ln, bias=cp[0:1, 1:2], scale=1.0))
                self.wait("scalar", "act", self.cnt["act"])
                self.op("scalar", "act", lambda e: e.activation(
                    t["vrow"], t["vrow"], AF.Exp, bias=0.0, scale=cp[0:1, 2:3]))
                self.wait("scalar", "act", self.cnt["act"])
                stat_act = self.cnt["act"]
                self.wait("tensor", "act", stat_act)
                self.wait("tensor", "dve", self.bc_free)
                self.wait("tensor", "dve", self.ffn_free[0])
                self.op("tensor", "pe", lambda e: e.matmul(
                    t["bcps"], t["onesf32c"], t["mrow"], start=True, stop=True))
                m_pe = self.cnt["pe"]
                self.stats_free = m_pe
                self.wait("vector", "pe", m_pe)
                self.wait("vector", "pe", self.xq_free)
                for ch in range(cfg.NCH):
                    self.vop(lambda e, ch=ch, resid=resid: e.tensor_tensor(
                        t["attnN"][:, ch * SW:(ch + 1) * SW],
                        resid[:, ch * SW:(ch + 1) * SW], t["bcps"], ALU.subtract))
                self.bc_free = self.cnt["dve"]
                self.wait("tensor", "dve", self.bc2_free)
                self.wait("tensor", "dve", self.ffn_free[1])
                self.op("tensor", "pe", lambda e: e.matmul(
                    t["bcps2"], t["onesf32c"], t["vrow"], start=True, stop=True))
                bc2_pe = self.cnt["pe"]
                self.stats_free = bc2_pe
                self.wait("vector", "pe", bc2_pe)
                for ch in range(cfg.NCH):
                    self.vop(lambda e, ch=ch: e.tensor_tensor(
                        t["attnN"][:, ch * SW:(ch + 1) * SW],
                        t["attnN"][:, ch * SW:(ch + 1) * SW], t["bcps2"], ALU.mult))
                    self.bc2_free = self.cnt["dve"]
                    self.q43_tile(t["attnN"][:, ch * SW:(ch + 1) * SW],
                                  t["xq"][:, ch * SW:(ch + 1) * SW], SW, 4)
                self.xq_ready = self.cnt["dve"]
                self.attnN_free = self.cnt["dve"]
                self.qf_free = self.cnt["pec"]
        self.wait("gpsimd", "s_out", self.cnt["s_out"])


def build_program(cfg):
    nc = bass.Bass("TRN2", target_bir_lowering=False, debug=False)
    T, D, HF, SW = cfg.T, cfg.D, cfg.HF, cfg.SW
    t = {}
    t["x_tok"] = nc.dram_tensor("x_tok", [T, D], F32, kind="ExternalInput").ap()
    t["x_own"] = nc.dram_tensor("x_own", [D, cfg.OWN], F32, kind="ExternalInput").ap()
    t["mask_own"] = nc.dram_tensor("mask_own", [D, cfg.OWN], F32,
                                   kind="ExternalInput").ap()
    t["w1"] = nc.dram_tensor("w1", [HF, D], F32, kind="ExternalInput").ap()
    t["w2"] = nc.dram_tensor("w2", [D, HF], F32, kind="ExternalInput").ap()
    t["b1f"] = nc.dram_tensor("b1f", [128, HF // 128], F32, kind="ExternalInput").ap()
    t["b2f"] = nc.dram_tensor("b2f", [128, D // 128], F32, kind="ExternalInput").ap()
    t["identin"] = nc.dram_tensor("identin", [128, 128], BF16,
                                  kind="ExternalInput").ap()
    t["out_fm"] = nc.dram_tensor("out_fm", [D, cfg.OWN], F32,
                                 kind="ExternalOutput").ap()
    t["Xn"] = nc.dram_tensor("Xn", [T, D], BF16, kind="Internal").ap()
    t["XnG"] = t["Xn"].rearrange("t (g c) -> t g c", g=D // 64, c=64)
    t["Kfm"] = nc.dram_tensor("Kfm", [D, T], BF16, kind="Internal").ap()
    t["w1qn"] = nc.dram_tensor("w1qn", [HF, D], BF16, kind="Internal").ap()
    t["w2qn"] = nc.dram_tensor("w2qn", [D, HF], BF16, kind="Internal").ap()
    t["w1qT"] = nc.dram_tensor("w1qT", [D, HF], BF16, kind="Internal").ap()
    t["w2qT"] = nc.dram_tensor("w2qT", [HF, D], BF16, kind="Internal").ap()
    t["w1qTr"] = t["w1qT"].rearrange("(c r) h -> r c h", c=D // 128, r=128)
    t["w2qTr"] = t["w2qT"].rearrange("(c r) d -> r c d", c=HF // 128, r=128)

    with contextlib.ExitStack() as es:
        def sb(name, shape, dtype):
            return es.enter_context(nc.sbuf_tensor(name, shape, dtype))[:]

        t["cpool"] = sb("cpool", [128, len(CPOOL_VALS)], F32)
        t["ident"] = sb("ident", [128, 128], BF16)
        t["onesbf"] = sb("onesbf", [1, 128], BF16)
        t["onesf32"] = sb("onesf32", [128, 1], F32)
        t["onesf32c"] = sb("onesf32c", [1, 128], F32)
        t["xbuf"] = [sb(f"xbuf{i}", [128, D], F32) for i in range(2)]
        t["s1"] = sb("s1", [128, 1], F32)
        t["s2"] = sb("s2", [128, 1], F32)
        t["s3"] = sb("s3", [128, 1], F32)
        xnch = [es.enter_context(nc.sbuf_tensor(f"xnc{i}", [128, D], BF16))
                for i in range(2)]
        t["xnc"] = [h[:] for h in xnch]
        t["xnc3"] = [h.ap().rearrange("p (h c) -> p h c", h=cfg.H, c=64)
                     for h in xnch]
        t["kch"] = [sb(f"kch{i}", [128, T], BF16) for i in range(2)]
        t["b1q"] = sb("b1q", [128, HF // 128], F32)
        t["b2q"] = sb("b2q", [128, D // 128], F32)
        for nm in ("qa", "qb", "qc", "qd", "qe"):
            t[nm] = sb(nm, [128, SW], F32)
        t["qi"] = sb("qi", [128, SW], I32)
        t["qf"] = sb("qfq", [128, (D // 128) * SW], BF16)
        vslh = [es.enter_context(nc.sbuf_tensor(f"vsl{i}", [128, 130], BF16))
                for i in range(4)]
        t["vsl"] = [h[:] for h in vslh]
        t["vsl3"] = [h.ap().rearrange("p (j c) -> p j c", j=2, c=65)
                     for h in vslh]
        t["etAB"] = [sb(f"etAB{i}", [128, 2 * SW], BF16) for i in range(2)]
        t["attnN"] = sb("attnN", [128, cfg.NCH * SW], F32)
        t["attnR"] = sb("attnR", [128, cfg.NCH * SW], F32)
        t["sqch"] = sb("sqch", [128, cfg.NCH * SW], F32)
        t["xq"] = sb("xq", [128, cfg.NCH * SW], BF16)
        t["hq"] = sb("hq", [128, max(cfg.NHJ * SW, HF)], BF16)
        t["wq"] = t["hq"][:, 0:HF]
        dall = sb("Dall", [1, 2 * SW], F32)
        drec = sb("Drec", [1, 2 * SW], F32)
        t["Dall"] = [dall[:, (h % 2) * SW:(h % 2 + 1) * SW] for h in range(cfg.H)]
        t["Drec"] = [drec[:, (h % 2) * SW:(h % 2 + 1) * SW] for h in range(cfg.H)]
        t["mrow"] = sb("mrow", [1, SW], F32)
        t["vrow"] = sb("vrow", [1, SW], F32)
        t["xoch"] = sb("xoch", [128, SW], F32)
        t["mkch"] = sb("mkch", [128, SW], F32)
        t["ych"] = sb("ych", [128, SW], F32)
        w1th = [es.enter_context(nc.sbuf_tensor(f"w1t{i}", [128, D], BF16))
                for i in range(2)]
        t["w1t"] = [h[:] for h in w1th]
        t["w1t3"] = [h.ap().rearrange("p (c k) -> p c k", c=cfg.NCH, k=128)
                     for h in w1th]
        w2th = [es.enter_context(nc.sbuf_tensor(f"w2t{i}", [128, HF], BF16))
                for i in range(2)]
        t["w2t"] = [h[:] for h in w2th]
        t["w2t3"] = [h.ap().rearrange("p (c k) -> p c k", c=cfg.NHJ, k=128)
                     for h in w2th]
        t["sq1"] = t["attnN"][:, 0:D]
        t["resid"] = [t["attnR"], t["sqch"]]
        if cfg.NCH * SW >= HF:
            t["wbuf"] = [t["attnR"][:, 0:HF], t["sqch"][:, 0:HF]]
        else:
            t["wbuf"] = [sb(f"wbuf{i}", [128, HF], F32) for i in range(2)]

        t["megaAB"] = [es.enter_context(
            nc.psum_tensor(f"megaAB{i}", [128, 2 * SW], F32))[:] for i in range(2)]
        t["ndA"] = es.enter_context(nc.psum_tensor("ndA", [65, SW], F32))[:]
        t["ndB"] = es.enter_context(nc.psum_tensor("ndB", [65, SW], F32))[:]
        t["ffnps"] = [es.enter_context(nc.psum_tensor(f"ffnps{i}", [128, SW], F32))[:]
                      for i in range(2)]
        t["bcps"] = t["ffnps"][0]
        t["bcps2"] = t["ffnps"][1]

        sems = {}
        semnames = ["gp", "dve", "act", "pe", "pec", "l_id", "l_b", "l_qf", "l_xo",
                    "l_mk", "s_out"]
        for i in range(2):
            semnames += [f"l_xb{i}", f"l_kc{i}", f"l_wb{i}", f"l_vs{i}",
                         f"l_w1{i}", f"l_w2{i}", f"s_vt{i}", f"s_tr{i}",
                         f"s_kc{i}", f"l_vs{i + 2}", f"s_nat{i}"]
        for nm in semnames:
            sems[nm] = es.enter_context(nc.semaphore(nm))
        t["sems"] = sems

        with nc.Block() as blk:
            @blk.sync
            def _(sync):
                P(nc, cfg, t, "sync", sync).emit()

            @blk.gpsimd
            def _(g):
                P(nc, cfg, t, "gpsimd", g).emit()

            @blk.vector
            def _(v):
                P(nc, cfg, t, "vector", v).emit()

            @blk.scalar
            def _(sc):
                P(nc, cfg, t, "scalar", sc).emit()

            @blk.tensor
            def _(te):
                P(nc, cfg, t, "tensor", te).emit()
    return nc




def make_in_maps(x, w1, b1, w2, b2, cfg, n_cores=8):
    B, T, D = x.shape
    keep = dropout_keep_mask((B, T, D))
    mask = np.where(keep, np.float32(1.25), np.float32(0.0))
    ident = np.eye(128).astype(bf16)
    b1f = np.ascontiguousarray(b1.reshape(cfg.NHJ, 128).T)
    b2f = np.ascontiguousarray(b2.reshape(cfg.NCH, 128).T)
    per_batch = n_cores // B
    in_maps = []
    for c in range(n_cores):
        b = c // per_batch
        qlo = (c % per_batch) * cfg.OWN
        in_maps.append({
            "x_tok": np.ascontiguousarray(np.roll(x[b], -qlo, axis=0)),
            "x_own": np.ascontiguousarray(x[b, qlo:qlo + cfg.OWN].T),
            "mask_own": np.ascontiguousarray(mask[b, qlo:qlo + cfg.OWN].T),
            "w1": w1, "w2": w2, "b1f": b1f, "b2f": b2f,
            "identin": ident,
        })
    return in_maps


def kernel(L0_attn_x, L0_ffn_w1, L0_ffn_b1, L0_ffn_w2, L0_ffn_b2):
    x = np.ascontiguousarray(np.asarray(L0_attn_x, np.float32))
    w1 = np.ascontiguousarray(np.asarray(L0_ffn_w1, np.float32))
    b1 = np.asarray(L0_ffn_b1, np.float32)
    w2 = np.ascontiguousarray(np.asarray(L0_ffn_w2, np.float32))
    b2 = np.asarray(L0_ffn_b2, np.float32)
    B, T, D = x.shape
    cfg = Cfg(T=T, OWN=T * B // 8, SW=512)
    nc = build_program(cfg)
    in_maps = make_in_maps(x, w1, b1, w2, b2, cfg)
    res = run_bass_kernel_spmd(nc, in_maps, core_ids=list(range(8)))
    out = np.empty((B, T, D), np.float32)
    per_batch = 8 // B
    for c in range(8):
        b = c // per_batch
        qlo = (c % per_batch) * cfg.OWN
        out[b, qlo:qlo + cfg.OWN] = np.asarray(res.results[c]["out_fm"],
                                               np.float32).T
    return out


# version 28 (final)
# speedup vs baseline: 1.2048x; 1.0083x over previous
import contextlib
import numpy as np
import ml_dtypes
import concourse.bass as bass
import concourse.mybir as mybir
from concourse.bass_utils import run_bass_kernel_spmd

F32 = mybir.dt.float32
BF16 = mybir.dt.bfloat16
I32 = mybir.dt.int32
AF = mybir.ActivationFunctionType
ALU = mybir.AluOpType
LN2 = float(np.log(2.0))
bf16 = ml_dtypes.bfloat16



def _threefry2x32(k0, k1, x0, x1):
    R0 = [13, 15, 26, 6]
    R1 = [17, 29, 16, 24]
    ks0 = np.uint32(k0)
    ks1 = np.uint32(k1)
    ks2 = np.uint32(ks0 ^ ks1 ^ np.uint32(0x1BD11BDA))
    ks = [ks0, ks1, ks2]
    x0 = (x0 + ks[0]).astype(np.uint32)
    x1 = (x1 + ks[1]).astype(np.uint32)
    for i, rots in enumerate([R0, R1, R0, R1, R0]):
        for r in rots:
            x0 = (x0 + x1).astype(np.uint32)
            x1 = ((x1 << np.uint32(r)) | (x1 >> np.uint32(32 - r))).astype(np.uint32)
            x1 = (x1 ^ x0).astype(np.uint32)
        x0 = (x0 + ks[(i + 1) % 3]).astype(np.uint32)
        x1 = (x1 + ks[(i + 2) % 3] + np.uint32(i + 1)).astype(np.uint32)
    return x0, x1


def dropout_keep_mask(shape, key=42, p_keep=0.8):
    n = int(np.prod(shape))
    y0, y1 = _threefry2x32(0, key, np.zeros(n, np.uint32),
                           np.arange(n, dtype=np.uint32))
    bits = (y0 ^ y1).astype(np.uint32)
    f = ((bits >> np.uint32(9)) | np.uint32(0x3F800000)).view(np.float32) \
        - np.float32(1.0)
    return (f < np.float32(p_keep)).reshape(shape)




class Cfg:
    def __init__(self, T=8192, OWN=2048, SW=512):
        self.T = T
        self.OWN = OWN
        self.SW = SW
        self.D = 1024
        self.H = 16
        self.HD = 64
        self.HF = 4096
        self.NSW = OWN // SW
        self.NKT = T // 128
        self.NTT = T // 128
        self.NCH = self.D // 128
        self.NHJ = self.HF // 128


CPOOL_VALS = [
    1e-38,
    1e-5,
    -0.5,
    -80.0,
    -LN2,
    LN2,
    9.0 * LN2,
    -9.0 * LN2,
    24.0 * LN2,
    -24.0 * LN2,
]


class P:

    def __init__(self, nc, cfg, t, eng_name, eng):
        self.nc = nc
        self.cfg = cfg
        self.t = t
        self.eng_name = eng_name
        self.eng = eng
        self.sems = t["sems"]
        self.cnt = {k: 0 for k in self.sems}
        self.tr_evac = [0, 0]
        self.tr_store = [0, 0]
        self.vt_store = [0, 0]
        self.s1_act = 0
        self.natst = [0, 0]
        self.xb_done = [0, 0]
        self.kb_done = [0, 0]
        self.kch_free = [0, 0]
        self.vsl_free = [0, 0, 0, 0]
        self.mega_freeA = [0, 0]
        self.mega_freeB = [0, 0]
        self.et_free = [0, 0]
        self.nd_free = 0
        self.bc_free = 0
        self.bc2_free = 0
        self.mmps_free = [0, 0]
        self.w1t_free = [0, 0]
        self.w2t_free = [0, 0]
        self.qf_free = 0
        self.wbuf_done = [0, 0]
        self.ych_store = 0
        self.xoch_done = 0
        self.mkch_done = 0
        self.xnc_free = [0, 0]
        self.drec_free = 0
        self.stats_free = 0
        self.xq_free = 0
        self.hq_free = 0
        self.ffn_free = [0, 0]
        self.evac_mark = 0
        self.nd_free_act = 0
        self.nd_free_dve = 0
        self.attnN_free = 0
        self.dall_free = 0
        self.xq_ready = 0
        self.hq_ready = 0

    def on(self, name):
        return self.eng_name == name

    def bump(self, sem, n):
        self.cnt[sem] += n
        return self.cnt[sem]

    def wait(self, engname, sem, val):
        if self.on(engname) and val > 0:
            self.eng.wait_ge(self.sems[sem], val)

    def dma(self, q, out, in_, sem="ld"):
        v = self.bump(sem, 16)
        if self.on(q):
            self.eng.dma_start(out=out, in_=in_).then_inc(self.sems[sem], 16)
        return v

    def op(self, engname, sem, fn):
        v = self.bump(sem, 1)
        if self.on(engname):
            fn(self.eng).then_inc(self.sems[sem], 1)
        return v

    def opq(self, engname, fn):
        if self.on(engname):
            fn(self.eng)

    def vop(self, fn):
        self.op("vector", "dve", fn)
        self.wait("vector", "dve", self.cnt["dve"])
        return self.cnt["dve"]

    def ladder(self, z_ap, b_ap, nlev):
        for k in reversed(range(nlev)):
            c = float(2.0 ** k)
            self.vop(lambda e, c=c: e.tensor_scalar(b_ap, z_ap, c, None, ALU.is_ge))
            self.vop(lambda e, c=c: e.scalar_tensor_tensor(
                z_ap, b_ap, -c, z_ap, ALU.mult, ALU.add))

    def q43_tile(self, v_ap, out_ap, W, elev, bufs=None):
        t = self.t
        if bufs is None:
            qa = t["qa"][:, 0:W]
            qb = t["qb"][:, 0:W]
            qc = t["qc"][:, 0:W]
            qd = t["qd"][:, 0:W]
        else:
            qa, qb, qc, qd = (b[:, 0:W] for b in bufs)
        cp = t["cpool"]
        self.wait("scalar", "dve", self.cnt["dve"])
        self.op("scalar", "act", lambda e: e.activation(
            qa, v_ap, AF.Square, bias=0.0, scale=1.0))
        self.wait("scalar", "act", self.cnt["act"])
        self.op("scalar", "act", lambda e: e.activation(
            qb, qa, AF.Ln, bias=cp[:, 0:1], scale=1.0))
        self.wait("scalar", "act", self.cnt["act"])
        av = self.cnt["act"]
        self.wait("vector", "act", av)
        self.vop(lambda e: e.tensor_scalar(qb, qb, 0.5 / LN2, 6.0, ALU.mult, ALU.add))
        self.vop(lambda e: e.tensor_scalar(
            qb, qb, 0.0, float(2.0 ** elev) - 1e-3, ALU.max, ALU.min))
        self.vop(lambda e: e.tensor_copy(qc, qb))
        self.ladder(qb, qd, elev)
        self.vop(lambda e: e.tensor_tensor(qc, qc, qb, ALU.subtract))
        dv = self.cnt["dve"]
        self.wait("scalar", "dve", dv)
        self.op("scalar", "act", lambda e: e.activation(
            qb, qc, AF.Exp, bias=cp[:, 6:7], scale=cp[:, 4:5]))
        self.op("scalar", "act", lambda e: e.activation(
            qd, qc, AF.Exp, bias=cp[:, 7:8], scale=cp[:, 5:6]))
        av = self.cnt["act"]
        self.wait("vector", "act", av)
        self.vop(lambda e: e.tensor_tensor(qa, v_ap, qb, ALU.mult))
        self.vop(lambda e: e.tensor_scalar(qa, qa, 16.5, None, ALU.add))
        self.vop(lambda e: e.tensor_copy(qc, qa))
        self.ladder(qa, qb, 6)
        self.vop(lambda e: e.tensor_tensor(qc, qc, qa, ALU.subtract))
        self.vop(lambda e: e.scalar_tensor_tensor(
            out_ap, qc, -16.0, qd, ALU.add, ALU.mult))
        return self.cnt["dve"]

    def qh_tile(self, psum_ap, b1col, out_ap, W):
        t = self.t
        qa = t["qa"][:, 0:W]
        qb = t["qb"][:, 0:W]
        qc = t["qc"][:, 0:W]
        qd = t["qd"][:, 0:W]
        qe = t["qe"][:, 0:W]
        cp = t["cpool"]
        self.vop(lambda e: e.tensor_scalar(
            qa, psum_ap, b1col, 1e-30, ALU.add, ALU.max))
        self.evac_mark = self.cnt["dve"]
        dv = self.cnt["dve"]
        self.wait("scalar", "dve", dv)
        self.op("scalar", "act", lambda e: e.activation(
            qb, qa, AF.Ln, bias=0.0, scale=1.0))
        self.wait("scalar", "act", self.cnt["act"])
        av = self.cnt["act"]
        self.wait("vector", "act", av)
        self.vop(lambda e: e.tensor_scalar(qb, qb, 1.0 / LN2, 6.0, ALU.mult, ALU.add))
        self.vop(lambda e: e.tensor_scalar(qb, qb, 0.0, 12.0 - 1e-3, ALU.max, ALU.min))
        self.vop(lambda e: e.tensor_copy(qc, qb))
        self.ladder(qb, qd, 4)
        self.vop(lambda e: e.tensor_tensor(qc, qc, qb, ALU.subtract))
        dv = self.cnt["dve"]
        self.wait("scalar", "dve", dv)
        self.op("scalar", "act", lambda e: e.activation(
            qb, qc, AF.Exp, bias=cp[:, 6:7], scale=cp[:, 4:5]))
        self.op("scalar", "act", lambda e: e.activation(
            qd, qc, AF.Exp, bias=cp[:, 7:8], scale=cp[:, 5:6]))
        av = self.cnt["act"]
        self.wait("vector", "act", av)
        self.vop(lambda e: e.tensor_tensor(qa, qa, qb, ALU.mult))
        self.vop(lambda e: e.tensor_copy(qc, qa))
        self.ladder(qa, qe, 4)
        self.vop(lambda e: e.tensor_tensor(qc, qc, qa, ALU.subtract))
        self.vop(lambda e: e.tensor_scalar(qa, qa, -0.5, None, ALU.add))
        self.vop(lambda e: e.scalar_tensor_tensor(
            qb, qa, 0.0, qc, ALU.is_ge, ALU.add))
        self.vop(lambda e: e.tensor_tensor(qc, qc, qe, ALU.add))
        self.vop(lambda e: e.tensor_tensor(qa, qa, qa, ALU.mult))
        self.vop(lambda e: e.tensor_scalar(
            qa, qa, float(2.0 ** -16 * 1.002), None, ALU.is_le))
        self.vop(lambda e: e.tensor_tensor(qc, qc, qb, ALU.subtract))
        self.vop(lambda e: e.tensor_tensor(qc, qa, qc, ALU.mult))
        self.vop(lambda e: e.tensor_tensor(qb, qb, qc, ALU.add))
        self.vop(lambda e: e.tensor_tensor(out_ap, qb, qd, ALU.mult))
        return self.cnt["dve"]

    def qy_tile(self, psum_ap, b2col, out_ap, W):
        t = self.t
        qa = t["qa"][:, 0:W]
        qb = t["qb"][:, 0:W]
        qc = t["qc"][:, 0:W]
        qd = t["qd"][:, 0:W]
        qi = t["qi"][:, 0:W]
        cp = t["cpool"]
        self.vop(lambda e: e.tensor_scalar(qa, psum_ap, b2col, None, ALU.add))
        self.evac_mark = self.cnt["dve"]
        self.wait("scalar", "dve", self.cnt["dve"])
        self.op("scalar", "act", lambda e: e.activation(
            qb, qa, AF.Square, bias=0.0, scale=1.0))
        self.wait("scalar", "act", self.cnt["act"])
        self.op("scalar", "act", lambda e: e.activation(
            qb, qb, AF.Ln, bias=cp[:, 0:1], scale=1.0))
        self.wait("scalar", "act", self.cnt["act"])
        av = self.cnt["act"]
        self.wait("vector", "act", av)
        self.vop(lambda e: e.tensor_scalar(qb, qb, 0.5 / LN2, 14.0, ALU.mult, ALU.add))
        self.vop(lambda e: e.tensor_scalar(qb, qb, 0.0, 20.0 - 1e-3, ALU.max, ALU.min))
        self.vop(lambda e: e.tensor_copy(qc, qb))
        self.ladder(qb, qd, 5)
        self.vop(lambda e: e.tensor_tensor(qc, qc, qb, ALU.subtract))
        dv = self.cnt["dve"]
        self.wait("scalar", "dve", dv)
        self.op("scalar", "act", lambda e: e.activation(
            qb, qc, AF.Exp, bias=cp[:, 8:9], scale=cp[:, 4:5]))
        self.op("scalar", "act", lambda e: e.activation(
            qd, qc, AF.Exp, bias=cp[:, 9:10], scale=cp[:, 5:6]))
        av = self.cnt["act"]
        self.wait("vector", "act", av)
        self.vop(lambda e: e.tensor_tensor(qa, qa, qb, ALU.mult))
        self.vop(lambda e: e.tensor_copy(qi, qa))
        self.vop(lambda e: e.tensor_copy(qa, qi))
        self.vop(lambda e: e.tensor_tensor(out_ap, qa, qd, ALU.mult))
        return self.cnt["dve"]

    def emit(self):
        cfg = self.cfg
        t = self.t
        cp = t["cpool"]
        SW = cfg.SW

        id_ld = self.dma("sync", t["ident"], t["identin"], sem="l_id")
        for j, val in enumerate(CPOOL_VALS):
            self.op("gpsimd", "gp", lambda e, j=j, val=val: e.memset(
                cp[:, j:j + 1], val))
        for par in range(4):
            self.op("gpsimd", "gp", lambda e, par=par: e.memset(
                t["vsl"][par][:, 64:65], 1.0))
            self.op("gpsimd", "gp", lambda e, par=par: e.memset(
                t["vsl"][par][:, 129:130], 1.0))
        self.op("gpsimd", "gp", lambda e: e.memset(t["onesbf"], 1.0))
        self.op("gpsimd", "gp", lambda e: e.memset(t["onesf32"], 1.0))
        self.op("gpsimd", "gp", lambda e: e.memset(t["onesf32c"], 1.0))
        gp_init = self.cnt["gp"]
        for en in ("vector", "scalar", "tensor"):
            self.wait(en, "gp", gp_init)
        self.wait("tensor", "l_id", id_ld)

        if cfg.NCH * SW >= 4096:
            WQP = 1024
            wbufs_p = (t["attnN"][:, 0:WQP], t["attnN"][:, WQP:2 * WQP],
                       t["attnN"][:, 2 * WQP:3 * WQP],
                       t["attnN"][:, 3 * WQP:4 * WQP])
        else:
            WQP = SW
            wbufs_p = None

        def wj_unit(wsrc, nat, j, ncols):
            par = j % 2
            dbl = 2 * ncols <= max(cfg.NHJ * SW, cfg.HF)
            wqs = t["hq"][:, (j % 2) * ncols:(j % 2 + 1) * ncols] if dbl                 else t["hq"][:, 0:ncols]
            wb = t["wbuf"][par][:, 0:ncols]
            self.wait("sync", "dve", self.wbuf_done[par])
            ldv = self.dma("sync", wb, wsrc[j * 128:(j + 1) * 128, :],
                           sem=f"l_wb{par}")
            self.wait("vector", f"l_wb{par}", ldv)
            self.wait("scalar", f"l_wb{par}", ldv)
            self.wait("vector", f"s_nat{par}", self.natst[par])
            if not dbl:
                self.wait("vector", f"s_nat{1 - par}", self.natst[1 - par])
            for sl in range(ncols // WQP):
                self.q43_tile(wb[:, sl * WQP:(sl + 1) * WQP],
                              wqs[:, sl * WQP:(sl + 1) * WQP], WQP, 3,
                              bufs=wbufs_p)
            self.wbuf_done[par] = self.cnt["dve"]
            self.wait("gpsimd", "dve", self.wbuf_done[par])
            self.natst[par] = self.dma(
                "gpsimd", nat[j * 128:(j + 1) * 128, :],
                wqs, sem=f"s_nat{par}")


        for tt in range(cfg.NTT):
            par = tt % 2
            xbuf = t["xbuf"][par]
            self.wait("sync", "dve", self.xb_done[par])
            ldv = self.dma("sync", xbuf, t["x_tok"][tt * 128:(tt + 1) * 128, :],
                           sem=f"l_xb{par}")
            self.wait("vector", f"l_xb{par}", ldv)
            self.wait("scalar", f"l_xb{par}", ldv)
            self.wait("scalar", "dve", self.cnt["dve"])
            self.op("scalar", "act", lambda e: e.activation(
                t["sq1"], xbuf, AF.Square, bias=0.0, scale=1.0))
            sq_act = self.cnt["act"]
            self.vop(lambda e: e.reduce_sum(t["s1"], xbuf,
                                            axis=mybir.AxisListType.X))
            self.wait("vector", "act", sq_act)
            self.vop(lambda e: e.reduce_sum(t["s2"], t["sq1"],
                                            axis=mybir.AxisListType.X))
            self.vop(lambda e: e.tensor_scalar(
                t["s1"], t["s1"], 1.0 / 1024.0, None, ALU.mult))
            self.vop(lambda e: e.tensor_tensor(t["s3"], t["s1"], t["s1"], ALU.mult))
            self.vop(lambda e: e.scalar_tensor_tensor(
                t["s2"], t["s2"], 1.0 / 1024.0, t["s3"], ALU.mult, ALU.subtract))
            dvv = self.cnt["dve"]
            self.wait("scalar", "dve", dvv)
            self.op("scalar", "act", lambda e: e.activation(
                t["s2"], t["s2"], AF.Ln, bias=cp[:, 1:2], scale=1.0))
            self.wait("scalar", "act", self.cnt["act"])
            self.op("scalar", "act", lambda e: e.activation(
                t["s2"], t["s2"], AF.Exp, bias=0.0, scale=cp[:, 2:3]))
            self.wait("scalar", "act", self.cnt["act"])
            avv = self.cnt["act"]
            self.wait("vector", "act", avv)
            self.wait("vector", f"s_vt{par}", self.vt_store[par])
            self.vop(lambda e, par=par: e.tensor_scalar(
                t["xnc"][par], xbuf, t["s1"][:, 0:1], t["s2"][:, 0:1],
                ALU.subtract, ALU.mult))
            self.xb_done[par] = self.cnt["dve"]
            xnc_dve = self.cnt["dve"]
            self.wait("gpsimd", "dve", xnc_dve)
            self.vt_store[par] = self.dma(
                "gpsimd", t["Xn"][tt * 128:(tt + 1) * 128, :],
                t["xnc"][par], sem=f"s_vt{par}")
        p0_vt = [self.cnt["s_vt0"], self.cnt["s_vt1"]]
        for ch in range(cfg.NCH):
            par = ch % 2
            self.wait("sync", "s_vt0", p0_vt[0])
            self.wait("sync", "s_vt1", p0_vt[1])
            self.wait("sync", f"s_kc{par}", self.kb_done[par])
            if self.on("sync"):
                self.eng.dma_start_transpose(
                    t["kch"][par], t["Xn"][:, ch * 128:(ch + 1) * 128]
                ).then_inc(self.sems[f"l_kc{par}"], 16)
            ldv = self.bump(f"l_kc{par}", 16)
            self.wait("vector", f"l_kc{par}", ldv)
            self.wait("vector", "act", self.s1_act)
            self.vop(lambda e, par=par: e.reduce_sum(
                t["s1"], t["kch"][par], axis=mybir.AxisListType.X))
            self.vop(lambda e: e.tensor_scalar(
                t["s1"], t["s1"], -1.0 / float(cfg.T), None, ALU.mult))
            kdv = self.cnt["dve"]
            self.wait("scalar", f"l_kc{par}", ldv)
            self.wait("scalar", "dve", kdv)
            self.op("scalar", "act", lambda e, par=par: e.activation(
                t["kch"][par], t["kch"][par], AF.Identity,
                bias=t["s1"][:, 0:1], scale=1.0))
            kca = self.cnt["act"]
            self.s1_act = kca
            self.wait("gpsimd", "act", kca)
            self.kb_done[par] = self.dma(
                "gpsimd", t["Kfm"][ch * 128:(ch + 1) * 128, :], t["kch"][par],
                sem=f"s_kc{par}")
            wj_unit(t["w1"], t["w1qn"], 2 * ch, cfg.D)
            wj_unit(t["w1"], t["w1qn"], 2 * ch + 1, cfg.D)
        p1_kc = [self.cnt["s_kc0"], self.cnt["s_kc1"]]
        self.dma("sync", t["b1q"], t["b1f"], sem="l_b")
        ldb = self.dma("sync", t["b2q"], t["b2f"], sem="l_b")
        self.wait("vector", "l_b", ldb)
        self.wait("scalar", "l_b", ldb)
        self.q43_tile(t["b1q"], t["b1q"], cfg.NHJ, 3)
        self.q43_tile(t["b2q"], t["b2q"], cfg.NCH, 3)

        def w_prep(wsrc, nat, dst, nrow_tiles, ncols, stages, stage_sems,
                   j0=0):
            for j in range(j0, nrow_tiles):
                wj_unit(wsrc, nat, j, ncols)
            nat_done = (self.cnt["s_nat0"], self.cnt["s_nat1"])
            for ch in range(ncols // 128):
                sp = ch % 2
                self.wait("sync", "s_nat0", nat_done[0])
                self.wait("sync", "s_nat1", nat_done[1])
                self.wait("sync", f"s_tr{sp}", self.tr_store[sp])
                if self.on("sync"):
                    self.eng.dma_start_transpose(
                        stages[sp][:, 0:nrow_tiles * 128],
                        nat[:, ch * 128:(ch + 1) * 128]
                    ).then_inc(self.sems[stage_sems[sp]], 16)
                sld = self.bump(stage_sems[sp], 16)
                self.wait("gpsimd", stage_sems[sp], sld)
                self.tr_store[sp] = self.dma(
                    "gpsimd", dst[ch * 128:(ch + 1) * 128, :],
                    stages[sp][:, 0:nrow_tiles * 128], sem=f"s_tr{sp}")

        w_prep(t["w1"], t["w1qn"], t["w1qT"], cfg.NHJ, cfg.D,
               (t["w2t"][0], t["w2t"][1]), ("l_w20", "l_w21"),
               j0=2 * cfg.NCH)
        w_prep(t["w2"], t["w2qn"], t["w2qT"], cfg.NCH, cfg.HF,
               (t["w1t"][0], t["w1t"][1]), ("l_w10", "l_w11"))
        p2_nat = (self.cnt["s_nat0"], self.cnt["s_nat1"])
        p2_tr0 = self.tr_store[0]
        p2_tr1 = self.tr_store[1]
        self.wait("vector", "s_nat0", p2_nat[0])
        self.wait("vector", "s_nat1", p2_nat[1])
        self.wait("vector", "s_tr0", p2_tr0)
        self.wait("vector", "s_tr1", p2_tr1)

        def emit_ffn_unit(fs, kind, idx):
            resid = t["resid"][fs % 2]
            if kind == "hj":
                hj = idx
                mpar = hj % 2
                self.wait("sync", "pe", self.w1t_free[mpar])
                w1_ld = self.dma(
                    "sync", t["w1t3"][mpar],
                    t["w1qTr"][:, :, hj * 128:(hj + 1) * 128],
                    sem=f"l_w1{mpar}")
                self.wait("tensor", f"l_w1{mpar}", w1_ld)
                self.wait("tensor", "dve", self.xq_ready)
                self.wait("tensor", "dve", self.ffn_free[mpar])
                if mpar == 0:
                    self.wait("tensor", "dve", self.bc_free)
                else:
                    self.wait("tensor", "dve", self.bc2_free)
                for ch in range(cfg.NCH - 1):
                    self.opq("tensor", lambda e, ch=ch, mpar=mpar: e.matmul(
                        t["ffnps"][mpar], t["w1t"][mpar][:, ch * 128:(ch + 1) * 128],
                        t["xq"][:, ch * SW:(ch + 1) * SW],
                        start=(ch == 0), stop=False))
                self.op("tensor", "pe", lambda e, mpar=mpar: e.matmul(
                    t["ffnps"][mpar],
                    t["w1t"][mpar][:, (cfg.NCH - 1) * 128:cfg.NCH * 128],
                    t["xq"][:, (cfg.NCH - 1) * SW:cfg.NCH * SW],
                    start=False, stop=True))
                mm1_pe = self.cnt["pe"]
                self.w1t_free[mpar] = mm1_pe
                self.xq_free = mm1_pe
                self.wait("vector", "pe", mm1_pe)
                self.qh_tile(t["ffnps"][mpar], t["b1q"][:, hj:hj + 1],
                             t["hq"][:, hj * SW:(hj + 1) * SW], SW)
                self.ffn_free[mpar] = self.evac_mark
                if hj == cfg.NHJ - 1:
                    self.hq_ready = self.cnt["dve"]
            else:
                dj = idx
                mpar = dj % 2
                self.wait("sync", "pe", self.w2t_free[mpar])
                w2_ld = self.dma(
                    "sync", t["w2t3"][mpar],
                    t["w2qTr"][:, :, dj * 128:(dj + 1) * 128],
                    sem=f"l_w2{mpar}")
                self.wait("tensor", f"l_w2{mpar}", w2_ld)
                self.wait("tensor", "dve", self.hq_ready)
                self.wait("tensor", "dve", self.ffn_free[mpar])
                if mpar == 0:
                    self.wait("tensor", "dve", self.bc_free)
                else:
                    self.wait("tensor", "dve", self.bc2_free)
                for k in range(cfg.NHJ - 1):
                    self.opq("tensor", lambda e, k=k, mpar=mpar: e.matmul(
                        t["ffnps"][mpar], t["w2t"][mpar][:, k * 128:(k + 1) * 128],
                        t["hq"][:, k * SW:(k + 1) * SW],
                        start=(k == 0), stop=False))
                self.op("tensor", "pe", lambda e, mpar=mpar: e.matmul(
                    t["ffnps"][mpar],
                    t["w2t"][mpar][:, (cfg.NHJ - 1) * 128:cfg.NHJ * 128],
                    t["hq"][:, (cfg.NHJ - 1) * SW:cfg.NHJ * SW],
                    start=False, stop=True))
                mm2f_pe = self.cnt["pe"]
                self.w2t_free[mpar] = mm2f_pe
                self.hq_free = mm2f_pe
                self.wait("vector", "pe", mm2f_pe)
                self.wait("vector", "s_out", self.ych_store)
                self.qy_tile(t["ffnps"][mpar], t["b2q"][:, dj:dj + 1], t["ych"], SW)
                self.ffn_free[mpar] = self.evac_mark
                self.wait("sync", "dve", self.mkch_done)
                mk_ld = self.dma("sync", t["mkch"],
                                 t["mask_own"][dj * 128:(dj + 1) * 128,
                                               fs * SW:(fs + 1) * SW], sem="l_mk")
                self.wait("vector", "l_mk", mk_ld)
                self.vop(lambda e: e.tensor_tensor(
                    t["ych"], t["ych"], t["mkch"], ALU.mult))
                self.mkch_done = self.cnt["dve"]
                self.vop(lambda e, dj=dj, resid=resid: e.tensor_tensor(
                    t["ych"], t["ych"], resid[:, dj * SW:(dj + 1) * SW],
                    ALU.add))
                ych_dve = self.cnt["dve"]
                self.wait("gpsimd", "dve", ych_dve)
                self.ych_store = self.dma(
                    "gpsimd", t["out_fm"][dj * 128:(dj + 1) * 128,
                                          fs * SW:(fs + 1) * SW],
                    t["ych"], sem="s_out")

        units = [("hj", j) for j in range(cfg.NHJ)] + \
                [("dj", j) for j in range(cfg.NCH)]
        per_slot = (len(units) + cfg.H // 2 - 1) // (cfg.H // 2)

        for s in range(cfg.NSW + 1):
            if s < cfg.NSW:
                resid = t["resid"][s % 2]
                self.wait("sync", "s_tr0", p2_tr0)
                self.wait("sync", "s_tr1", p2_tr1)
                self.wait("sync", "s_kc0", p1_kc[0])
                self.wait("sync", "s_kc1", p1_kc[1])
                self.wait("sync", "s_vt0", p0_vt[0])
                self.wait("sync", "s_vt1", p0_vt[1])
                self.wait("sync", "pec", self.qf_free)
                for ch in range(cfg.NCH):
                    self.dma("sync", t["qf"][:, ch * SW:(ch + 1) * SW],
                             t["Kfm"][ch * 128:(ch + 1) * 128,
                                      s * SW:(s + 1) * SW], sem="l_qf")
                qf_ld = self.cnt["l_qf"]

                for p in range(cfg.H // 2):
                    par = p % 2
                    self.wait("sync", "pec", self.kch_free[par])
                    ldk = self.dma("sync", t["kch"][par],
                                   t["Kfm"][p * 128:(p + 1) * 128, :],
                                   sem=f"l_kc{par}")
                    prev = None
                    for kt in range(cfg.NKT + 1):
                        if kt < cfg.NKT:
                            kpar = kt % 2
                            vpar = kt % 4
                            self.wait("sync", "pe", self.vsl_free[vpar])
                            ldvs = self.dma(
                                "sync", t["vsl3"][vpar][:, :, 0:64],
                                t["XnG"][kt * 128:(kt + 1) * 128,
                                         2 * p:2 * p + 2, :],
                                sem=f"l_vs{vpar}")
                            if kt == 0:
                                self.wait("tensor", f"l_kc{par}", ldk)
                                self.wait("tensor", "l_qf", qf_ld)
                            self.wait("tensor", "act", self.mega_freeA[kpar])
                            self.opq("tensor", lambda e, p=p, kt=kt, kpar=kpar, par=par:
                                     e.matmul(t["megaAB"][kpar][:, 0:SW],
                                              t["kch"][par][0:64, kt * 128:(kt + 1) * 128],
                                              t["qf"][0:64, p * SW:(p + 1) * SW],
                                              start=True, stop=True, tile_position=(0, 0)))
                            self.op("tensor", "pec", lambda e, p=p, kt=kt, kpar=kpar, par=par:
                                    e.matmul(t["megaAB"][kpar][:, SW:2 * SW],
                                             t["kch"][par][64:128, kt * 128:(kt + 1) * 128],
                                             t["qf"][64:128, p * SW:(p + 1) * SW],
                                             start=True, stop=True, tile_position=(64, 0)))
                            corr_pe = self.cnt["pec"]
                            self.wait("scalar", "pec", corr_pe)
                            self.wait("scalar", "pe", self.et_free[kpar])
                            self.op("scalar", "act", lambda e, kpar=kpar: e.activation(
                                t["etAB"][kpar], t["megaAB"][kpar], AF.Exp,
                                bias=cp[:, 3:4], scale=1.0))
                            exp_act = self.cnt["act"]
                            self.mega_freeA[kpar] = exp_act
                            cur = (kt, kpar, vpar, exp_act, ldvs)
                        else:
                            cur = None
                        if prev is not None:
                            pkt, pkpar, pvpar, pexp, pldvs = prev
                            self.wait("tensor", "act", pexp)
                            self.wait("tensor", f"l_vs{pvpar}", pldvs)
                            if pkt == 0:
                                self.wait("tensor", "act", self.nd_free_act)
                                self.wait("tensor", "dve", self.nd_free_dve)
                            self.opq("tensor", lambda e, pkt=pkt, pkpar=pkpar,
                                     pvpar=pvpar: e.matmul(
                                t["ndA"], t["vsl"][pvpar][:, 0:65],
                                t["etAB"][pkpar][:, 0:SW],
                                start=(pkt == 0), stop=(pkt == cfg.NKT - 1)))
                            self.op("tensor", "pe", lambda e, pkt=pkt, pkpar=pkpar,
                                    pvpar=pvpar: e.matmul(
                                t["ndB"], t["vsl"][pvpar][:, 65:130],
                                t["etAB"][pkpar][:, SW:2 * SW],
                                start=(pkt == 0), stop=(pkt == cfg.NKT - 1)))
                            mm2_pe = self.cnt["pe"]
                            self.et_free[pkpar] = mm2_pe
                            self.vsl_free[pvpar] = mm2_pe
                        prev = cur
                    self.kch_free[par] = self.cnt["pec"]
                    nd_pe = self.cnt["pe"]
                    h0, h1 = 2 * p, 2 * p + 1
                    self.wait("scalar", "pe", nd_pe)
                    self.wait("scalar", "dve", self.attnN_free)
                    self.wait("scalar", "dve", self.dall_free)
                    self.op("scalar", "act", lambda e, p=p: e.copy(
                        t["attnN"][0:64, p * SW:(p + 1) * SW], t["ndA"][0:64, :]))
                    self.op("scalar", "act", lambda e, p=p: e.copy(
                        t["attnN"][64:128, p * SW:(p + 1) * SW], t["ndB"][0:64, :]))
                    self.op("scalar", "act", lambda e, h0=h0: e.copy(
                        t["Dall"][h0], t["ndA"][64:65, :]))
                    self.op("scalar", "act", lambda e, h1=h1: e.copy(
                        t["Dall"][h1], t["ndB"][64:65, :]))
                    self.nd_free_act = self.cnt["act"]
                    self.wait("vector", "act", self.nd_free_act)
                    self.wait("vector", "pe", self.drec_free)
                    self.vop(lambda e, h0=h0: e.reciprocal(t["Drec"][h0], t["Dall"][h0]))
                    self.vop(lambda e, h1=h1: e.reciprocal(t["Drec"][h1], t["Dall"][h1]))
                    self.dall_free = self.cnt["dve"]
                    self.wait("tensor", "dve", self.dall_free)
                    self.wait("tensor", "dve", self.ffn_free[0])
                    self.wait("tensor", "dve", self.bc_free)
                    self.opq("tensor", lambda e, h0=h0: e.matmul(
                        t["bcps"][0:64, :], t["onesf32c"][0:1, 0:64], t["Drec"][h0],
                        start=True, stop=True, tile_position=(0, 0)))
                    self.op("tensor", "pe", lambda e, h1=h1: e.matmul(
                        t["bcps"][64:128, :], t["onesf32c"][0:1, 0:64], t["Drec"][h1],
                        start=True, stop=True, tile_position=(0, 64)))
                    bc_pe = self.cnt["pe"]
                    self.drec_free = bc_pe
                    self.wait("sync", "dve", self.xoch_done)
                    xo_ld = self.dma("sync", t["xoch"],
                                     t["x_own"][p * 128:(p + 1) * 128,
                                                s * SW:(s + 1) * SW], sem="l_xo")
                    self.wait("vector", "pe", bc_pe)
                    self.wait("vector", "l_xo", xo_ld)
                    self.vop(lambda e, p=p, resid=resid: e.tensor_tensor(
                        resid[:, p * SW:(p + 1) * SW],
                        t["attnN"][:, p * SW:(p + 1) * SW], t["bcps"], ALU.mult))
                    self.bc_free = self.cnt["dve"]
                    self.vop(lambda e, p=p, resid=resid: e.tensor_tensor(
                        resid[:, p * SW:(p + 1) * SW],
                        resid[:, p * SW:(p + 1) * SW], t["xoch"], ALU.add))
                    self.xoch_done = self.cnt["dve"]
                    if s > 0:
                        for u in units[p * per_slot:(p + 1) * per_slot]:
                            emit_ffn_unit(s - 1, u[0], u[1])
            else:
                for u in units:
                    emit_ffn_unit(s - 1, u[0], u[1])

            if s < cfg.NSW:
                resid = t["resid"][s % 2]
                for ch in range(cfg.NCH):
                    self.vop(lambda e, ch=ch, resid=resid: e.tensor_tensor(
                        t["attnN"][:, ch * SW:(ch + 1) * SW],
                        resid[:, ch * SW:(ch + 1) * SW],
                        resid[:, ch * SW:(ch + 1) * SW], ALU.mult))
                sq_dve = self.cnt["dve"]
                self.wait("tensor", "dve", sq_dve)
                self.wait("tensor", "act", self.nd_free_act)
                self.wait("tensor", "dve", self.nd_free_dve)
                for ch in range(cfg.NCH):
                    if ch < cfg.NCH - 1:
                        self.opq("tensor", lambda e, ch=ch, resid=resid: e.matmul(
                            t["ndA"][0:1, :], t["onesf32"][:, 0:1],
                            resid[:, ch * SW:(ch + 1) * SW],
                            start=(ch == 0), stop=False))
                    else:
                        self.op("tensor", "pe", lambda e, ch=ch, resid=resid: e.matmul(
                            t["ndA"][0:1, :], t["onesf32"][:, 0:1],
                            resid[:, ch * SW:(ch + 1) * SW],
                            start=False, stop=True))
                for ch in range(cfg.NCH):
                    if ch < cfg.NCH - 1:
                        self.opq("tensor", lambda e, ch=ch: e.matmul(
                            t["ndB"][0:1, :], t["onesf32"][:, 0:1],
                            t["attnN"][:, ch * SW:(ch + 1) * SW],
                            start=(ch == 0), stop=False))
                    else:
                        self.op("tensor", "pe", lambda e, ch=ch: e.matmul(
                            t["ndB"][0:1, :], t["onesf32"][:, 0:1],
                            t["attnN"][:, ch * SW:(ch + 1) * SW],
                            start=False, stop=True))
                sums_pe = self.cnt["pe"]
                self.wait("vector", "pe", sums_pe)
                self.wait("vector", "pe", self.stats_free)
                self.vop(lambda e: e.tensor_scalar(
                    t["mrow"], t["ndA"][0:1, :], 1.0 / 1024.0, None, ALU.mult))
                self.vop(lambda e: e.tensor_tensor(
                    t["vrow"], t["mrow"], t["mrow"], ALU.mult))
                self.vop(lambda e: e.scalar_tensor_tensor(
                    t["vrow"], t["ndB"][0:1, :], 1.0 / 1024.0, t["vrow"],
                    ALU.mult, ALU.subtract))
                self.nd_free_dve = self.cnt["dve"]
                st_dve = self.cnt["dve"]
                self.wait("scalar", "dve", st_dve)
                self.op("scalar", "act", lambda e: e.activation(
                    t["vrow"], t["vrow"], AF.Ln, bias=cp[0:1, 1:2], scale=1.0))
                self.wait("scalar", "act", self.cnt["act"])
                self.op("scalar", "act", lambda e: e.activation(
                    t["vrow"], t["vrow"], AF.Exp, bias=0.0, scale=cp[0:1, 2:3]))
                self.wait("scalar", "act", self.cnt["act"])
                stat_act = self.cnt["act"]
                self.wait("tensor", "act", stat_act)
                self.wait("tensor", "dve", self.bc_free)
                self.wait("tensor", "dve", self.ffn_free[0])
                self.op("tensor", "pe", lambda e: e.matmul(
                    t["bcps"], t["onesf32c"], t["mrow"], start=True, stop=True))
                m_pe = self.cnt["pe"]
                self.stats_free = m_pe
                self.wait("vector", "pe", m_pe)
                self.wait("vector", "pe", self.xq_free)
                for ch in range(cfg.NCH):
                    self.vop(lambda e, ch=ch, resid=resid: e.tensor_tensor(
                        t["attnN"][:, ch * SW:(ch + 1) * SW],
                        resid[:, ch * SW:(ch + 1) * SW], t["bcps"], ALU.subtract))
                self.bc_free = self.cnt["dve"]
                self.wait("tensor", "dve", self.bc2_free)
                self.wait("tensor", "dve", self.ffn_free[1])
                self.op("tensor", "pe", lambda e: e.matmul(
                    t["bcps2"], t["onesf32c"], t["vrow"], start=True, stop=True))
                bc2_pe = self.cnt["pe"]
                self.stats_free = bc2_pe
                self.wait("vector", "pe", bc2_pe)
                for ch in range(cfg.NCH):
                    self.vop(lambda e, ch=ch: e.tensor_tensor(
                        t["attnN"][:, ch * SW:(ch + 1) * SW],
                        t["attnN"][:, ch * SW:(ch + 1) * SW], t["bcps2"], ALU.mult))
                    self.bc2_free = self.cnt["dve"]
                    self.q43_tile(t["attnN"][:, ch * SW:(ch + 1) * SW],
                                  t["xq"][:, ch * SW:(ch + 1) * SW], SW, 4)
                self.xq_ready = self.cnt["dve"]
                self.attnN_free = self.cnt["dve"]
                self.qf_free = self.cnt["pec"]
        self.wait("gpsimd", "s_out", self.cnt["s_out"])


def build_program(cfg):
    nc = bass.Bass("TRN2", target_bir_lowering=False, debug=False)
    T, D, HF, SW = cfg.T, cfg.D, cfg.HF, cfg.SW
    t = {}
    t["x_tok"] = nc.dram_tensor("x_tok", [T, D], F32, kind="ExternalInput").ap()
    t["x_own"] = nc.dram_tensor("x_own", [D, cfg.OWN], F32, kind="ExternalInput").ap()
    t["mask_own"] = nc.dram_tensor("mask_own", [D, cfg.OWN], F32,
                                   kind="ExternalInput").ap()
    t["w1"] = nc.dram_tensor("w1", [HF, D], F32, kind="ExternalInput").ap()
    t["w2"] = nc.dram_tensor("w2", [D, HF], F32, kind="ExternalInput").ap()
    t["b1f"] = nc.dram_tensor("b1f", [128, HF // 128], F32, kind="ExternalInput").ap()
    t["b2f"] = nc.dram_tensor("b2f", [128, D // 128], F32, kind="ExternalInput").ap()
    t["identin"] = nc.dram_tensor("identin", [128, 128], BF16,
                                  kind="ExternalInput").ap()
    t["out_fm"] = nc.dram_tensor("out_fm", [D, cfg.OWN], F32,
                                 kind="ExternalOutput").ap()
    t["Xn"] = nc.dram_tensor("Xn", [T, D], BF16, kind="Internal").ap()
    t["XnG"] = t["Xn"].rearrange("t (g c) -> t g c", g=D // 64, c=64)
    t["Kfm"] = nc.dram_tensor("Kfm", [D, T], BF16, kind="Internal").ap()
    t["w1qn"] = nc.dram_tensor("w1qn", [HF, D], BF16, kind="Internal").ap()
    t["w2qn"] = nc.dram_tensor("w2qn", [D, HF], BF16, kind="Internal").ap()
    t["w1qT"] = nc.dram_tensor("w1qT", [D, HF], BF16, kind="Internal").ap()
    t["w2qT"] = nc.dram_tensor("w2qT", [HF, D], BF16, kind="Internal").ap()
    t["w1qTr"] = t["w1qT"].rearrange("(c r) h -> r c h", c=D // 128, r=128)
    t["w2qTr"] = t["w2qT"].rearrange("(c r) d -> r c d", c=HF // 128, r=128)

    with contextlib.ExitStack() as es:
        def sb(name, shape, dtype):
            return es.enter_context(nc.sbuf_tensor(name, shape, dtype))[:]

        t["cpool"] = sb("cpool", [128, len(CPOOL_VALS)], F32)
        t["ident"] = sb("ident", [128, 128], BF16)
        t["onesbf"] = sb("onesbf", [1, 128], BF16)
        t["onesf32"] = sb("onesf32", [128, 1], F32)
        t["onesf32c"] = sb("onesf32c", [1, 128], F32)
        t["xbuf"] = [sb(f"xbuf{i}", [128, D], F32) for i in range(2)]
        t["s1"] = sb("s1", [128, 1], F32)
        t["s2"] = sb("s2", [128, 1], F32)
        t["s3"] = sb("s3", [128, 1], F32)
        xnch = [es.enter_context(nc.sbuf_tensor(f"xnc{i}", [128, D], BF16))
                for i in range(2)]
        t["xnc"] = [h[:] for h in xnch]
        t["xnc3"] = [h.ap().rearrange("p (h c) -> p h c", h=cfg.H, c=64)
                     for h in xnch]
        t["kch"] = [sb(f"kch{i}", [128, T], BF16) for i in range(2)]
        t["b1q"] = sb("b1q", [128, HF // 128], F32)
        t["b2q"] = sb("b2q", [128, D // 128], F32)
        for nm in ("qa", "qb", "qc", "qd", "qe"):
            t[nm] = sb(nm, [128, SW], F32)
        t["qi"] = sb("qi", [128, SW], I32)
        t["qf"] = sb("qfq", [128, (D // 128) * SW], BF16)
        vslh = [es.enter_context(nc.sbuf_tensor(f"vsl{i}", [128, 130], BF16))
                for i in range(4)]
        t["vsl"] = [h[:] for h in vslh]
        t["vsl3"] = [h.ap().rearrange("p (j c) -> p j c", j=2, c=65)
                     for h in vslh]
        t["etAB"] = [sb(f"etAB{i}", [128, 2 * SW], BF16) for i in range(2)]
        t["attnN"] = sb("attnN", [128, cfg.NCH * SW], F32)
        t["attnR"] = sb("attnR", [128, cfg.NCH * SW], F32)
        t["sqch"] = sb("sqch", [128, cfg.NCH * SW], F32)
        t["xq"] = sb("xq", [128, cfg.NCH * SW], BF16)
        t["hq"] = sb("hq", [128, max(cfg.NHJ * SW, HF)], BF16)
        t["wq"] = t["hq"][:, 0:HF]
        dall = sb("Dall", [1, 2 * SW], F32)
        drec = sb("Drec", [1, 2 * SW], F32)
        t["Dall"] = [dall[:, (h % 2) * SW:(h % 2 + 1) * SW] for h in range(cfg.H)]
        t["Drec"] = [drec[:, (h % 2) * SW:(h % 2 + 1) * SW] for h in range(cfg.H)]
        t["mrow"] = sb("mrow", [1, SW], F32)
        t["vrow"] = sb("vrow", [1, SW], F32)
        t["xoch"] = sb("xoch", [128, SW], F32)
        t["mkch"] = sb("mkch", [128, SW], F32)
        t["ych"] = sb("ych", [128, SW], F32)
        w1th = [es.enter_context(nc.sbuf_tensor(f"w1t{i}", [128, D], BF16))
                for i in range(2)]
        t["w1t"] = [h[:] for h in w1th]
        t["w1t3"] = [h.ap().rearrange("p (c k) -> p c k", c=cfg.NCH, k=128)
                     for h in w1th]
        w2th = [es.enter_context(nc.sbuf_tensor(f"w2t{i}", [128, HF], BF16))
                for i in range(2)]
        t["w2t"] = [h[:] for h in w2th]
        t["w2t3"] = [h.ap().rearrange("p (c k) -> p c k", c=cfg.NHJ, k=128)
                     for h in w2th]
        t["sq1"] = t["attnN"][:, 0:D]
        t["resid"] = [t["attnR"], t["sqch"]]
        if cfg.NCH * SW >= HF:
            t["wbuf"] = [t["attnR"][:, 0:HF], t["sqch"][:, 0:HF]]
        else:
            t["wbuf"] = [sb(f"wbuf{i}", [128, HF], F32) for i in range(2)]

        t["megaAB"] = [es.enter_context(
            nc.psum_tensor(f"megaAB{i}", [128, 2 * SW], F32))[:] for i in range(2)]
        t["ndA"] = es.enter_context(nc.psum_tensor("ndA", [65, SW], F32))[:]
        t["ndB"] = es.enter_context(nc.psum_tensor("ndB", [65, SW], F32))[:]
        t["ffnps"] = [es.enter_context(nc.psum_tensor(f"ffnps{i}", [128, SW], F32))[:]
                      for i in range(2)]
        t["bcps"] = t["ffnps"][0]
        t["bcps2"] = t["ffnps"][1]

        sems = {}
        semnames = ["gp", "dve", "act", "pe", "pec", "l_id", "l_b", "l_qf", "l_xo",
                    "l_mk", "s_out"]
        for i in range(2):
            semnames += [f"l_xb{i}", f"l_kc{i}", f"l_wb{i}", f"l_vs{i}",
                         f"l_w1{i}", f"l_w2{i}", f"s_vt{i}", f"s_tr{i}",
                         f"s_kc{i}", f"l_vs{i + 2}", f"s_nat{i}"]
        for nm in semnames:
            sems[nm] = es.enter_context(nc.semaphore(nm))
        t["sems"] = sems

        with nc.Block() as blk:
            @blk.sync
            def _(sync):
                P(nc, cfg, t, "sync", sync).emit()

            @blk.gpsimd
            def _(g):
                P(nc, cfg, t, "gpsimd", g).emit()

            @blk.vector
            def _(v):
                P(nc, cfg, t, "vector", v).emit()

            @blk.scalar
            def _(sc):
                P(nc, cfg, t, "scalar", sc).emit()

            @blk.tensor
            def _(te):
                P(nc, cfg, t, "tensor", te).emit()
    return nc




def make_in_maps(x, w1, b1, w2, b2, cfg, n_cores=8):
    B, T, D = x.shape
    keep = dropout_keep_mask((B, T, D))
    mask = np.where(keep, np.float32(1.25), np.float32(0.0))
    ident = np.eye(128).astype(bf16)
    b1f = np.ascontiguousarray(b1.reshape(cfg.NHJ, 128).T)
    b2f = np.ascontiguousarray(b2.reshape(cfg.NCH, 128).T)
    per_batch = n_cores // B
    in_maps = []
    for c in range(n_cores):
        b = c // per_batch
        qlo = (c % per_batch) * cfg.OWN
        in_maps.append({
            "x_tok": np.ascontiguousarray(np.roll(x[b], -qlo, axis=0)),
            "x_own": np.ascontiguousarray(x[b, qlo:qlo + cfg.OWN].T),
            "mask_own": np.ascontiguousarray(mask[b, qlo:qlo + cfg.OWN].T),
            "w1": w1, "w2": w2, "b1f": b1f, "b2f": b2f,
            "identin": ident,
        })
    return in_maps


def kernel(L0_attn_x, L0_ffn_w1, L0_ffn_b1, L0_ffn_w2, L0_ffn_b2):
    x = np.ascontiguousarray(np.asarray(L0_attn_x, np.float32))
    w1 = np.ascontiguousarray(np.asarray(L0_ffn_w1, np.float32))
    b1 = np.asarray(L0_ffn_b1, np.float32)
    w2 = np.ascontiguousarray(np.asarray(L0_ffn_w2, np.float32))
    b2 = np.asarray(L0_ffn_b2, np.float32)
    B, T, D = x.shape
    cfg = Cfg(T=T, OWN=T * B // 8, SW=512)
    nc = build_program(cfg)
    in_maps = make_in_maps(x, w1, b1, w2, b2, cfg)
    res = run_bass_kernel_spmd(nc, in_maps, core_ids=list(range(8)))
    out = np.empty((B, T, D), np.float32)
    per_batch = 8 // B
    for c in range(8):
        b = c // per_batch
        qlo = (c % per_batch) * cfg.OWN
        out[b, qlo:qlo + cfg.OWN] = np.asarray(res.results[c]["out_fm"],
                                               np.float32).T
    return out


# version 29
# speedup vs baseline: 1.2053x; 1.0004x over previous
import contextlib
import numpy as np
import ml_dtypes
import concourse.bass as bass
import concourse.mybir as mybir
from concourse.bass_utils import run_bass_kernel_spmd

F32 = mybir.dt.float32
BF16 = mybir.dt.bfloat16
I32 = mybir.dt.int32
AF = mybir.ActivationFunctionType
ALU = mybir.AluOpType
LN2 = float(np.log(2.0))
bf16 = ml_dtypes.bfloat16



def _threefry2x32(k0, k1, x0, x1):
    R0 = [13, 15, 26, 6]
    R1 = [17, 29, 16, 24]
    ks0 = np.uint32(k0)
    ks1 = np.uint32(k1)
    ks2 = np.uint32(ks0 ^ ks1 ^ np.uint32(0x1BD11BDA))
    ks = [ks0, ks1, ks2]
    x0 = (x0 + ks[0]).astype(np.uint32)
    x1 = (x1 + ks[1]).astype(np.uint32)
    for i, rots in enumerate([R0, R1, R0, R1, R0]):
        for r in rots:
            x0 = (x0 + x1).astype(np.uint32)
            x1 = ((x1 << np.uint32(r)) | (x1 >> np.uint32(32 - r))).astype(np.uint32)
            x1 = (x1 ^ x0).astype(np.uint32)
        x0 = (x0 + ks[(i + 1) % 3]).astype(np.uint32)
        x1 = (x1 + ks[(i + 2) % 3] + np.uint32(i + 1)).astype(np.uint32)
    return x0, x1


def dropout_keep_mask(shape, key=42, p_keep=0.8):
    n = int(np.prod(shape))
    y0, y1 = _threefry2x32(0, key, np.zeros(n, np.uint32),
                           np.arange(n, dtype=np.uint32))
    bits = (y0 ^ y1).astype(np.uint32)
    f = ((bits >> np.uint32(9)) | np.uint32(0x3F800000)).view(np.float32) \
        - np.float32(1.0)
    return (f < np.float32(p_keep)).reshape(shape)




class Cfg:
    def __init__(self, T=8192, OWN=2048, SW=512):
        self.T = T
        self.OWN = OWN
        self.SW = SW
        self.D = 1024
        self.H = 16
        self.HD = 64
        self.HF = 4096
        self.NSW = OWN // SW
        self.NKT = T // 128
        self.NTT = T // 128
        self.NCH = self.D // 128
        self.NHJ = self.HF // 128


CPOOL_VALS = [
    1e-38,
    1e-5,
    -0.5,
    -80.0,
    -LN2,
    LN2,
    9.0 * LN2,
    -9.0 * LN2,
    24.0 * LN2,
    -24.0 * LN2,
]


class P:

    def __init__(self, nc, cfg, t, eng_name, eng):
        self.nc = nc
        self.cfg = cfg
        self.t = t
        self.eng_name = eng_name
        self.eng = eng
        self.sems = t["sems"]
        self.cnt = {k: 0 for k in self.sems}
        self.tr_evac = [0, 0]
        self.tr_store = [0, 0]
        self.vt_store = [0, 0]
        self.s1_act = 0
        self.natst = [0, 0]
        self.xb_done = [0, 0]
        self.kb_done = [0, 0]
        self.kch_free = [0, 0]
        self.vsl_free = [0, 0, 0, 0]
        self.mega_freeA = [0, 0]
        self.mega_freeB = [0, 0]
        self.et_free = [0, 0]
        self.nd_free = 0
        self.bc_free = 0
        self.bc2_free = 0
        self.mmps_free = [0, 0]
        self.w1t_free = [0, 0]
        self.w2t_free = [0, 0]
        self.qf_free = 0
        self.wbuf_done = [0, 0]
        self.ych_store = 0
        self.xoch_done = 0
        self.mkch_done = 0
        self.xnc_free = [0, 0]
        self.drec_free = 0
        self.stats_free = 0
        self.xq_free = 0
        self.hq_free = 0
        self.ffn_free = [0, 0]
        self.evac_mark = 0
        self.nd_free_act = 0
        self.nd_free_dve = 0
        self.attnN_free = 0
        self.dall_free = 0
        self.xq_ready = 0
        self.hq_ready = 0

    def on(self, name):
        return self.eng_name == name

    def bump(self, sem, n):
        self.cnt[sem] += n
        return self.cnt[sem]

    def wait(self, engname, sem, val):
        if self.on(engname) and val > 0:
            self.eng.wait_ge(self.sems[sem], val)

    def dma(self, q, out, in_, sem="ld"):
        v = self.bump(sem, 16)
        if self.on(q):
            self.eng.dma_start(out=out, in_=in_).then_inc(self.sems[sem], 16)
        return v

    def op(self, engname, sem, fn):
        v = self.bump(sem, 1)
        if self.on(engname):
            fn(self.eng).then_inc(self.sems[sem], 1)
        return v

    def opq(self, engname, fn):
        if self.on(engname):
            fn(self.eng)

    def vop(self, fn):
        self.op("vector", "dve", fn)
        self.wait("vector", "dve", self.cnt["dve"])
        return self.cnt["dve"]

    def ladder(self, z_ap, b_ap, nlev):
        for k in reversed(range(nlev)):
            c = float(2.0 ** k)
            self.vop(lambda e, c=c: e.tensor_scalar(b_ap, z_ap, c, None, ALU.is_ge))
            self.vop(lambda e, c=c: e.scalar_tensor_tensor(
                z_ap, b_ap, -c, z_ap, ALU.mult, ALU.add))

    def q43_tile(self, v_ap, out_ap, W, elev, bufs=None):
        t = self.t
        if bufs is None:
            qa = t["qa"][:, 0:W]
            qb = t["qb"][:, 0:W]
            qc = t["qc"][:, 0:W]
            qd = t["qd"][:, 0:W]
        else:
            qa, qb, qc, qd = (b[:, 0:W] for b in bufs)
        cp = t["cpool"]
        self.wait("scalar", "dve", self.cnt["dve"])
        self.op("scalar", "act", lambda e: e.activation(
            qa, v_ap, AF.Square, bias=0.0, scale=1.0))
        self.wait("scalar", "act", self.cnt["act"])
        self.op("scalar", "act", lambda e: e.activation(
            qb, qa, AF.Ln, bias=cp[:, 0:1], scale=1.0))
        self.wait("scalar", "act", self.cnt["act"])
        av = self.cnt["act"]
        self.wait("vector", "act", av)
        self.vop(lambda e: e.tensor_scalar(qb, qb, 0.5 / LN2, 6.0, ALU.mult, ALU.add))
        self.vop(lambda e: e.tensor_scalar(
            qb, qb, 0.0, float(2.0 ** elev) - 1e-3, ALU.max, ALU.min))
        self.vop(lambda e: e.tensor_copy(qc, qb))
        self.ladder(qb, qd, elev)
        self.vop(lambda e: e.tensor_tensor(qc, qc, qb, ALU.subtract))
        dv = self.cnt["dve"]
        self.wait("scalar", "dve", dv)
        self.op("scalar", "act", lambda e: e.activation(
            qb, qc, AF.Exp, bias=cp[:, 6:7], scale=cp[:, 4:5]))
        self.op("scalar", "act", lambda e: e.activation(
            qd, qc, AF.Exp, bias=cp[:, 7:8], scale=cp[:, 5:6]))
        av = self.cnt["act"]
        self.wait("vector", "act", av)
        self.vop(lambda e: e.tensor_tensor(qa, v_ap, qb, ALU.mult))
        self.vop(lambda e: e.tensor_scalar(qa, qa, 16.5, None, ALU.add))
        self.vop(lambda e: e.tensor_copy(qc, qa))
        self.ladder(qa, qb, 6)
        self.vop(lambda e: e.tensor_tensor(qc, qc, qa, ALU.subtract))
        self.vop(lambda e: e.scalar_tensor_tensor(
            out_ap, qc, -16.0, qd, ALU.add, ALU.mult))
        return self.cnt["dve"]

    def qh_tile(self, psum_ap, b1col, out_ap, W):
        t = self.t
        qa = t["qa"][:, 0:W]
        qb = t["qb"][:, 0:W]
        qc = t["qc"][:, 0:W]
        qd = t["qd"][:, 0:W]
        qe = t["qe"][:, 0:W]
        cp = t["cpool"]
        self.vop(lambda e: e.tensor_scalar(
            qa, psum_ap, b1col, 1e-30, ALU.add, ALU.max))
        self.evac_mark = self.cnt["dve"]
        dv = self.cnt["dve"]
        self.wait("scalar", "dve", dv)
        self.op("scalar", "act", lambda e: e.activation(
            qb, qa, AF.Ln, bias=0.0, scale=1.0))
        self.wait("scalar", "act", self.cnt["act"])
        av = self.cnt["act"]
        self.wait("vector", "act", av)
        self.vop(lambda e: e.tensor_scalar(qb, qb, 1.0 / LN2, 6.0, ALU.mult, ALU.add))
        self.vop(lambda e: e.tensor_scalar(qb, qb, 0.0, 12.0 - 1e-3, ALU.max, ALU.min))
        self.vop(lambda e: e.tensor_copy(qc, qb))
        self.ladder(qb, qd, 4)
        self.vop(lambda e: e.tensor_tensor(qc, qc, qb, ALU.subtract))
        dv = self.cnt["dve"]
        self.wait("scalar", "dve", dv)
        self.op("scalar", "act", lambda e: e.activation(
            qb, qc, AF.Exp, bias=cp[:, 6:7], scale=cp[:, 4:5]))
        self.op("scalar", "act", lambda e: e.activation(
            qd, qc, AF.Exp, bias=cp[:, 7:8], scale=cp[:, 5:6]))
        av = self.cnt["act"]
        self.wait("vector", "act", av)
        self.vop(lambda e: e.tensor_tensor(qa, qa, qb, ALU.mult))
        self.vop(lambda e: e.tensor_copy(qc, qa))
        self.ladder(qa, qe, 4)
        self.vop(lambda e: e.tensor_tensor(qc, qc, qa, ALU.subtract))
        self.vop(lambda e: e.tensor_scalar(qa, qa, -0.5, None, ALU.add))
        self.vop(lambda e: e.scalar_tensor_tensor(
            qb, qa, 0.0, qc, ALU.is_ge, ALU.add))
        self.vop(lambda e: e.tensor_tensor(qc, qc, qe, ALU.add))
        self.vop(lambda e: e.tensor_tensor(qa, qa, qa, ALU.mult))
        self.vop(lambda e: e.tensor_scalar(
            qa, qa, float(2.0 ** -16 * 1.002), None, ALU.is_le))
        self.vop(lambda e: e.tensor_tensor(qc, qc, qb, ALU.subtract))
        self.vop(lambda e: e.tensor_tensor(qc, qa, qc, ALU.mult))
        self.vop(lambda e: e.tensor_tensor(qb, qb, qc, ALU.add))
        self.vop(lambda e: e.tensor_tensor(out_ap, qb, qd, ALU.mult))
        return self.cnt["dve"]

    def qy_tile(self, psum_ap, b2col, out_ap, W):
        t = self.t
        qa = t["qa"][:, 0:W]
        qb = t["qb"][:, 0:W]
        qc = t["qc"][:, 0:W]
        qd = t["qd"][:, 0:W]
        qi = t["qi"][:, 0:W]
        cp = t["cpool"]
        self.vop(lambda e: e.tensor_scalar(qa, psum_ap, b2col, None, ALU.add))
        self.evac_mark = self.cnt["dve"]
        self.wait("scalar", "dve", self.cnt["dve"])
        self.op("scalar", "act", lambda e: e.activation(
            qb, qa, AF.Square, bias=0.0, scale=1.0))
        self.wait("scalar", "act", self.cnt["act"])
        self.op("scalar", "act", lambda e: e.activation(
            qb, qb, AF.Ln, bias=cp[:, 0:1], scale=1.0))
        self.wait("scalar", "act", self.cnt["act"])
        av = self.cnt["act"]
        self.wait("vector", "act", av)
        self.vop(lambda e: e.tensor_scalar(qb, qb, 0.5 / LN2, 14.0, ALU.mult, ALU.add))
        self.vop(lambda e: e.tensor_scalar(qb, qb, 0.0, 20.0 - 1e-3, ALU.max, ALU.min))
        self.vop(lambda e: e.tensor_copy(qc, qb))
        self.ladder(qb, qd, 5)
        self.vop(lambda e: e.tensor_tensor(qc, qc, qb, ALU.subtract))
        dv = self.cnt["dve"]
        self.wait("scalar", "dve", dv)
        self.op("scalar", "act", lambda e: e.activation(
            qb, qc, AF.Exp, bias=cp[:, 8:9], scale=cp[:, 4:5]))
        self.op("scalar", "act", lambda e: e.activation(
            qd, qc, AF.Exp, bias=cp[:, 9:10], scale=cp[:, 5:6]))
        av = self.cnt["act"]
        self.wait("vector", "act", av)
        self.vop(lambda e: e.tensor_tensor(qa, qa, qb, ALU.mult))
        self.vop(lambda e: e.tensor_copy(qi, qa))
        self.vop(lambda e: e.tensor_copy(qa, qi))
        self.vop(lambda e: e.tensor_tensor(out_ap, qa, qd, ALU.mult))
        return self.cnt["dve"]

    def emit(self):
        cfg = self.cfg
        t = self.t
        cp = t["cpool"]
        SW = cfg.SW

        id_ld = self.dma("sync", t["ident"], t["identin"], sem="l_id")
        for j, val in enumerate(CPOOL_VALS):
            self.op("gpsimd", "gp", lambda e, j=j, val=val: e.memset(
                cp[:, j:j + 1], val))
        for par in range(4):
            self.op("gpsimd", "gp", lambda e, par=par: e.memset(
                t["vsl"][par][:, 64:65], 1.0))
            self.op("gpsimd", "gp", lambda e, par=par: e.memset(
                t["vsl"][par][:, 129:130], 1.0))
        self.op("gpsimd", "gp", lambda e: e.memset(t["onesbf"], 1.0))
        self.op("gpsimd", "gp", lambda e: e.memset(t["onesf32"], 1.0))
        self.op("gpsimd", "gp", lambda e: e.memset(t["onesf32c"], 1.0))
        gp_init = self.cnt["gp"]
        for en in ("vector", "scalar", "tensor"):
            self.wait(en, "gp", gp_init)
        self.wait("tensor", "l_id", id_ld)

        if cfg.NCH * SW >= 4096:
            WQP = 1024
            wbufs_p = (t["attnN"][:, 0:WQP], t["attnN"][:, WQP:2 * WQP],
                       t["attnN"][:, 2 * WQP:3 * WQP],
                       t["attnN"][:, 3 * WQP:4 * WQP])
        else:
            WQP = SW
            wbufs_p = None

        def wj_unit(wsrc, nat, j, ncols):
            par = j % 2
            dbl = 2 * ncols <= max(cfg.NHJ * SW, cfg.HF)
            wqs = t["hq"][:, (j % 2) * ncols:(j % 2 + 1) * ncols] if dbl                 else t["hq"][:, 0:ncols]
            wb = t["wbuf"][par][:, 0:ncols]
            self.wait("sync", "dve", self.wbuf_done[par])
            ldv = self.dma("sync", wb, wsrc[j * 128:(j + 1) * 128, :],
                           sem=f"l_wb{par}")
            self.wait("vector", f"l_wb{par}", ldv)
            self.wait("scalar", f"l_wb{par}", ldv)
            self.wait("vector", f"s_nat{par}", self.natst[par])
            if not dbl:
                self.wait("vector", f"s_nat{1 - par}", self.natst[1 - par])
            for sl in range(ncols // WQP):
                self.q43_tile(wb[:, sl * WQP:(sl + 1) * WQP],
                              wqs[:, sl * WQP:(sl + 1) * WQP], WQP, 3,
                              bufs=wbufs_p)
            self.wbuf_done[par] = self.cnt["dve"]
            self.wait("gpsimd", "dve", self.wbuf_done[par])
            self.natst[par] = self.dma(
                "gpsimd", nat[j * 128:(j + 1) * 128, :],
                wqs, sem=f"s_nat{par}")


        for tt in range(cfg.NTT):
            par = tt % 2
            xbuf = t["xbuf"][par]
            self.wait("sync", "dve", self.xb_done[par])
            ldv = self.dma("sync", xbuf, t["x_tok"][tt * 128:(tt + 1) * 128, :],
                           sem=f"l_xb{par}")
            self.wait("vector", f"l_xb{par}", ldv)
            self.wait("scalar", f"l_xb{par}", ldv)
            self.wait("scalar", "dve", self.cnt["dve"])
            self.op("scalar", "act", lambda e: e.activation(
                t["sq1"], xbuf, AF.Square, bias=0.0, scale=1.0))
            sq_act = self.cnt["act"]
            self.vop(lambda e: e.reduce_sum(t["s1"], xbuf,
                                            axis=mybir.AxisListType.X))
            self.wait("vector", "act", sq_act)
            self.vop(lambda e: e.reduce_sum(t["s2"], t["sq1"],
                                            axis=mybir.AxisListType.X))
            self.vop(lambda e: e.tensor_scalar(
                t["s1"], t["s1"], 1.0 / 1024.0, None, ALU.mult))
            self.vop(lambda e: e.tensor_tensor(t["s3"], t["s1"], t["s1"], ALU.mult))
            self.vop(lambda e: e.scalar_tensor_tensor(
                t["s2"], t["s2"], 1.0 / 1024.0, t["s3"], ALU.mult, ALU.subtract))
            dvv = self.cnt["dve"]
            self.wait("scalar", "dve", dvv)
            self.op("scalar", "act", lambda e: e.activation(
                t["s2"], t["s2"], AF.Ln, bias=cp[:, 1:2], scale=1.0))
            self.wait("scalar", "act", self.cnt["act"])
            self.op("scalar", "act", lambda e: e.activation(
                t["s2"], t["s2"], AF.Exp, bias=0.0, scale=cp[:, 2:3]))
            self.wait("scalar", "act", self.cnt["act"])
            avv = self.cnt["act"]
            self.wait("vector", "act", avv)
            self.wait("vector", f"s_vt{par}", self.vt_store[par])
            self.vop(lambda e, par=par: e.tensor_scalar(
                t["xnc"][par], xbuf, t["s1"][:, 0:1], t["s2"][:, 0:1],
                ALU.subtract, ALU.mult))
            self.xb_done[par] = self.cnt["dve"]
            xnc_dve = self.cnt["dve"]
            self.wait("gpsimd", "dve", xnc_dve)
            self.vt_store[par] = self.dma(
                "gpsimd", t["Xn"][tt * 128:(tt + 1) * 128, :],
                t["xnc"][par], sem=f"s_vt{par}")
        p0_vt = [self.cnt["s_vt0"], self.cnt["s_vt1"]]
        for ch in range(cfg.NCH):
            par = ch % 2
            self.wait("sync", "s_vt0", p0_vt[0])
            self.wait("sync", "s_vt1", p0_vt[1])
            self.wait("sync", f"s_kc{par}", self.kb_done[par])
            if self.on("sync"):
                self.eng.dma_start_transpose(
                    t["kch"][par], t["Xn"][:, ch * 128:(ch + 1) * 128]
                ).then_inc(self.sems[f"l_kc{par}"], 16)
            ldv = self.bump(f"l_kc{par}", 16)
            self.wait("vector", f"l_kc{par}", ldv)
            self.wait("vector", "act", self.s1_act)
            self.vop(lambda e, par=par: e.reduce_sum(
                t["s1"], t["kch"][par], axis=mybir.AxisListType.X))
            self.vop(lambda e: e.tensor_scalar(
                t["s1"], t["s1"], -1.0 / float(cfg.T), None, ALU.mult))
            kdv = self.cnt["dve"]
            self.wait("scalar", f"l_kc{par}", ldv)
            self.wait("scalar", "dve", kdv)
            self.op("scalar", "act", lambda e, par=par: e.activation(
                t["kch"][par], t["kch"][par], AF.Identity,
                bias=t["s1"][:, 0:1], scale=1.0))
            kca = self.cnt["act"]
            self.s1_act = kca
            self.wait("gpsimd", "act", kca)
            self.kb_done[par] = self.dma(
                "gpsimd", t["Kfm"][ch * 128:(ch + 1) * 128, :], t["kch"][par],
                sem=f"s_kc{par}")
            for jj in range(3 * ch, min(3 * ch + 3, cfg.NHJ)):
                wj_unit(t["w1"], t["w1qn"], jj, cfg.D)
        p1_kc = [self.cnt["s_kc0"], self.cnt["s_kc1"]]
        self.dma("sync", t["b1q"], t["b1f"], sem="l_b")
        ldb = self.dma("sync", t["b2q"], t["b2f"], sem="l_b")
        self.wait("vector", "l_b", ldb)
        self.wait("scalar", "l_b", ldb)
        self.q43_tile(t["b1q"], t["b1q"], cfg.NHJ, 3)
        self.q43_tile(t["b2q"], t["b2q"], cfg.NCH, 3)

        def w_prep(wsrc, nat, dst, nrow_tiles, ncols, stages, stage_sems,
                   j0=0):
            for j in range(j0, nrow_tiles):
                wj_unit(wsrc, nat, j, ncols)
            nat_done = (self.cnt["s_nat0"], self.cnt["s_nat1"])
            for ch in range(ncols // 128):
                sp = ch % 2
                self.wait("sync", "s_nat0", nat_done[0])
                self.wait("sync", "s_nat1", nat_done[1])
                self.wait("sync", f"s_tr{sp}", self.tr_store[sp])
                if self.on("sync"):
                    self.eng.dma_start_transpose(
                        stages[sp][:, 0:nrow_tiles * 128],
                        nat[:, ch * 128:(ch + 1) * 128]
                    ).then_inc(self.sems[stage_sems[sp]], 16)
                sld = self.bump(stage_sems[sp], 16)
                self.wait("gpsimd", stage_sems[sp], sld)
                self.tr_store[sp] = self.dma(
                    "gpsimd", dst[ch * 128:(ch + 1) * 128, :],
                    stages[sp][:, 0:nrow_tiles * 128], sem=f"s_tr{sp}")

        w_prep(t["w1"], t["w1qn"], t["w1qT"], cfg.NHJ, cfg.D,
               (t["w2t"][0], t["w2t"][1]), ("l_w20", "l_w21"),
               j0=min(3 * cfg.NCH, cfg.NHJ))
        w_prep(t["w2"], t["w2qn"], t["w2qT"], cfg.NCH, cfg.HF,
               (t["w1t"][0], t["w1t"][1]), ("l_w10", "l_w11"))
        p2_nat = (self.cnt["s_nat0"], self.cnt["s_nat1"])
        p2_tr0 = self.tr_store[0]
        p2_tr1 = self.tr_store[1]
        self.wait("vector", "s_nat0", p2_nat[0])
        self.wait("vector", "s_nat1", p2_nat[1])
        self.wait("vector", "s_tr0", p2_tr0)
        self.wait("vector", "s_tr1", p2_tr1)

        def emit_ffn_unit(fs, kind, idx):
            resid = t["resid"][fs % 2]
            if kind == "hj":
                hj = idx
                mpar = hj % 2
                self.wait("sync", "pe", self.w1t_free[mpar])
                w1_ld = self.dma(
                    "sync", t["w1t3"][mpar],
                    t["w1qTr"][:, :, hj * 128:(hj + 1) * 128],
                    sem=f"l_w1{mpar}")
                self.wait("tensor", f"l_w1{mpar}", w1_ld)
                self.wait("tensor", "dve", self.xq_ready)
                self.wait("tensor", "dve", self.ffn_free[mpar])
                if mpar == 0:
                    self.wait("tensor", "dve", self.bc_free)
                else:
                    self.wait("tensor", "dve", self.bc2_free)
                for ch in range(cfg.NCH - 1):
                    self.opq("tensor", lambda e, ch=ch, mpar=mpar: e.matmul(
                        t["ffnps"][mpar], t["w1t"][mpar][:, ch * 128:(ch + 1) * 128],
                        t["xq"][:, ch * SW:(ch + 1) * SW],
                        start=(ch == 0), stop=False))
                self.op("tensor", "pe", lambda e, mpar=mpar: e.matmul(
                    t["ffnps"][mpar],
                    t["w1t"][mpar][:, (cfg.NCH - 1) * 128:cfg.NCH * 128],
                    t["xq"][:, (cfg.NCH - 1) * SW:cfg.NCH * SW],
                    start=False, stop=True))
                mm1_pe = self.cnt["pe"]
                self.w1t_free[mpar] = mm1_pe
                self.xq_free = mm1_pe
                self.wait("vector", "pe", mm1_pe)
                self.qh_tile(t["ffnps"][mpar], t["b1q"][:, hj:hj + 1],
                             t["hq"][:, hj * SW:(hj + 1) * SW], SW)
                self.ffn_free[mpar] = self.evac_mark
                if hj == cfg.NHJ - 1:
                    self.hq_ready = self.cnt["dve"]
            else:
                dj = idx
                mpar = dj % 2
                self.wait("sync", "pe", self.w2t_free[mpar])
                w2_ld = self.dma(
                    "sync", t["w2t3"][mpar],
                    t["w2qTr"][:, :, dj * 128:(dj + 1) * 128],
                    sem=f"l_w2{mpar}")
                self.wait("tensor", f"l_w2{mpar}", w2_ld)
                self.wait("tensor", "dve", self.hq_ready)
                self.wait("tensor", "dve", self.ffn_free[mpar])
                if mpar == 0:
                    self.wait("tensor", "dve", self.bc_free)
                else:
                    self.wait("tensor", "dve", self.bc2_free)
                for k in range(cfg.NHJ - 1):
                    self.opq("tensor", lambda e, k=k, mpar=mpar: e.matmul(
                        t["ffnps"][mpar], t["w2t"][mpar][:, k * 128:(k + 1) * 128],
                        t["hq"][:, k * SW:(k + 1) * SW],
                        start=(k == 0), stop=False))
                self.op("tensor", "pe", lambda e, mpar=mpar: e.matmul(
                    t["ffnps"][mpar],
                    t["w2t"][mpar][:, (cfg.NHJ - 1) * 128:cfg.NHJ * 128],
                    t["hq"][:, (cfg.NHJ - 1) * SW:cfg.NHJ * SW],
                    start=False, stop=True))
                mm2f_pe = self.cnt["pe"]
                self.w2t_free[mpar] = mm2f_pe
                self.hq_free = mm2f_pe
                self.wait("vector", "pe", mm2f_pe)
                self.wait("vector", "s_out", self.ych_store)
                self.qy_tile(t["ffnps"][mpar], t["b2q"][:, dj:dj + 1], t["ych"], SW)
                self.ffn_free[mpar] = self.evac_mark
                self.wait("sync", "dve", self.mkch_done)
                mk_ld = self.dma("sync", t["mkch"],
                                 t["mask_own"][dj * 128:(dj + 1) * 128,
                                               fs * SW:(fs + 1) * SW], sem="l_mk")
                self.wait("vector", "l_mk", mk_ld)
                self.vop(lambda e: e.tensor_tensor(
                    t["ych"], t["ych"], t["mkch"], ALU.mult))
                self.mkch_done = self.cnt["dve"]
                self.vop(lambda e, dj=dj, resid=resid: e.tensor_tensor(
                    t["ych"], t["ych"], resid[:, dj * SW:(dj + 1) * SW],
                    ALU.add))
                ych_dve = self.cnt["dve"]
                self.wait("gpsimd", "dve", ych_dve)
                self.ych_store = self.dma(
                    "gpsimd", t["out_fm"][dj * 128:(dj + 1) * 128,
                                          fs * SW:(fs + 1) * SW],
                    t["ych"], sem="s_out")

        units = [("hj", j) for j in range(cfg.NHJ)] + \
                [("dj", j) for j in range(cfg.NCH)]
        per_slot = (len(units) + cfg.H // 2 - 1) // (cfg.H // 2)

        for s in range(cfg.NSW + 1):
            if s < cfg.NSW:
                resid = t["resid"][s % 2]
                self.wait("sync", "s_tr0", p2_tr0)
                self.wait("sync", "s_tr1", p2_tr1)
                self.wait("sync", "s_kc0", p1_kc[0])
                self.wait("sync", "s_kc1", p1_kc[1])
                self.wait("sync", "s_vt0", p0_vt[0])
                self.wait("sync", "s_vt1", p0_vt[1])
                self.wait("sync", "pec", self.qf_free)
                for ch in range(cfg.NCH):
                    self.dma("sync", t["qf"][:, ch * SW:(ch + 1) * SW],
                             t["Kfm"][ch * 128:(ch + 1) * 128,
                                      s * SW:(s + 1) * SW], sem="l_qf")
                qf_ld = self.cnt["l_qf"]

                for p in range(cfg.H // 2):
                    par = p % 2
                    self.wait("sync", "pec", self.kch_free[par])
                    ldk = self.dma("sync", t["kch"][par],
                                   t["Kfm"][p * 128:(p + 1) * 128, :],
                                   sem=f"l_kc{par}")
                    prev = None
                    for kt in range(cfg.NKT + 1):
                        if kt < cfg.NKT:
                            kpar = kt % 2
                            vpar = kt % 4
                            self.wait("sync", "pe", self.vsl_free[vpar])
                            ldvs = self.dma(
                                "sync", t["vsl3"][vpar][:, :, 0:64],
                                t["XnG"][kt * 128:(kt + 1) * 128,
                                         2 * p:2 * p + 2, :],
                                sem=f"l_vs{vpar}")
                            if kt == 0:
                                self.wait("tensor", f"l_kc{par}", ldk)
                                self.wait("tensor", "l_qf", qf_ld)
                            self.wait("tensor", "act", self.mega_freeA[kpar])
                            self.opq("tensor", lambda e, p=p, kt=kt, kpar=kpar, par=par:
                                     e.matmul(t["megaAB"][kpar][:, 0:SW],
                                              t["kch"][par][0:64, kt * 128:(kt + 1) * 128],
                                              t["qf"][0:64, p * SW:(p + 1) * SW],
                                              start=True, stop=True, tile_position=(0, 0)))
                            self.op("tensor", "pec", lambda e, p=p, kt=kt, kpar=kpar, par=par:
                                    e.matmul(t["megaAB"][kpar][:, SW:2 * SW],
                                             t["kch"][par][64:128, kt * 128:(kt + 1) * 128],
                                             t["qf"][64:128, p * SW:(p + 1) * SW],
                                             start=True, stop=True, tile_position=(64, 0)))
                            corr_pe = self.cnt["pec"]
                            self.wait("scalar", "pec", corr_pe)
                            self.wait("scalar", "pe", self.et_free[kpar])
                            self.op("scalar", "act", lambda e, kpar=kpar: e.activation(
                                t["etAB"][kpar], t["megaAB"][kpar], AF.Exp,
                                bias=cp[:, 3:4], scale=1.0))
                            exp_act = self.cnt["act"]
                            self.mega_freeA[kpar] = exp_act
                            cur = (kt, kpar, vpar, exp_act, ldvs)
                        else:
                            cur = None
                        if prev is not None:
                            pkt, pkpar, pvpar, pexp, pldvs = prev
                            self.wait("tensor", "act", pexp)
                            self.wait("tensor", f"l_vs{pvpar}", pldvs)
                            if pkt == 0:
                                self.wait("tensor", "act", self.nd_free_act)
                                self.wait("tensor", "dve", self.nd_free_dve)
                            self.opq("tensor", lambda e, pkt=pkt, pkpar=pkpar,
                                     pvpar=pvpar: e.matmul(
                                t["ndA"], t["vsl"][pvpar][:, 0:65],
                                t["etAB"][pkpar][:, 0:SW],
                                start=(pkt == 0), stop=(pkt == cfg.NKT - 1)))
                            self.op("tensor", "pe", lambda e, pkt=pkt, pkpar=pkpar,
                                    pvpar=pvpar: e.matmul(
                                t["ndB"], t["vsl"][pvpar][:, 65:130],
                                t["etAB"][pkpar][:, SW:2 * SW],
                                start=(pkt == 0), stop=(pkt == cfg.NKT - 1)))
                            mm2_pe = self.cnt["pe"]
                            self.et_free[pkpar] = mm2_pe
                            self.vsl_free[pvpar] = mm2_pe
                        prev = cur
                    self.kch_free[par] = self.cnt["pec"]
                    nd_pe = self.cnt["pe"]
                    h0, h1 = 2 * p, 2 * p + 1
                    self.wait("scalar", "pe", nd_pe)
                    self.wait("scalar", "dve", self.attnN_free)
                    self.wait("scalar", "dve", self.dall_free)
                    self.op("scalar", "act", lambda e, p=p: e.copy(
                        t["attnN"][0:64, p * SW:(p + 1) * SW], t["ndA"][0:64, :]))
                    self.op("scalar", "act", lambda e, p=p: e.copy(
                        t["attnN"][64:128, p * SW:(p + 1) * SW], t["ndB"][0:64, :]))
                    self.op("scalar", "act", lambda e, h0=h0: e.copy(
                        t["Dall"][h0], t["ndA"][64:65, :]))
                    self.op("scalar", "act", lambda e, h1=h1: e.copy(
                        t["Dall"][h1], t["ndB"][64:65, :]))
                    self.nd_free_act = self.cnt["act"]
                    self.wait("vector", "act", self.nd_free_act)
                    self.wait("vector", "pe", self.drec_free)
                    self.vop(lambda e, h0=h0: e.reciprocal(t["Drec"][h0], t["Dall"][h0]))
                    self.vop(lambda e, h1=h1: e.reciprocal(t["Drec"][h1], t["Dall"][h1]))
                    self.dall_free = self.cnt["dve"]
                    self.wait("tensor", "dve", self.dall_free)
                    self.wait("tensor", "dve", self.ffn_free[0])
                    self.wait("tensor", "dve", self.bc_free)
                    self.opq("tensor", lambda e, h0=h0: e.matmul(
                        t["bcps"][0:64, :], t["onesf32c"][0:1, 0:64], t["Drec"][h0],
                        start=True, stop=True, tile_position=(0, 0)))
                    self.op("tensor", "pe", lambda e, h1=h1: e.matmul(
                        t["bcps"][64:128, :], t["onesf32c"][0:1, 0:64], t["Drec"][h1],
                        start=True, stop=True, tile_position=(0, 64)))
                    bc_pe = self.cnt["pe"]
                    self.drec_free = bc_pe
                    self.wait("sync", "dve", self.xoch_done)
                    xo_ld = self.dma("sync", t["xoch"],
                                     t["x_own"][p * 128:(p + 1) * 128,
                                                s * SW:(s + 1) * SW], sem="l_xo")
                    self.wait("vector", "pe", bc_pe)
                    self.wait("vector", "l_xo", xo_ld)
                    self.vop(lambda e, p=p, resid=resid: e.tensor_tensor(
                        resid[:, p * SW:(p + 1) * SW],
                        t["attnN"][:, p * SW:(p + 1) * SW], t["bcps"], ALU.mult))
                    self.bc_free = self.cnt["dve"]
                    self.vop(lambda e, p=p, resid=resid: e.tensor_tensor(
                        resid[:, p * SW:(p + 1) * SW],
                        resid[:, p * SW:(p + 1) * SW], t["xoch"], ALU.add))
                    self.xoch_done = self.cnt["dve"]
                    if s > 0:
                        for u in units[p * per_slot:(p + 1) * per_slot]:
                            emit_ffn_unit(s - 1, u[0], u[1])
            else:
                for u in units:
                    emit_ffn_unit(s - 1, u[0], u[1])

            if s < cfg.NSW:
                resid = t["resid"][s % 2]
                for ch in range(cfg.NCH):
                    self.vop(lambda e, ch=ch, resid=resid: e.tensor_tensor(
                        t["attnN"][:, ch * SW:(ch + 1) * SW],
                        resid[:, ch * SW:(ch + 1) * SW],
                        resid[:, ch * SW:(ch + 1) * SW], ALU.mult))
                sq_dve = self.cnt["dve"]
                self.wait("tensor", "dve", sq_dve)
                self.wait("tensor", "act", self.nd_free_act)
                self.wait("tensor", "dve", self.nd_free_dve)
                for ch in range(cfg.NCH):
                    if ch < cfg.NCH - 1:
                        self.opq("tensor", lambda e, ch=ch, resid=resid: e.matmul(
                            t["ndA"][0:1, :], t["onesf32"][:, 0:1],
                            resid[:, ch * SW:(ch + 1) * SW],
                            start=(ch == 0), stop=False))
                    else:
                        self.op("tensor", "pe", lambda e, ch=ch, resid=resid: e.matmul(
                            t["ndA"][0:1, :], t["onesf32"][:, 0:1],
                            resid[:, ch * SW:(ch + 1) * SW],
                            start=False, stop=True))
                for ch in range(cfg.NCH):
                    if ch < cfg.NCH - 1:
                        self.opq("tensor", lambda e, ch=ch: e.matmul(
                            t["ndB"][0:1, :], t["onesf32"][:, 0:1],
                            t["attnN"][:, ch * SW:(ch + 1) * SW],
                            start=(ch == 0), stop=False))
                    else:
                        self.op("tensor", "pe", lambda e, ch=ch: e.matmul(
                            t["ndB"][0:1, :], t["onesf32"][:, 0:1],
                            t["attnN"][:, ch * SW:(ch + 1) * SW],
                            start=False, stop=True))
                sums_pe = self.cnt["pe"]
                self.wait("vector", "pe", sums_pe)
                self.wait("vector", "pe", self.stats_free)
                self.vop(lambda e: e.tensor_scalar(
                    t["mrow"], t["ndA"][0:1, :], 1.0 / 1024.0, None, ALU.mult))
                self.vop(lambda e: e.tensor_tensor(
                    t["vrow"], t["mrow"], t["mrow"], ALU.mult))
                self.vop(lambda e: e.scalar_tensor_tensor(
                    t["vrow"], t["ndB"][0:1, :], 1.0 / 1024.0, t["vrow"],
                    ALU.mult, ALU.subtract))
                self.nd_free_dve = self.cnt["dve"]
                st_dve = self.cnt["dve"]
                self.wait("scalar", "dve", st_dve)
                self.op("scalar", "act", lambda e: e.activation(
                    t["vrow"], t["vrow"], AF.Ln, bias=cp[0:1, 1:2], scale=1.0))
                self.wait("scalar", "act", self.cnt["act"])
                self.op("scalar", "act", lambda e: e.activation(
                    t["vrow"], t["vrow"], AF.Exp, bias=0.0, scale=cp[0:1, 2:3]))
                self.wait("scalar", "act", self.cnt["act"])
                stat_act = self.cnt["act"]
                self.wait("tensor", "act", stat_act)
                self.wait("tensor", "dve", self.bc_free)
                self.wait("tensor", "dve", self.ffn_free[0])
                self.op("tensor", "pe", lambda e: e.matmul(
                    t["bcps"], t["onesf32c"], t["mrow"], start=True, stop=True))
                m_pe = self.cnt["pe"]
                self.stats_free = m_pe
                self.wait("vector", "pe", m_pe)
                self.wait("vector", "pe", self.xq_free)
                for ch in range(cfg.NCH):
                    self.vop(lambda e, ch=ch, resid=resid: e.tensor_tensor(
                        t["attnN"][:, ch * SW:(ch + 1) * SW],
                        resid[:, ch * SW:(ch + 1) * SW], t["bcps"], ALU.subtract))
                self.bc_free = self.cnt["dve"]
                self.wait("tensor", "dve", self.bc2_free)
                self.wait("tensor", "dve", self.ffn_free[1])
                self.op("tensor", "pe", lambda e: e.matmul(
                    t["bcps2"], t["onesf32c"], t["vrow"], start=True, stop=True))
                bc2_pe = self.cnt["pe"]
                self.stats_free = bc2_pe
                self.wait("vector", "pe", bc2_pe)
                for ch in range(cfg.NCH):
                    self.vop(lambda e, ch=ch: e.tensor_tensor(
                        t["attnN"][:, ch * SW:(ch + 1) * SW],
                        t["attnN"][:, ch * SW:(ch + 1) * SW], t["bcps2"], ALU.mult))
                    self.bc2_free = self.cnt["dve"]
                    self.q43_tile(t["attnN"][:, ch * SW:(ch + 1) * SW],
                                  t["xq"][:, ch * SW:(ch + 1) * SW], SW, 4)
                self.xq_ready = self.cnt["dve"]
                self.attnN_free = self.cnt["dve"]
                self.qf_free = self.cnt["pec"]
        self.wait("gpsimd", "s_out", self.cnt["s_out"])


def build_program(cfg):
    nc = bass.Bass("TRN2", target_bir_lowering=False, debug=False)
    T, D, HF, SW = cfg.T, cfg.D, cfg.HF, cfg.SW
    t = {}
    t["x_tok"] = nc.dram_tensor("x_tok", [T, D], F32, kind="ExternalInput").ap()
    t["x_own"] = nc.dram_tensor("x_own", [D, cfg.OWN], F32, kind="ExternalInput").ap()
    t["mask_own"] = nc.dram_tensor("mask_own", [D, cfg.OWN], F32,
                                   kind="ExternalInput").ap()
    t["w1"] = nc.dram_tensor("w1", [HF, D], F32, kind="ExternalInput").ap()
    t["w2"] = nc.dram_tensor("w2", [D, HF], F32, kind="ExternalInput").ap()
    t["b1f"] = nc.dram_tensor("b1f", [128, HF // 128], F32, kind="ExternalInput").ap()
    t["b2f"] = nc.dram_tensor("b2f", [128, D // 128], F32, kind="ExternalInput").ap()
    t["identin"] = nc.dram_tensor("identin", [128, 128], BF16,
                                  kind="ExternalInput").ap()
    t["out_fm"] = nc.dram_tensor("out_fm", [D, cfg.OWN], F32,
                                 kind="ExternalOutput").ap()
    t["Xn"] = nc.dram_tensor("Xn", [T, D], BF16, kind="Internal").ap()
    t["XnG"] = t["Xn"].rearrange("t (g c) -> t g c", g=D // 64, c=64)
    t["Kfm"] = nc.dram_tensor("Kfm", [D, T], BF16, kind="Internal").ap()
    t["w1qn"] = nc.dram_tensor("w1qn", [HF, D], BF16, kind="Internal").ap()
    t["w2qn"] = nc.dram_tensor("w2qn", [D, HF], BF16, kind="Internal").ap()
    t["w1qT"] = nc.dram_tensor("w1qT", [D, HF], BF16, kind="Internal").ap()
    t["w2qT"] = nc.dram_tensor("w2qT", [HF, D], BF16, kind="Internal").ap()
    t["w1qTr"] = t["w1qT"].rearrange("(c r) h -> r c h", c=D // 128, r=128)
    t["w2qTr"] = t["w2qT"].rearrange("(c r) d -> r c d", c=HF // 128, r=128)

    with contextlib.ExitStack() as es:
        def sb(name, shape, dtype):
            return es.enter_context(nc.sbuf_tensor(name, shape, dtype))[:]

        t["cpool"] = sb("cpool", [128, len(CPOOL_VALS)], F32)
        t["ident"] = sb("ident", [128, 128], BF16)
        t["onesbf"] = sb("onesbf", [1, 128], BF16)
        t["onesf32"] = sb("onesf32", [128, 1], F32)
        t["onesf32c"] = sb("onesf32c", [1, 128], F32)
        t["xbuf"] = [sb(f"xbuf{i}", [128, D], F32) for i in range(2)]
        t["s1"] = sb("s1", [128, 1], F32)
        t["s2"] = sb("s2", [128, 1], F32)
        t["s3"] = sb("s3", [128, 1], F32)
        xnch = [es.enter_context(nc.sbuf_tensor(f"xnc{i}", [128, D], BF16))
                for i in range(2)]
        t["xnc"] = [h[:] for h in xnch]
        t["xnc3"] = [h.ap().rearrange("p (h c) -> p h c", h=cfg.H, c=64)
                     for h in xnch]
        t["kch"] = [sb(f"kch{i}", [128, T], BF16) for i in range(2)]
        t["b1q"] = sb("b1q", [128, HF // 128], F32)
        t["b2q"] = sb("b2q", [128, D // 128], F32)
        for nm in ("qa", "qb", "qc", "qd", "qe"):
            t[nm] = sb(nm, [128, SW], F32)
        t["qi"] = sb("qi", [128, SW], I32)
        t["qf"] = sb("qfq", [128, (D // 128) * SW], BF16)
        vslh = [es.enter_context(nc.sbuf_tensor(f"vsl{i}", [128, 130], BF16))
                for i in range(4)]
        t["vsl"] = [h[:] for h in vslh]
        t["vsl3"] = [h.ap().rearrange("p (j c) -> p j c", j=2, c=65)
                     for h in vslh]
        t["etAB"] = [sb(f"etAB{i}", [128, 2 * SW], BF16) for i in range(2)]
        t["attnN"] = sb("attnN", [128, cfg.NCH * SW], F32)
        t["attnR"] = sb("attnR", [128, cfg.NCH * SW], F32)
        t["sqch"] = sb("sqch", [128, cfg.NCH * SW], F32)
        t["xq"] = sb("xq", [128, cfg.NCH * SW], BF16)
        t["hq"] = sb("hq", [128, max(cfg.NHJ * SW, HF)], BF16)
        t["wq"] = t["hq"][:, 0:HF]
        dall = sb("Dall", [1, 2 * SW], F32)
        drec = sb("Drec", [1, 2 * SW], F32)
        t["Dall"] = [dall[:, (h % 2) * SW:(h % 2 + 1) * SW] for h in range(cfg.H)]
        t["Drec"] = [drec[:, (h % 2) * SW:(h % 2 + 1) * SW] for h in range(cfg.H)]
        t["mrow"] = sb("mrow", [1, SW], F32)
        t["vrow"] = sb("vrow", [1, SW], F32)
        t["xoch"] = sb("xoch", [128, SW], F32)
        t["mkch"] = sb("mkch", [128, SW], F32)
        t["ych"] = sb("ych", [128, SW], F32)
        w1th = [es.enter_context(nc.sbuf_tensor(f"w1t{i}", [128, D], BF16))
                for i in range(2)]
        t["w1t"] = [h[:] for h in w1th]
        t["w1t3"] = [h.ap().rearrange("p (c k) -> p c k", c=cfg.NCH, k=128)
                     for h in w1th]
        w2th = [es.enter_context(nc.sbuf_tensor(f"w2t{i}", [128, HF], BF16))
                for i in range(2)]
        t["w2t"] = [h[:] for h in w2th]
        t["w2t3"] = [h.ap().rearrange("p (c k) -> p c k", c=cfg.NHJ, k=128)
                     for h in w2th]
        t["sq1"] = t["attnN"][:, 0:D]
        t["resid"] = [t["attnR"], t["sqch"]]
        if cfg.NCH * SW >= HF:
            t["wbuf"] = [t["attnR"][:, 0:HF], t["sqch"][:, 0:HF]]
        else:
            t["wbuf"] = [sb(f"wbuf{i}", [128, HF], F32) for i in range(2)]

        t["megaAB"] = [es.enter_context(
            nc.psum_tensor(f"megaAB{i}", [128, 2 * SW], F32))[:] for i in range(2)]
        t["ndA"] = es.enter_context(nc.psum_tensor("ndA", [65, SW], F32))[:]
        t["ndB"] = es.enter_context(nc.psum_tensor("ndB", [65, SW], F32))[:]
        t["ffnps"] = [es.enter_context(nc.psum_tensor(f"ffnps{i}", [128, SW], F32))[:]
                      for i in range(2)]
        t["bcps"] = t["ffnps"][0]
        t["bcps2"] = t["ffnps"][1]

        sems = {}
        semnames = ["gp", "dve", "act", "pe", "pec", "l_id", "l_b", "l_qf", "l_xo",
                    "l_mk", "s_out"]
        for i in range(2):
            semnames += [f"l_xb{i}", f"l_kc{i}", f"l_wb{i}", f"l_vs{i}",
                         f"l_w1{i}", f"l_w2{i}", f"s_vt{i}", f"s_tr{i}",
                         f"s_kc{i}", f"l_vs{i + 2}", f"s_nat{i}"]
        for nm in semnames:
            sems[nm] = es.enter_context(nc.semaphore(nm))
        t["sems"] = sems

        with nc.Block() as blk:
            @blk.sync
            def _(sync):
                P(nc, cfg, t, "sync", sync).emit()

            @blk.gpsimd
            def _(g):
                P(nc, cfg, t, "gpsimd", g).emit()

            @blk.vector
            def _(v):
                P(nc, cfg, t, "vector", v).emit()

            @blk.scalar
            def _(sc):
                P(nc, cfg, t, "scalar", sc).emit()

            @blk.tensor
            def _(te):
                P(nc, cfg, t, "tensor", te).emit()
    return nc




def make_in_maps(x, w1, b1, w2, b2, cfg, n_cores=8):
    B, T, D = x.shape
    keep = dropout_keep_mask((B, T, D))
    mask = np.where(keep, np.float32(1.25), np.float32(0.0))
    ident = np.eye(128).astype(bf16)
    b1f = np.ascontiguousarray(b1.reshape(cfg.NHJ, 128).T)
    b2f = np.ascontiguousarray(b2.reshape(cfg.NCH, 128).T)
    per_batch = n_cores // B
    in_maps = []
    for c in range(n_cores):
        b = c // per_batch
        qlo = (c % per_batch) * cfg.OWN
        in_maps.append({
            "x_tok": np.ascontiguousarray(np.roll(x[b], -qlo, axis=0)),
            "x_own": np.ascontiguousarray(x[b, qlo:qlo + cfg.OWN].T),
            "mask_own": np.ascontiguousarray(mask[b, qlo:qlo + cfg.OWN].T),
            "w1": w1, "w2": w2, "b1f": b1f, "b2f": b2f,
            "identin": ident,
        })
    return in_maps


def kernel(L0_attn_x, L0_ffn_w1, L0_ffn_b1, L0_ffn_w2, L0_ffn_b2):
    x = np.ascontiguousarray(np.asarray(L0_attn_x, np.float32))
    w1 = np.ascontiguousarray(np.asarray(L0_ffn_w1, np.float32))
    b1 = np.asarray(L0_ffn_b1, np.float32)
    w2 = np.ascontiguousarray(np.asarray(L0_ffn_w2, np.float32))
    b2 = np.asarray(L0_ffn_b2, np.float32)
    B, T, D = x.shape
    cfg = Cfg(T=T, OWN=T * B // 8, SW=512)
    nc = build_program(cfg)
    in_maps = make_in_maps(x, w1, b1, w2, b2, cfg)
    res = run_bass_kernel_spmd(nc, in_maps, core_ids=list(range(8)))
    out = np.empty((B, T, D), np.float32)
    per_batch = 8 // B
    for c in range(8):
        b = c // per_batch
        qlo = (c % per_batch) * cfg.OWN
        out[b, qlo:qlo + cfg.OWN] = np.asarray(res.results[c]["out_fm"],
                                               np.float32).T
    return out


# version 30
# speedup vs baseline: 1.2061x; 1.0007x over previous
import contextlib
import numpy as np
import ml_dtypes
import concourse.bass as bass
import concourse.mybir as mybir
from concourse.bass_utils import run_bass_kernel_spmd

F32 = mybir.dt.float32
BF16 = mybir.dt.bfloat16
I32 = mybir.dt.int32
AF = mybir.ActivationFunctionType
ALU = mybir.AluOpType
LN2 = float(np.log(2.0))
bf16 = ml_dtypes.bfloat16



def _threefry2x32(k0, k1, x0, x1):
    R0 = [13, 15, 26, 6]
    R1 = [17, 29, 16, 24]
    ks0 = np.uint32(k0)
    ks1 = np.uint32(k1)
    ks2 = np.uint32(ks0 ^ ks1 ^ np.uint32(0x1BD11BDA))
    ks = [ks0, ks1, ks2]
    x0 = (x0 + ks[0]).astype(np.uint32)
    x1 = (x1 + ks[1]).astype(np.uint32)
    for i, rots in enumerate([R0, R1, R0, R1, R0]):
        for r in rots:
            x0 = (x0 + x1).astype(np.uint32)
            x1 = ((x1 << np.uint32(r)) | (x1 >> np.uint32(32 - r))).astype(np.uint32)
            x1 = (x1 ^ x0).astype(np.uint32)
        x0 = (x0 + ks[(i + 1) % 3]).astype(np.uint32)
        x1 = (x1 + ks[(i + 2) % 3] + np.uint32(i + 1)).astype(np.uint32)
    return x0, x1


def dropout_keep_mask(shape, key=42, p_keep=0.8):
    n = int(np.prod(shape))
    y0, y1 = _threefry2x32(0, key, np.zeros(n, np.uint32),
                           np.arange(n, dtype=np.uint32))
    bits = (y0 ^ y1).astype(np.uint32)
    f = ((bits >> np.uint32(9)) | np.uint32(0x3F800000)).view(np.float32) \
        - np.float32(1.0)
    return (f < np.float32(p_keep)).reshape(shape)




class Cfg:
    def __init__(self, T=8192, OWN=2048, SW=512):
        self.T = T
        self.OWN = OWN
        self.SW = SW
        self.D = 1024
        self.H = 16
        self.HD = 64
        self.HF = 4096
        self.NSW = OWN // SW
        self.NKT = T // 128
        self.NTT = T // 128
        self.NCH = self.D // 128
        self.NHJ = self.HF // 128


CPOOL_VALS = [
    1e-38,
    1e-5,
    -0.5,
    -80.0,
    -LN2,
    LN2,
    9.0 * LN2,
    -9.0 * LN2,
    24.0 * LN2,
    -24.0 * LN2,
]


class P:

    def __init__(self, nc, cfg, t, eng_name, eng):
        self.nc = nc
        self.cfg = cfg
        self.t = t
        self.eng_name = eng_name
        self.eng = eng
        self.sems = t["sems"]
        self.cnt = {k: 0 for k in self.sems}
        self.tr_evac = [0, 0]
        self.tr_store = [0, 0]
        self.vt_store = [0, 0]
        self.s1_act = 0
        self.natst = [0, 0]
        self.xb_done = [0, 0]
        self.kb_done = [0, 0]
        self.kch_free = [0, 0]
        self.vsl_free = [0, 0, 0, 0]
        self.mega_freeA = [0, 0]
        self.mega_freeB = [0, 0]
        self.et_free = [0, 0]
        self.nd_free = 0
        self.bc_free = 0
        self.bc2_free = 0
        self.mmps_free = [0, 0]
        self.w1t_free = [0, 0]
        self.w2t_free = [0, 0]
        self.qf_free = 0
        self.wbuf_done = [0, 0]
        self.ych_store = 0
        self.xoch_done = 0
        self.mkch_done = 0
        self.xnc_free = [0, 0]
        self.drec_free = 0
        self.stats_free = 0
        self.xq_free = 0
        self.hq_free = 0
        self.ffn_free = [0, 0]
        self.evac_mark = 0
        self.nd_free_act = 0
        self.nd_free_dve = 0
        self.attnN_free = 0
        self.dall_free = 0
        self.xq_ready = 0
        self.hq_ready = 0

    def on(self, name):
        return self.eng_name == name

    def bump(self, sem, n):
        self.cnt[sem] += n
        return self.cnt[sem]

    def wait(self, engname, sem, val):
        if self.on(engname) and val > 0:
            self.eng.wait_ge(self.sems[sem], val)

    def dma(self, q, out, in_, sem="ld"):
        v = self.bump(sem, 16)
        if self.on(q):
            self.eng.dma_start(out=out, in_=in_).then_inc(self.sems[sem], 16)
        return v

    def op(self, engname, sem, fn):
        v = self.bump(sem, 1)
        if self.on(engname):
            fn(self.eng).then_inc(self.sems[sem], 1)
        return v

    def opq(self, engname, fn):
        if self.on(engname):
            fn(self.eng)

    def vop(self, fn):
        self.op("vector", "dve", fn)
        self.wait("vector", "dve", self.cnt["dve"])
        return self.cnt["dve"]

    def ladder(self, z_ap, b_ap, nlev):
        for k in reversed(range(nlev)):
            c = float(2.0 ** k)
            self.vop(lambda e, c=c: e.tensor_scalar(b_ap, z_ap, c, None, ALU.is_ge))
            self.vop(lambda e, c=c: e.scalar_tensor_tensor(
                z_ap, b_ap, -c, z_ap, ALU.mult, ALU.add))

    def q43_tile(self, v_ap, out_ap, W, elev, bufs=None):
        t = self.t
        if bufs is None:
            qa = t["qa"][:, 0:W]
            qb = t["qb"][:, 0:W]
            qc = t["qc"][:, 0:W]
            qd = t["qd"][:, 0:W]
        else:
            qa, qb, qc, qd = (b[:, 0:W] for b in bufs)
        cp = t["cpool"]
        self.wait("scalar", "dve", self.cnt["dve"])
        self.op("scalar", "act", lambda e: e.activation(
            qa, v_ap, AF.Square, bias=0.0, scale=1.0))
        self.wait("scalar", "act", self.cnt["act"])
        self.op("scalar", "act", lambda e: e.activation(
            qb, qa, AF.Ln, bias=cp[:, 0:1], scale=1.0))
        self.wait("scalar", "act", self.cnt["act"])
        av = self.cnt["act"]
        self.wait("vector", "act", av)
        self.vop(lambda e: e.tensor_scalar(qb, qb, 0.5 / LN2, 6.0, ALU.mult, ALU.add))
        self.vop(lambda e: e.tensor_scalar(
            qb, qb, 0.0, float(2.0 ** elev) - 1e-3, ALU.max, ALU.min))
        self.vop(lambda e: e.tensor_copy(qc, qb))
        self.ladder(qb, qd, elev)
        self.vop(lambda e: e.tensor_tensor(qc, qc, qb, ALU.subtract))
        dv = self.cnt["dve"]
        self.wait("scalar", "dve", dv)
        self.op("scalar", "act", lambda e: e.activation(
            qb, qc, AF.Exp, bias=cp[:, 6:7], scale=cp[:, 4:5]))
        self.op("scalar", "act", lambda e: e.activation(
            qd, qc, AF.Exp, bias=cp[:, 7:8], scale=cp[:, 5:6]))
        av = self.cnt["act"]
        self.wait("vector", "act", av)
        self.vop(lambda e: e.tensor_tensor(qa, v_ap, qb, ALU.mult))
        self.vop(lambda e: e.tensor_scalar(qa, qa, 16.5, None, ALU.add))
        self.vop(lambda e: e.tensor_copy(qc, qa))
        self.ladder(qa, qb, 6)
        self.vop(lambda e: e.tensor_tensor(qc, qc, qa, ALU.subtract))
        self.vop(lambda e: e.scalar_tensor_tensor(
            out_ap, qc, -16.0, qd, ALU.add, ALU.mult))
        return self.cnt["dve"]

    def qh_tile(self, psum_ap, b1col, out_ap, W):
        t = self.t
        qa = t["qa"][:, 0:W]
        qb = t["qb"][:, 0:W]
        qc = t["qc"][:, 0:W]
        qd = t["qd"][:, 0:W]
        qe = t["qe"][:, 0:W]
        cp = t["cpool"]
        self.vop(lambda e: e.tensor_scalar(
            qa, psum_ap, b1col, 1e-30, ALU.add, ALU.max))
        self.evac_mark = self.cnt["dve"]
        dv = self.cnt["dve"]
        self.wait("scalar", "dve", dv)
        self.op("scalar", "act", lambda e: e.activation(
            qb, qa, AF.Ln, bias=0.0, scale=1.0))
        self.wait("scalar", "act", self.cnt["act"])
        av = self.cnt["act"]
        self.wait("vector", "act", av)
        self.vop(lambda e: e.tensor_scalar(qb, qb, 1.0 / LN2, 6.0, ALU.mult, ALU.add))
        self.vop(lambda e: e.tensor_scalar(qb, qb, 0.0, 12.0 - 1e-3, ALU.max, ALU.min))
        self.vop(lambda e: e.tensor_copy(qc, qb))
        self.ladder(qb, qd, 4)
        self.vop(lambda e: e.tensor_tensor(qc, qc, qb, ALU.subtract))
        dv = self.cnt["dve"]
        self.wait("scalar", "dve", dv)
        self.op("scalar", "act", lambda e: e.activation(
            qb, qc, AF.Exp, bias=cp[:, 6:7], scale=cp[:, 4:5]))
        self.op("scalar", "act", lambda e: e.activation(
            qd, qc, AF.Exp, bias=cp[:, 7:8], scale=cp[:, 5:6]))
        av = self.cnt["act"]
        self.wait("vector", "act", av)
        self.vop(lambda e: e.tensor_tensor(qa, qa, qb, ALU.mult))
        self.vop(lambda e: e.tensor_copy(qc, qa))
        self.ladder(qa, qe, 4)
        self.vop(lambda e: e.tensor_tensor(qc, qc, qa, ALU.subtract))
        self.wait("gpsimd", "dve", self.cnt["dve"])
        gpm = self.op("gpsimd", "gp", lambda e: e.tensor_tensor(
            qe, qc, qe, ALU.add))
        self.vop(lambda e: e.tensor_scalar(qa, qa, -0.5, None, ALU.add))
        self.vop(lambda e: e.scalar_tensor_tensor(
            qb, qa, 0.0, qc, ALU.is_ge, ALU.add))
        self.vop(lambda e: e.tensor_tensor(qa, qa, qa, ALU.mult))
        self.vop(lambda e: e.tensor_scalar(
            qa, qa, float(2.0 ** -16 * 1.002), None, ALU.is_le))
        self.wait("vector", "gp", gpm)
        self.vop(lambda e: e.tensor_tensor(qc, qe, qb, ALU.subtract))
        self.vop(lambda e: e.tensor_tensor(qc, qa, qc, ALU.mult))
        self.vop(lambda e: e.tensor_tensor(qb, qb, qc, ALU.add))
        self.vop(lambda e: e.tensor_tensor(out_ap, qb, qd, ALU.mult))
        return self.cnt["dve"]

    def qy_tile(self, psum_ap, b2col, out_ap, W):
        t = self.t
        qa = t["qa"][:, 0:W]
        qb = t["qb"][:, 0:W]
        qc = t["qc"][:, 0:W]
        qd = t["qd"][:, 0:W]
        qi = t["qi"][:, 0:W]
        cp = t["cpool"]
        self.vop(lambda e: e.tensor_scalar(qa, psum_ap, b2col, None, ALU.add))
        self.evac_mark = self.cnt["dve"]
        self.wait("scalar", "dve", self.cnt["dve"])
        self.op("scalar", "act", lambda e: e.activation(
            qb, qa, AF.Square, bias=0.0, scale=1.0))
        self.wait("scalar", "act", self.cnt["act"])
        self.op("scalar", "act", lambda e: e.activation(
            qb, qb, AF.Ln, bias=cp[:, 0:1], scale=1.0))
        self.wait("scalar", "act", self.cnt["act"])
        av = self.cnt["act"]
        self.wait("vector", "act", av)
        self.vop(lambda e: e.tensor_scalar(qb, qb, 0.5 / LN2, 14.0, ALU.mult, ALU.add))
        self.vop(lambda e: e.tensor_scalar(qb, qb, 0.0, 20.0 - 1e-3, ALU.max, ALU.min))
        self.vop(lambda e: e.tensor_copy(qc, qb))
        self.ladder(qb, qd, 5)
        self.vop(lambda e: e.tensor_tensor(qc, qc, qb, ALU.subtract))
        dv = self.cnt["dve"]
        self.wait("scalar", "dve", dv)
        self.op("scalar", "act", lambda e: e.activation(
            qb, qc, AF.Exp, bias=cp[:, 8:9], scale=cp[:, 4:5]))
        self.op("scalar", "act", lambda e: e.activation(
            qd, qc, AF.Exp, bias=cp[:, 9:10], scale=cp[:, 5:6]))
        av = self.cnt["act"]
        self.wait("vector", "act", av)
        self.vop(lambda e: e.tensor_tensor(qa, qa, qb, ALU.mult))
        self.vop(lambda e: e.tensor_copy(qi, qa))
        self.vop(lambda e: e.tensor_copy(qa, qi))
        self.vop(lambda e: e.tensor_tensor(out_ap, qa, qd, ALU.mult))
        return self.cnt["dve"]

    def emit(self):
        cfg = self.cfg
        t = self.t
        cp = t["cpool"]
        SW = cfg.SW

        id_ld = self.dma("sync", t["ident"], t["identin"], sem="l_id")
        for j, val in enumerate(CPOOL_VALS):
            self.op("gpsimd", "gp", lambda e, j=j, val=val: e.memset(
                cp[:, j:j + 1], val))
        for par in range(4):
            self.op("gpsimd", "gp", lambda e, par=par: e.memset(
                t["vsl"][par][:, 64:65], 1.0))
            self.op("gpsimd", "gp", lambda e, par=par: e.memset(
                t["vsl"][par][:, 129:130], 1.0))
        self.op("gpsimd", "gp", lambda e: e.memset(t["onesbf"], 1.0))
        self.op("gpsimd", "gp", lambda e: e.memset(t["onesf32"], 1.0))
        self.op("gpsimd", "gp", lambda e: e.memset(t["onesf32c"], 1.0))
        gp_init = self.cnt["gp"]
        for en in ("vector", "scalar", "tensor"):
            self.wait(en, "gp", gp_init)
        self.wait("tensor", "l_id", id_ld)

        if cfg.NCH * SW >= 4096:
            WQP = 1024
            wbufs_p = (t["attnN"][:, 0:WQP], t["attnN"][:, WQP:2 * WQP],
                       t["attnN"][:, 2 * WQP:3 * WQP],
                       t["attnN"][:, 3 * WQP:4 * WQP])
        else:
            WQP = SW
            wbufs_p = None

        def wj_unit(wsrc, nat, j, ncols):
            par = j % 2
            dbl = 2 * ncols <= max(cfg.NHJ * SW, cfg.HF)
            wqs = t["hq"][:, (j % 2) * ncols:(j % 2 + 1) * ncols] if dbl                 else t["hq"][:, 0:ncols]
            wb = t["wbuf"][par][:, 0:ncols]
            self.wait("sync", "dve", self.wbuf_done[par])
            ldv = self.dma("sync", wb, wsrc[j * 128:(j + 1) * 128, :],
                           sem=f"l_wb{par}")
            self.wait("vector", f"l_wb{par}", ldv)
            self.wait("scalar", f"l_wb{par}", ldv)
            self.wait("vector", f"s_nat{par}", self.natst[par])
            if not dbl:
                self.wait("vector", f"s_nat{1 - par}", self.natst[1 - par])
            for sl in range(ncols // WQP):
                self.q43_tile(wb[:, sl * WQP:(sl + 1) * WQP],
                              wqs[:, sl * WQP:(sl + 1) * WQP], WQP, 3,
                              bufs=wbufs_p)
            self.wbuf_done[par] = self.cnt["dve"]
            self.wait("gpsimd", "dve", self.wbuf_done[par])
            self.natst[par] = self.dma(
                "gpsimd", nat[j * 128:(j + 1) * 128, :],
                wqs, sem=f"s_nat{par}")


        for tt in range(cfg.NTT):
            par = tt % 2
            xbuf = t["xbuf"][par]
            self.wait("sync", "dve", self.xb_done[par])
            ldv = self.dma("sync", xbuf, t["x_tok"][tt * 128:(tt + 1) * 128, :],
                           sem=f"l_xb{par}")
            self.wait("vector", f"l_xb{par}", ldv)
            self.wait("scalar", f"l_xb{par}", ldv)
            self.wait("scalar", "dve", self.cnt["dve"])
            self.op("scalar", "act", lambda e: e.activation(
                t["sq1"], xbuf, AF.Square, bias=0.0, scale=1.0))
            sq_act = self.cnt["act"]
            self.vop(lambda e: e.reduce_sum(t["s1"], xbuf,
                                            axis=mybir.AxisListType.X))
            self.wait("vector", "act", sq_act)
            self.vop(lambda e: e.reduce_sum(t["s2"], t["sq1"],
                                            axis=mybir.AxisListType.X))
            self.vop(lambda e: e.tensor_scalar(
                t["s1"], t["s1"], 1.0 / 1024.0, None, ALU.mult))
            self.vop(lambda e: e.tensor_tensor(t["s3"], t["s1"], t["s1"], ALU.mult))
            self.vop(lambda e: e.scalar_tensor_tensor(
                t["s2"], t["s2"], 1.0 / 1024.0, t["s3"], ALU.mult, ALU.subtract))
            dvv = self.cnt["dve"]
            self.wait("scalar", "dve", dvv)
            self.op("scalar", "act", lambda e: e.activation(
                t["s2"], t["s2"], AF.Ln, bias=cp[:, 1:2], scale=1.0))
            self.wait("scalar", "act", self.cnt["act"])
            self.op("scalar", "act", lambda e: e.activation(
                t["s2"], t["s2"], AF.Exp, bias=0.0, scale=cp[:, 2:3]))
            self.wait("scalar", "act", self.cnt["act"])
            avv = self.cnt["act"]
            self.wait("vector", "act", avv)
            self.wait("vector", f"s_vt{par}", self.vt_store[par])
            self.vop(lambda e, par=par: e.tensor_scalar(
                t["xnc"][par], xbuf, t["s1"][:, 0:1], t["s2"][:, 0:1],
                ALU.subtract, ALU.mult))
            self.xb_done[par] = self.cnt["dve"]
            xnc_dve = self.cnt["dve"]
            self.wait("gpsimd", "dve", xnc_dve)
            self.vt_store[par] = self.dma(
                "gpsimd", t["Xn"][tt * 128:(tt + 1) * 128, :],
                t["xnc"][par], sem=f"s_vt{par}")
        p0_vt = [self.cnt["s_vt0"], self.cnt["s_vt1"]]
        for ch in range(cfg.NCH):
            par = ch % 2
            self.wait("sync", "s_vt0", p0_vt[0])
            self.wait("sync", "s_vt1", p0_vt[1])
            self.wait("sync", f"s_kc{par}", self.kb_done[par])
            if self.on("sync"):
                self.eng.dma_start_transpose(
                    t["kch"][par], t["Xn"][:, ch * 128:(ch + 1) * 128]
                ).then_inc(self.sems[f"l_kc{par}"], 16)
            ldv = self.bump(f"l_kc{par}", 16)
            self.wait("vector", f"l_kc{par}", ldv)
            self.wait("vector", "act", self.s1_act)
            self.vop(lambda e, par=par: e.reduce_sum(
                t["s1"], t["kch"][par], axis=mybir.AxisListType.X))
            self.vop(lambda e: e.tensor_scalar(
                t["s1"], t["s1"], -1.0 / float(cfg.T), None, ALU.mult))
            kdv = self.cnt["dve"]
            self.wait("scalar", f"l_kc{par}", ldv)
            self.wait("scalar", "dve", kdv)
            self.op("scalar", "act", lambda e, par=par: e.activation(
                t["kch"][par], t["kch"][par], AF.Identity,
                bias=t["s1"][:, 0:1], scale=1.0))
            kca = self.cnt["act"]
            self.s1_act = kca
            self.wait("gpsimd", "act", kca)
            self.kb_done[par] = self.dma(
                "gpsimd", t["Kfm"][ch * 128:(ch + 1) * 128, :], t["kch"][par],
                sem=f"s_kc{par}")
            wj_unit(t["w1"], t["w1qn"], 2 * ch, cfg.D)
            wj_unit(t["w1"], t["w1qn"], 2 * ch + 1, cfg.D)
        p1_kc = [self.cnt["s_kc0"], self.cnt["s_kc1"]]
        self.dma("sync", t["b1q"], t["b1f"], sem="l_b")
        ldb = self.dma("sync", t["b2q"], t["b2f"], sem="l_b")
        self.wait("vector", "l_b", ldb)
        self.wait("scalar", "l_b", ldb)
        self.q43_tile(t["b1q"], t["b1q"], cfg.NHJ, 3)
        self.q43_tile(t["b2q"], t["b2q"], cfg.NCH, 3)

        def w_prep(wsrc, nat, dst, nrow_tiles, ncols, stages, stage_sems,
                   j0=0):
            for j in range(j0, nrow_tiles):
                wj_unit(wsrc, nat, j, ncols)
            nat_done = (self.cnt["s_nat0"], self.cnt["s_nat1"])
            for ch in range(ncols // 128):
                sp = ch % 2
                self.wait("sync", "s_nat0", nat_done[0])
                self.wait("sync", "s_nat1", nat_done[1])
                self.wait("sync", f"s_tr{sp}", self.tr_store[sp])
                if self.on("sync"):
                    self.eng.dma_start_transpose(
                        stages[sp][:, 0:nrow_tiles * 128],
                        nat[:, ch * 128:(ch + 1) * 128]
                    ).then_inc(self.sems[stage_sems[sp]], 16)
                sld = self.bump(stage_sems[sp], 16)
                self.wait("gpsimd", stage_sems[sp], sld)
                self.tr_store[sp] = self.dma(
                    "gpsimd", dst[ch * 128:(ch + 1) * 128, :],
                    stages[sp][:, 0:nrow_tiles * 128], sem=f"s_tr{sp}")

        w_prep(t["w1"], t["w1qn"], t["w1qT"], cfg.NHJ, cfg.D,
               (t["w2t"][0], t["w2t"][1]), ("l_w20", "l_w21"),
               j0=2 * cfg.NCH)
        w_prep(t["w2"], t["w2qn"], t["w2qT"], cfg.NCH, cfg.HF,
               (t["w1t"][0], t["w1t"][1]), ("l_w10", "l_w11"))
        p2_nat = (self.cnt["s_nat0"], self.cnt["s_nat1"])
        p2_tr0 = self.tr_store[0]
        p2_tr1 = self.tr_store[1]
        self.wait("vector", "s_nat0", p2_nat[0])
        self.wait("vector", "s_nat1", p2_nat[1])
        self.wait("vector", "s_tr0", p2_tr0)
        self.wait("vector", "s_tr1", p2_tr1)

        def emit_ffn_unit(fs, kind, idx):
            resid = t["resid"][fs % 2]
            if kind == "hj":
                hj = idx
                mpar = hj % 2
                self.wait("sync", "pe", self.w1t_free[mpar])
                w1_ld = self.dma(
                    "sync", t["w1t3"][mpar],
                    t["w1qTr"][:, :, hj * 128:(hj + 1) * 128],
                    sem=f"l_w1{mpar}")
                self.wait("tensor", f"l_w1{mpar}", w1_ld)
                self.wait("tensor", "dve", self.xq_ready)
                self.wait("tensor", "dve", self.ffn_free[mpar])
                if mpar == 0:
                    self.wait("tensor", "dve", self.bc_free)
                else:
                    self.wait("tensor", "dve", self.bc2_free)
                for ch in range(cfg.NCH - 1):
                    self.opq("tensor", lambda e, ch=ch, mpar=mpar: e.matmul(
                        t["ffnps"][mpar], t["w1t"][mpar][:, ch * 128:(ch + 1) * 128],
                        t["xq"][:, ch * SW:(ch + 1) * SW],
                        start=(ch == 0), stop=False))
                self.op("tensor", "pe", lambda e, mpar=mpar: e.matmul(
                    t["ffnps"][mpar],
                    t["w1t"][mpar][:, (cfg.NCH - 1) * 128:cfg.NCH * 128],
                    t["xq"][:, (cfg.NCH - 1) * SW:cfg.NCH * SW],
                    start=False, stop=True))
                mm1_pe = self.cnt["pe"]
                self.w1t_free[mpar] = mm1_pe
                self.xq_free = mm1_pe
                self.wait("vector", "pe", mm1_pe)
                self.qh_tile(t["ffnps"][mpar], t["b1q"][:, hj:hj + 1],
                             t["hq"][:, hj * SW:(hj + 1) * SW], SW)
                self.ffn_free[mpar] = self.evac_mark
                if hj == cfg.NHJ - 1:
                    self.hq_ready = self.cnt["dve"]
            else:
                dj = idx
                mpar = dj % 2
                self.wait("sync", "pe", self.w2t_free[mpar])
                w2_ld = self.dma(
                    "sync", t["w2t3"][mpar],
                    t["w2qTr"][:, :, dj * 128:(dj + 1) * 128],
                    sem=f"l_w2{mpar}")
                self.wait("tensor", f"l_w2{mpar}", w2_ld)
                self.wait("tensor", "dve", self.hq_ready)
                self.wait("tensor", "dve", self.ffn_free[mpar])
                if mpar == 0:
                    self.wait("tensor", "dve", self.bc_free)
                else:
                    self.wait("tensor", "dve", self.bc2_free)
                for k in range(cfg.NHJ - 1):
                    self.opq("tensor", lambda e, k=k, mpar=mpar: e.matmul(
                        t["ffnps"][mpar], t["w2t"][mpar][:, k * 128:(k + 1) * 128],
                        t["hq"][:, k * SW:(k + 1) * SW],
                        start=(k == 0), stop=False))
                self.op("tensor", "pe", lambda e, mpar=mpar: e.matmul(
                    t["ffnps"][mpar],
                    t["w2t"][mpar][:, (cfg.NHJ - 1) * 128:cfg.NHJ * 128],
                    t["hq"][:, (cfg.NHJ - 1) * SW:cfg.NHJ * SW],
                    start=False, stop=True))
                mm2f_pe = self.cnt["pe"]
                self.w2t_free[mpar] = mm2f_pe
                self.hq_free = mm2f_pe
                self.wait("vector", "pe", mm2f_pe)
                self.wait("vector", "s_out", self.ych_store)
                self.qy_tile(t["ffnps"][mpar], t["b2q"][:, dj:dj + 1], t["ych"], SW)
                self.ffn_free[mpar] = self.evac_mark
                self.wait("sync", "dve", self.mkch_done)
                mk_ld = self.dma("sync", t["mkch"],
                                 t["mask_own"][dj * 128:(dj + 1) * 128,
                                               fs * SW:(fs + 1) * SW], sem="l_mk")
                self.wait("vector", "l_mk", mk_ld)
                self.vop(lambda e: e.tensor_tensor(
                    t["ych"], t["ych"], t["mkch"], ALU.mult))
                self.mkch_done = self.cnt["dve"]
                self.vop(lambda e, dj=dj, resid=resid: e.tensor_tensor(
                    t["ych"], t["ych"], resid[:, dj * SW:(dj + 1) * SW],
                    ALU.add))
                ych_dve = self.cnt["dve"]
                self.wait("gpsimd", "dve", ych_dve)
                self.ych_store = self.dma(
                    "gpsimd", t["out_fm"][dj * 128:(dj + 1) * 128,
                                          fs * SW:(fs + 1) * SW],
                    t["ych"], sem="s_out")

        units = [("hj", j) for j in range(cfg.NHJ)] + \
                [("dj", j) for j in range(cfg.NCH)]
        per_slot = (len(units) + cfg.H // 2 - 1) // (cfg.H // 2)

        for s in range(cfg.NSW + 1):
            if s < cfg.NSW:
                resid = t["resid"][s % 2]
                self.wait("sync", "s_tr0", p2_tr0)
                self.wait("sync", "s_tr1", p2_tr1)
                self.wait("sync", "s_kc0", p1_kc[0])
                self.wait("sync", "s_kc1", p1_kc[1])
                self.wait("sync", "s_vt0", p0_vt[0])
                self.wait("sync", "s_vt1", p0_vt[1])
                self.wait("sync", "pec", self.qf_free)
                for ch in range(cfg.NCH):
                    self.dma("sync", t["qf"][:, ch * SW:(ch + 1) * SW],
                             t["Kfm"][ch * 128:(ch + 1) * 128,
                                      s * SW:(s + 1) * SW], sem="l_qf")
                qf_ld = self.cnt["l_qf"]

                for p in range(cfg.H // 2):
                    par = p % 2
                    self.wait("sync", "pec", self.kch_free[par])
                    ldk = self.dma("sync", t["kch"][par],
                                   t["Kfm"][p * 128:(p + 1) * 128, :],
                                   sem=f"l_kc{par}")
                    prev = None
                    for kt in range(cfg.NKT + 1):
                        if kt < cfg.NKT:
                            kpar = kt % 2
                            vpar = kt % 4
                            self.wait("sync", "pe", self.vsl_free[vpar])
                            ldvs = self.dma(
                                "sync", t["vsl3"][vpar][:, :, 0:64],
                                t["XnG"][kt * 128:(kt + 1) * 128,
                                         2 * p:2 * p + 2, :],
                                sem=f"l_vs{vpar}")
                            if kt == 0:
                                self.wait("tensor", f"l_kc{par}", ldk)
                                self.wait("tensor", "l_qf", qf_ld)
                            self.wait("tensor", "act", self.mega_freeA[kpar])
                            self.opq("tensor", lambda e, p=p, kt=kt, kpar=kpar, par=par:
                                     e.matmul(t["megaAB"][kpar][:, 0:SW],
                                              t["kch"][par][0:64, kt * 128:(kt + 1) * 128],
                                              t["qf"][0:64, p * SW:(p + 1) * SW],
                                              start=True, stop=True, tile_position=(0, 0)))
                            self.op("tensor", "pec", lambda e, p=p, kt=kt, kpar=kpar, par=par:
                                    e.matmul(t["megaAB"][kpar][:, SW:2 * SW],
                                             t["kch"][par][64:128, kt * 128:(kt + 1) * 128],
                                             t["qf"][64:128, p * SW:(p + 1) * SW],
                                             start=True, stop=True, tile_position=(64, 0)))
                            corr_pe = self.cnt["pec"]
                            self.wait("scalar", "pec", corr_pe)
                            self.wait("scalar", "pe", self.et_free[kpar])
                            self.op("scalar", "act", lambda e, kpar=kpar: e.activation(
                                t["etAB"][kpar], t["megaAB"][kpar], AF.Exp,
                                bias=cp[:, 3:4], scale=1.0))
                            exp_act = self.cnt["act"]
                            self.mega_freeA[kpar] = exp_act
                            cur = (kt, kpar, vpar, exp_act, ldvs)
                        else:
                            cur = None
                        if prev is not None:
                            pkt, pkpar, pvpar, pexp, pldvs = prev
                            self.wait("tensor", "act", pexp)
                            self.wait("tensor", f"l_vs{pvpar}", pldvs)
                            if pkt == 0:
                                self.wait("tensor", "act", self.nd_free_act)
                                self.wait("tensor", "dve", self.nd_free_dve)
                            self.opq("tensor", lambda e, pkt=pkt, pkpar=pkpar,
                                     pvpar=pvpar: e.matmul(
                                t["ndA"], t["vsl"][pvpar][:, 0:65],
                                t["etAB"][pkpar][:, 0:SW],
                                start=(pkt == 0), stop=(pkt == cfg.NKT - 1)))
                            self.op("tensor", "pe", lambda e, pkt=pkt, pkpar=pkpar,
                                    pvpar=pvpar: e.matmul(
                                t["ndB"], t["vsl"][pvpar][:, 65:130],
                                t["etAB"][pkpar][:, SW:2 * SW],
                                start=(pkt == 0), stop=(pkt == cfg.NKT - 1)))
                            mm2_pe = self.cnt["pe"]
                            self.et_free[pkpar] = mm2_pe
                            self.vsl_free[pvpar] = mm2_pe
                        prev = cur
                    self.kch_free[par] = self.cnt["pec"]
                    nd_pe = self.cnt["pe"]
                    h0, h1 = 2 * p, 2 * p + 1
                    self.wait("scalar", "pe", nd_pe)
                    self.wait("scalar", "dve", self.attnN_free)
                    self.wait("scalar", "dve", self.dall_free)
                    self.op("scalar", "act", lambda e, p=p: e.copy(
                        t["attnN"][0:64, p * SW:(p + 1) * SW], t["ndA"][0:64, :]))
                    self.op("scalar", "act", lambda e, p=p: e.copy(
                        t["attnN"][64:128, p * SW:(p + 1) * SW], t["ndB"][0:64, :]))
                    self.op("scalar", "act", lambda e, h0=h0: e.copy(
                        t["Dall"][h0], t["ndA"][64:65, :]))
                    self.op("scalar", "act", lambda e, h1=h1: e.copy(
                        t["Dall"][h1], t["ndB"][64:65, :]))
                    self.nd_free_act = self.cnt["act"]
                    self.wait("vector", "act", self.nd_free_act)
                    self.wait("vector", "pe", self.drec_free)
                    self.vop(lambda e, h0=h0: e.reciprocal(t["Drec"][h0], t["Dall"][h0]))
                    self.vop(lambda e, h1=h1: e.reciprocal(t["Drec"][h1], t["Dall"][h1]))
                    self.dall_free = self.cnt["dve"]
                    self.wait("tensor", "dve", self.dall_free)
                    self.wait("tensor", "dve", self.ffn_free[0])
                    self.wait("tensor", "dve", self.bc_free)
                    self.opq("tensor", lambda e, h0=h0: e.matmul(
                        t["bcps"][0:64, :], t["onesf32c"][0:1, 0:64], t["Drec"][h0],
                        start=True, stop=True, tile_position=(0, 0)))
                    self.op("tensor", "pe", lambda e, h1=h1: e.matmul(
                        t["bcps"][64:128, :], t["onesf32c"][0:1, 0:64], t["Drec"][h1],
                        start=True, stop=True, tile_position=(0, 64)))
                    bc_pe = self.cnt["pe"]
                    self.drec_free = bc_pe
                    self.wait("sync", "dve", self.xoch_done)
                    xo_ld = self.dma("sync", t["xoch"],
                                     t["x_own"][p * 128:(p + 1) * 128,
                                                s * SW:(s + 1) * SW], sem="l_xo")
                    self.wait("vector", "pe", bc_pe)
                    self.wait("vector", "l_xo", xo_ld)
                    self.vop(lambda e, p=p, resid=resid: e.tensor_tensor(
                        resid[:, p * SW:(p + 1) * SW],
                        t["attnN"][:, p * SW:(p + 1) * SW], t["bcps"], ALU.mult))
                    self.bc_free = self.cnt["dve"]
                    self.vop(lambda e, p=p, resid=resid: e.tensor_tensor(
                        resid[:, p * SW:(p + 1) * SW],
                        resid[:, p * SW:(p + 1) * SW], t["xoch"], ALU.add))
                    self.xoch_done = self.cnt["dve"]
                    if s > 0:
                        for u in units[p * per_slot:(p + 1) * per_slot]:
                            emit_ffn_unit(s - 1, u[0], u[1])
            else:
                for u in units:
                    emit_ffn_unit(s - 1, u[0], u[1])

            if s < cfg.NSW:
                resid = t["resid"][s % 2]
                for ch in range(cfg.NCH):
                    self.vop(lambda e, ch=ch, resid=resid: e.tensor_tensor(
                        t["attnN"][:, ch * SW:(ch + 1) * SW],
                        resid[:, ch * SW:(ch + 1) * SW],
                        resid[:, ch * SW:(ch + 1) * SW], ALU.mult))
                sq_dve = self.cnt["dve"]
                self.wait("tensor", "dve", sq_dve)
                self.wait("tensor", "act", self.nd_free_act)
                self.wait("tensor", "dve", self.nd_free_dve)
                for ch in range(cfg.NCH):
                    if ch < cfg.NCH - 1:
                        self.opq("tensor", lambda e, ch=ch, resid=resid: e.matmul(
                            t["ndA"][0:1, :], t["onesf32"][:, 0:1],
                            resid[:, ch * SW:(ch + 1) * SW],
                            start=(ch == 0), stop=False))
                    else:
                        self.op("tensor", "pe", lambda e, ch=ch, resid=resid: e.matmul(
                            t["ndA"][0:1, :], t["onesf32"][:, 0:1],
                            resid[:, ch * SW:(ch + 1) * SW],
                            start=False, stop=True))
                for ch in range(cfg.NCH):
                    if ch < cfg.NCH - 1:
                        self.opq("tensor", lambda e, ch=ch: e.matmul(
                            t["ndB"][0:1, :], t["onesf32"][:, 0:1],
                            t["attnN"][:, ch * SW:(ch + 1) * SW],
                            start=(ch == 0), stop=False))
                    else:
                        self.op("tensor", "pe", lambda e, ch=ch: e.matmul(
                            t["ndB"][0:1, :], t["onesf32"][:, 0:1],
                            t["attnN"][:, ch * SW:(ch + 1) * SW],
                            start=False, stop=True))
                sums_pe = self.cnt["pe"]
                self.wait("vector", "pe", sums_pe)
                self.wait("vector", "pe", self.stats_free)
                self.vop(lambda e: e.tensor_scalar(
                    t["mrow"], t["ndA"][0:1, :], 1.0 / 1024.0, None, ALU.mult))
                self.vop(lambda e: e.tensor_tensor(
                    t["vrow"], t["mrow"], t["mrow"], ALU.mult))
                self.vop(lambda e: e.scalar_tensor_tensor(
                    t["vrow"], t["ndB"][0:1, :], 1.0 / 1024.0, t["vrow"],
                    ALU.mult, ALU.subtract))
                self.nd_free_dve = self.cnt["dve"]
                st_dve = self.cnt["dve"]
                self.wait("scalar", "dve", st_dve)
                self.op("scalar", "act", lambda e: e.activation(
                    t["vrow"], t["vrow"], AF.Ln, bias=cp[0:1, 1:2], scale=1.0))
                self.wait("scalar", "act", self.cnt["act"])
                self.op("scalar", "act", lambda e: e.activation(
                    t["vrow"], t["vrow"], AF.Exp, bias=0.0, scale=cp[0:1, 2:3]))
                self.wait("scalar", "act", self.cnt["act"])
                stat_act = self.cnt["act"]
                self.wait("tensor", "act", stat_act)
                self.wait("tensor", "dve", self.bc_free)
                self.wait("tensor", "dve", self.ffn_free[0])
                self.op("tensor", "pe", lambda e: e.matmul(
                    t["bcps"], t["onesf32c"], t["mrow"], start=True, stop=True))
                m_pe = self.cnt["pe"]
                self.stats_free = m_pe
                self.wait("vector", "pe", m_pe)
                self.wait("vector", "pe", self.xq_free)
                for ch in range(cfg.NCH):
                    self.vop(lambda e, ch=ch, resid=resid: e.tensor_tensor(
                        t["attnN"][:, ch * SW:(ch + 1) * SW],
                        resid[:, ch * SW:(ch + 1) * SW], t["bcps"], ALU.subtract))
                self.bc_free = self.cnt["dve"]
                self.wait("tensor", "dve", self.bc2_free)
                self.wait("tensor", "dve", self.ffn_free[1])
                self.op("tensor", "pe", lambda e: e.matmul(
                    t["bcps2"], t["onesf32c"], t["vrow"], start=True, stop=True))
                bc2_pe = self.cnt["pe"]
                self.stats_free = bc2_pe
                self.wait("vector", "pe", bc2_pe)
                for ch in range(cfg.NCH):
                    self.vop(lambda e, ch=ch: e.tensor_tensor(
                        t["attnN"][:, ch * SW:(ch + 1) * SW],
                        t["attnN"][:, ch * SW:(ch + 1) * SW], t["bcps2"], ALU.mult))
                    self.bc2_free = self.cnt["dve"]
                    self.q43_tile(t["attnN"][:, ch * SW:(ch + 1) * SW],
                                  t["xq"][:, ch * SW:(ch + 1) * SW], SW, 4)
                self.xq_ready = self.cnt["dve"]
                self.attnN_free = self.cnt["dve"]
                self.qf_free = self.cnt["pec"]
        self.wait("gpsimd", "s_out", self.cnt["s_out"])


def build_program(cfg):
    nc = bass.Bass("TRN2", target_bir_lowering=False, debug=False)
    T, D, HF, SW = cfg.T, cfg.D, cfg.HF, cfg.SW
    t = {}
    t["x_tok"] = nc.dram_tensor("x_tok", [T, D], F32, kind="ExternalInput").ap()
    t["x_own"] = nc.dram_tensor("x_own", [D, cfg.OWN], F32, kind="ExternalInput").ap()
    t["mask_own"] = nc.dram_tensor("mask_own", [D, cfg.OWN], F32,
                                   kind="ExternalInput").ap()
    t["w1"] = nc.dram_tensor("w1", [HF, D], F32, kind="ExternalInput").ap()
    t["w2"] = nc.dram_tensor("w2", [D, HF], F32, kind="ExternalInput").ap()
    t["b1f"] = nc.dram_tensor("b1f", [128, HF // 128], F32, kind="ExternalInput").ap()
    t["b2f"] = nc.dram_tensor("b2f", [128, D // 128], F32, kind="ExternalInput").ap()
    t["identin"] = nc.dram_tensor("identin", [128, 128], BF16,
                                  kind="ExternalInput").ap()
    t["out_fm"] = nc.dram_tensor("out_fm", [D, cfg.OWN], F32,
                                 kind="ExternalOutput").ap()
    t["Xn"] = nc.dram_tensor("Xn", [T, D], BF16, kind="Internal").ap()
    t["XnG"] = t["Xn"].rearrange("t (g c) -> t g c", g=D // 64, c=64)
    t["Kfm"] = nc.dram_tensor("Kfm", [D, T], BF16, kind="Internal").ap()
    t["w1qn"] = nc.dram_tensor("w1qn", [HF, D], BF16, kind="Internal").ap()
    t["w2qn"] = nc.dram_tensor("w2qn", [D, HF], BF16, kind="Internal").ap()
    t["w1qT"] = nc.dram_tensor("w1qT", [D, HF], BF16, kind="Internal").ap()
    t["w2qT"] = nc.dram_tensor("w2qT", [HF, D], BF16, kind="Internal").ap()
    t["w1qTr"] = t["w1qT"].rearrange("(c r) h -> r c h", c=D // 128, r=128)
    t["w2qTr"] = t["w2qT"].rearrange("(c r) d -> r c d", c=HF // 128, r=128)

    with contextlib.ExitStack() as es:
        def sb(name, shape, dtype):
            return es.enter_context(nc.sbuf_tensor(name, shape, dtype))[:]

        t["cpool"] = sb("cpool", [128, len(CPOOL_VALS)], F32)
        t["ident"] = sb("ident", [128, 128], BF16)
        t["onesbf"] = sb("onesbf", [1, 128], BF16)
        t["onesf32"] = sb("onesf32", [128, 1], F32)
        t["onesf32c"] = sb("onesf32c", [1, 128], F32)
        t["xbuf"] = [sb(f"xbuf{i}", [128, D], F32) for i in range(2)]
        t["s1"] = sb("s1", [128, 1], F32)
        t["s2"] = sb("s2", [128, 1], F32)
        t["s3"] = sb("s3", [128, 1], F32)
        xnch = [es.enter_context(nc.sbuf_tensor(f"xnc{i}", [128, D], BF16))
                for i in range(2)]
        t["xnc"] = [h[:] for h in xnch]
        t["xnc3"] = [h.ap().rearrange("p (h c) -> p h c", h=cfg.H, c=64)
                     for h in xnch]
        t["kch"] = [sb(f"kch{i}", [128, T], BF16) for i in range(2)]
        t["b1q"] = sb("b1q", [128, HF // 128], F32)
        t["b2q"] = sb("b2q", [128, D // 128], F32)
        for nm in ("qa", "qb", "qc", "qd", "qe"):
            t[nm] = sb(nm, [128, SW], F32)
        t["qi"] = sb("qi", [128, SW], I32)
        t["qf"] = sb("qfq", [128, (D // 128) * SW], BF16)
        vslh = [es.enter_context(nc.sbuf_tensor(f"vsl{i}", [128, 130], BF16))
                for i in range(4)]
        t["vsl"] = [h[:] for h in vslh]
        t["vsl3"] = [h.ap().rearrange("p (j c) -> p j c", j=2, c=65)
                     for h in vslh]
        t["etAB"] = [sb(f"etAB{i}", [128, 2 * SW], BF16) for i in range(2)]
        t["attnN"] = sb("attnN", [128, cfg.NCH * SW], F32)
        t["attnR"] = sb("attnR", [128, cfg.NCH * SW], F32)
        t["sqch"] = sb("sqch", [128, cfg.NCH * SW], F32)
        t["xq"] = sb("xq", [128, cfg.NCH * SW], BF16)
        t["hq"] = sb("hq", [128, max(cfg.NHJ * SW, HF)], BF16)
        t["wq"] = t["hq"][:, 0:HF]
        dall = sb("Dall", [1, 2 * SW], F32)
        drec = sb("Drec", [1, 2 * SW], F32)
        t["Dall"] = [dall[:, (h % 2) * SW:(h % 2 + 1) * SW] for h in range(cfg.H)]
        t["Drec"] = [drec[:, (h % 2) * SW:(h % 2 + 1) * SW] for h in range(cfg.H)]
        t["mrow"] = sb("mrow", [1, SW], F32)
        t["vrow"] = sb("vrow", [1, SW], F32)
        t["xoch"] = sb("xoch", [128, SW], F32)
        t["mkch"] = sb("mkch", [128, SW], F32)
        t["ych"] = sb("ych", [128, SW], F32)
        w1th = [es.enter_context(nc.sbuf_tensor(f"w1t{i}", [128, D], BF16))
                for i in range(2)]
        t["w1t"] = [h[:] for h in w1th]
        t["w1t3"] = [h.ap().rearrange("p (c k) -> p c k", c=cfg.NCH, k=128)
                     for h in w1th]
        w2th = [es.enter_context(nc.sbuf_tensor(f"w2t{i}", [128, HF], BF16))
                for i in range(2)]
        t["w2t"] = [h[:] for h in w2th]
        t["w2t3"] = [h.ap().rearrange("p (c k) -> p c k", c=cfg.NHJ, k=128)
                     for h in w2th]
        t["sq1"] = t["attnN"][:, 0:D]
        t["resid"] = [t["attnR"], t["sqch"]]
        if cfg.NCH * SW >= HF:
            t["wbuf"] = [t["attnR"][:, 0:HF], t["sqch"][:, 0:HF]]
        else:
            t["wbuf"] = [sb(f"wbuf{i}", [128, HF], F32) for i in range(2)]

        t["megaAB"] = [es.enter_context(
            nc.psum_tensor(f"megaAB{i}", [128, 2 * SW], F32))[:] for i in range(2)]
        t["ndA"] = es.enter_context(nc.psum_tensor("ndA", [65, SW], F32))[:]
        t["ndB"] = es.enter_context(nc.psum_tensor("ndB", [65, SW], F32))[:]
        t["ffnps"] = [es.enter_context(nc.psum_tensor(f"ffnps{i}", [128, SW], F32))[:]
                      for i in range(2)]
        t["bcps"] = t["ffnps"][0]
        t["bcps2"] = t["ffnps"][1]

        sems = {}
        semnames = ["gp", "dve", "act", "pe", "pec", "l_id", "l_b", "l_qf", "l_xo",
                    "l_mk", "s_out"]
        for i in range(2):
            semnames += [f"l_xb{i}", f"l_kc{i}", f"l_wb{i}", f"l_vs{i}",
                         f"l_w1{i}", f"l_w2{i}", f"s_vt{i}", f"s_tr{i}",
                         f"s_kc{i}", f"l_vs{i + 2}", f"s_nat{i}"]
        for nm in semnames:
            sems[nm] = es.enter_context(nc.semaphore(nm))
        t["sems"] = sems

        with nc.Block() as blk:
            @blk.sync
            def _(sync):
                P(nc, cfg, t, "sync", sync).emit()

            @blk.gpsimd
            def _(g):
                P(nc, cfg, t, "gpsimd", g).emit()

            @blk.vector
            def _(v):
                P(nc, cfg, t, "vector", v).emit()

            @blk.scalar
            def _(sc):
                P(nc, cfg, t, "scalar", sc).emit()

            @blk.tensor
            def _(te):
                P(nc, cfg, t, "tensor", te).emit()
    return nc




def make_in_maps(x, w1, b1, w2, b2, cfg, n_cores=8):
    B, T, D = x.shape
    keep = dropout_keep_mask((B, T, D))
    mask = np.where(keep, np.float32(1.25), np.float32(0.0))
    ident = np.eye(128).astype(bf16)
    b1f = np.ascontiguousarray(b1.reshape(cfg.NHJ, 128).T)
    b2f = np.ascontiguousarray(b2.reshape(cfg.NCH, 128).T)
    per_batch = n_cores // B
    in_maps = []
    for c in range(n_cores):
        b = c // per_batch
        qlo = (c % per_batch) * cfg.OWN
        in_maps.append({
            "x_tok": np.ascontiguousarray(np.roll(x[b], -qlo, axis=0)),
            "x_own": np.ascontiguousarray(x[b, qlo:qlo + cfg.OWN].T),
            "mask_own": np.ascontiguousarray(mask[b, qlo:qlo + cfg.OWN].T),
            "w1": w1, "w2": w2, "b1f": b1f, "b2f": b2f,
            "identin": ident,
        })
    return in_maps


def kernel(L0_attn_x, L0_ffn_w1, L0_ffn_b1, L0_ffn_w2, L0_ffn_b2):
    x = np.ascontiguousarray(np.asarray(L0_attn_x, np.float32))
    w1 = np.ascontiguousarray(np.asarray(L0_ffn_w1, np.float32))
    b1 = np.asarray(L0_ffn_b1, np.float32)
    w2 = np.ascontiguousarray(np.asarray(L0_ffn_w2, np.float32))
    b2 = np.asarray(L0_ffn_b2, np.float32)
    B, T, D = x.shape
    cfg = Cfg(T=T, OWN=T * B // 8, SW=512)
    nc = build_program(cfg)
    in_maps = make_in_maps(x, w1, b1, w2, b2, cfg)
    res = run_bass_kernel_spmd(nc, in_maps, core_ids=list(range(8)))
    out = np.empty((B, T, D), np.float32)
    per_batch = 8 // B
    for c in range(8):
        b = c // per_batch
        qlo = (c % per_batch) * cfg.OWN
        out[b, qlo:qlo + cfg.OWN] = np.asarray(res.results[c]["out_fm"],
                                               np.float32).T
    return out
